# version 03
# baseline (speedup 1.0000x reference)
import contextlib
import numpy as np
import ml_dtypes
import concourse.bass as bass
import concourse.mybir as mybir
from concourse.bass_utils import run_bass_kernel_spmd
from concourse.alu_op_type import AluOpType

F32 = mybir.dt.float32
BF16 = mybir.dt.bfloat16
AF = mybir.ActivationFunctionType
AX = mybir.AxisListType.X

NH = 16
NPAIR = 8
C = 512
NMC = 8
NPC = 8
W = 256
NWT = 4
NE = 16
SH_FF = 2816
NFB = SH_FF // 128

S4 = float(32.0 ** -0.25)
LN32_HALF = float(0.5 * np.log(32.0))


class Stream:
    def __init__(self, sem, step=1):
        self.sem = sem
        self.v = 0
        self.step = step

    def tag(self, instr, n=None):
        n = self.step if n is None else n
        instr.then_inc(self.sem, n)
        self.v += n
        return self.v


def build_nc(n_prefix_chunks=NPC, n_main_chunks=NMC):
    nc = bass.Bass("TRN2", target_bir_lowering=False)
    n_seg = max(1, (n_main_chunks + 1) // 2)

    def din(name, shape):
        return nc.dram_tensor(name, shape, F32, kind="ExternalInput")

    xm_h = din("xm", [4096, 1024])
    xp_h = din("xp", [4096, 1024])
    wqkv_h = din("wqkv", [3072, 1024])
    bqkv_h = din("bqkv", [1, 3072])
    wout_h = din("wout", [1024, 1024])
    bout_h = din("bout", [1, 1024])
    omg_h = din("omg", [2, 16, 64, 32])
    qn_h = din("qn", [2, 1])
    qw_h = din("qw", [2, 1])
    gw_h = din("gw", [16, 1024])
    wgu_h = din("wgu", [16, 1024, 1024])
    wdn_h = din("wdn", [16, 512, 1024])
    swg_h = din("swg", [2816, 1024])
    swu_h = din("swu", [2816, 1024])
    swd_h = din("swd", [1024, 2816])
    sgw_h = din("sgw", [1, 1024])
    cident_h = din("cident", [128, 128])
    cmaskT_h = din("cmaskT", [128, 128])
    cgbd_h = din("cgbd", [16, 1024])
    cece_h = din("cece", [16, 2048])
    cnormE_h = din("cnormE", [128, 128])
    cpatT_h = din("cpatT", [2, 128])
    pz_h = din("pz", [128, 1])

    out_h = nc.dram_tensor("out", [4096, 1024], F32, kind="ExternalOutput")

    xbf_h = nc.dram_tensor("xbf", [8192, 1024], BF16, kind="Internal")
    wqkv_bf_h = nc.dram_tensor("wqkv_bf", [3072, 1024], BF16, kind="Internal")
    wout_bf_h = nc.dram_tensor("wout_bf", [1024, 1024], BF16, kind="Internal")
    swg_bf_h = nc.dram_tensor("swg_bf", [2816, 1024], BF16, kind="Internal")
    swu_bf_h = nc.dram_tensor("swu_bf", [2816, 1024], BF16, kind="Internal")
    swd_bf_h = nc.dram_tensor("swd_bf", [1024, 2816], BF16, kind="Internal")
    swgT_h = nc.dram_tensor("swgT", [1024, 2816], BF16, kind="Internal")
    swuT_h = nc.dram_tensor("swuT", [1024, 2816], BF16, kind="Internal")
    swdT_h = nc.dram_tensor("swdT", [2816, 1024], BF16, kind="Internal")
    wgu_bf_h = nc.dram_tensor("wgu_bf", [16, 1024, 1024], BF16, kind="Internal")
    wdn_bf_h = nc.dram_tensor("wdn_bf", [16, 512, 1024], BF16, kind="Internal")

    es = contextlib.ExitStack()
    sb = lambda name, shape, dt: es.enter_context(nc.sbuf_tensor(name, shape, dt))

    ident_b = sb("ident_b", [128, 128], BF16)
    maskT_b = sb("maskT_b", [128, 128], BF16)
    ones_b = sb("ones_b", [128, 128], BF16)
    ones_f = sb("ones_f", [128, 128], F32)
    gbd_b = sb("gbd_b", [16, 1024], BF16)
    ece_b = sb("ece_b", [16, 2048], BF16)
    normE_b = sb("normE_b", [128, 128], BF16)
    patT_f = sb("patT_f", [2, 128], F32)
    q2_f = sb("q2_f", [2, 8], F32)
    pz_f = sb("pz_f", [128, 1], F32)
    qsc_f = sb("qsc_f", [128, 8], F32)
    bqkv_f = sb("bqkv_f", [128, 24], F32)
    bvbc_f = sb("bvbc_f", [128, 1024], F32)
    boutbc_f = sb("boutbc_f", [128, 1024], F32)
    omgbd_b = sb("omgbd_b", [128, 1024], BF16)
    gwT_b = sb("gwT_b", [128, 128], BF16)
    sgwT_b = sb("sgwT_b", [128, 8], BF16)
    state_f = sb("state_f", [128, 1024], F32)
    sa_b = sb("sa_b", [128, 520], BF16)
    sbp_b = sb("sbp_b", [128, 520], BF16)
    h1f = sb("h1f", [128, 8192], F32)
    stg_b = sb("stg_b", [128, 1024], BF16)

    G = Stream(es.enter_context(nc.semaphore("semG")), 16)
    X = Stream(es.enter_context(nc.semaphore("semX")), 16)
    P = Stream(es.enter_context(nc.semaphore("semP")), 1)
    V = Stream(es.enter_context(nc.semaphore("semV")), 1)
    A = Stream(es.enter_context(nc.semaphore("semA")), 1)
    O = Stream(es.enter_context(nc.semaphore("semO")), 16)

    block = es.enter_context(nc.Block())

    def on(engname, fn):
        getattr(block, engname)(fn)

    def barrier():
        vals = [(G, G.v), (X, X.v), (P, P.v), (V, V.v), (A, A.v)]
        for eng in ("tensor", "vector", "scalar", "gpsimd", "sync"):
            def _b(e, vals=vals):
                for s, v in vals:
                    if v:
                        e.wait_ge(s.sem, v)
            on(eng, _b)

    def v_pre(v):
        v.memset(ones_b[:, :], 1.0)
        v.memset(ones_f[:, :], 1.0)
        v.memset(omgbd_b[:, :], 0.0)
        v.memset(state_f[:, :], 0.0)
        v.memset(sa_b[:, :], 0.0)
        v.memset(sbp_b[:, :], 0.0)
        v.memset(qsc_f[:, :], 0.0)
        v.memset(qsc_f[:, 3:4], 1e-5)
        v.memset(qsc_f[:, 4:5], 2e-5)
        v.memset(qsc_f[:, 5:6], 0.34657359027997264)
        v.memset(bvbc_f[:, :], 0.0)
        V.tag(v.memset(boutbc_f[:, :], 0.0))
    on("vector", v_pre)
    v_memset = V.v

    mil = {}

    def g_startup(g):
        G.tag(g.dma_start(ident_b[:, :], cident_h.ap()))
        G.tag(g.dma_start(maskT_b[:, :], cmaskT_h.ap()))
        G.tag(g.dma_start(gbd_b[:, :], cgbd_h.ap()))
        G.tag(g.dma_start(ece_b[:, :], cece_h.ap()))
        G.tag(g.dma_start(normE_b[:, :], cnormE_h.ap()))
        G.tag(g.dma_start(patT_f[:, :], cpatT_h.ap()))
        G.tag(g.dma_start(q2_f[:, 0:1], qn_h.ap()))
        G.tag(g.dma_start(q2_f[:, 1:2], qw_h.ap()))
        mil["qnqw"] = G.v
        g.wait_ge(V.sem, v_memset)
        G.tag(g.dma_start(bqkv_f[:, :], bass.AP(bqkv_h, 0, [[1, 128], [128, 24]]),
                          allow_slow_non_contiguous=True))
        G.tag(g.dma_start(bvbc_f[0:1, :], bqkv_h.ap()[0:1, 2048:3072]))
        G.tag(g.dma_start(boutbc_f[0:1, :], bout_h.ap()))
        mil["biasrows"] = G.v
        for j in range(NPAIR):
            for par in range(2):
                for r in range(2):
                    c0 = 128 * j + 64 * par + 32 * r
                    G.tag(g.dma_start(omgbd_b[64 * par:64 * par + 64, c0:c0 + 32],
                                      omg_h.ap()[r, 2 * j + par]))
        for dc in range(8):
            G.tag(g.dma_start(gwT_b[:, 16 * dc:16 * dc + 16],
                              bass.AP(gw_h, 128 * dc, [[1, 128], [1024, 16]]),
                              allow_slow_non_contiguous=True))
        G.tag(g.dma_start(sgwT_b[:, :], bass.AP(sgw_h, 0, [[1, 128], [128, 8]]),
                          allow_slow_non_contiguous=True))
        G.tag(g.dma_start(pz_f[:, :], pz_h.ap()))
        mil["pz"] = G.v
        mil["smalls"] = G.v
        for pc in range(n_prefix_chunks):
            G.tag(g.dma_start(xbf_h.ap()[4096 + pc * C:4096 + (pc + 1) * C, :],
                              xp_h.ap()[pc * C:(pc + 1) * C, :]))
            mil[("xp", pc)] = G.v
        G.tag(g.dma_start(wqkv_bf_h.ap(), wqkv_h.ap()))
        mil["wqkv_bf"] = G.v
        G.tag(g.dma_start(wout_bf_h.ap(), wout_h.ap()))
        mil["wout_bf"] = G.v
        for mc in range(n_main_chunks):
            G.tag(g.dma_start(xbf_h.ap()[mc * C:(mc + 1) * C, :],
                              xm_h.ap()[mc * C:(mc + 1) * C, :]))
            mil[("xm", mc)] = G.v
        G.tag(g.dma_start(swg_bf_h.ap(), swg_h.ap()))
        mil["swg_bf"] = G.v
        G.tag(g.dma_start(swu_bf_h.ap(), swu_h.ap()))
        mil["swu_bf"] = G.v
        G.tag(g.dma_start(swd_bf_h.ap(), swd_h.ap()))
        mil["swd_bf"] = G.v
        for e in range(NE):
            G.tag(g.dma_start(wgu_bf_h.ap()[e], wgu_h.ap()[e]))
            G.tag(g.dma_start(wdn_bf_h.ap()[e], wdn_h.ap()[e]))
        mil["moe_bf"] = G.v
    on("gpsimd", g_startup)

    def v_quad(v):
        v.wait_ge(G.sem, mil["qnqw"])
        v.tensor_scalar(q2_f[:, 2:3], q2_f[:, 0:1], 2.0, 1e-30,
                        AluOpType.mult, AluOpType.max)
        v.tensor_scalar(q2_f[:, 4:5], q2_f[:, 1:2], 1e-30, None, AluOpType.max)
        V.tag(v.tensor_scalar(q2_f[:, 3:4], q2_f[:, 0:1], -1.0, None, AluOpType.mult))
    on("vector", v_quad)
    v_quad_c = V.v

    def a_quad(s):
        s.wait_ge(V.sem, v_quad_c)
        s.activation(q2_f[:, 2:3], q2_f[:, 2:3], AF.Ln)
        s.activation(q2_f[:, 2:3], q2_f[:, 2:3], AF.Exp, scale=0.5)
        A.tag(s.activation(q2_f[:, 4:5], q2_f[:, 4:5], AF.Ln))
    on("scalar", a_quad)
    a_quad_c = A.v

    def v_quad2(v):
        v.wait_ge(A.sem, a_quad_c)
        V.tag(v.tensor_scalar(q2_f[:, 4:5], q2_f[:, 4:5], 0.5, -LN32_HALF,
                              AluOpType.mult, AluOpType.add))
    on("vector", v_quad2)
    v_quad2_c = V.v

    with nc.psum_tensor("pst0", [128, 1024], F32) as pst0, \
         nc.psum_tensor("pst1", [128, 1024], F32) as pst1:
        def t_bc(t):
            t.wait_ge(V.sem, v_quad2_c)
            t.wait_ge(G.sem, mil["biasrows"])
            t.matmul(pst0[:, 0:3], patT_f[:, :], q2_f[:, 2:5], start=True, stop=True)
            t.matmul(pst0[:, 512:1024], ones_f[0:1, :], bvbc_f[0:1, 0:512], start=True, stop=True)
            t.matmul(pst1[:, 0:512], ones_f[0:1, :], bvbc_f[0:1, 512:1024], start=True, stop=True)
            P.tag(t.matmul(pst1[:, 512:1024], ones_f[0:1, :], boutbc_f[0:1, 0:512], start=True, stop=True))
        on("tensor", t_bc)
        p_bc = P.v

        def v_bc(v):
            v.wait_ge(P.sem, p_bc)
            v.tensor_copy(qsc_f[:, 0:3], pst0[:, 0:3])
            v.tensor_copy(bvbc_f[:, 0:512], pst0[:, 512:1024])
            v.tensor_copy(bvbc_f[:, 512:1024], pst1[:, 0:512])
            V.tag(v.tensor_copy(boutbc_f[:, 0:512], pst1[:, 512:1024]))
        on("vector", v_bc)
        v_bc_c = V.v

        def t_bc2(t):
            t.wait_ge(V.sem, v_bc_c)
            P.tag(t.matmul(pst0[:, 0:512], ones_f[0:1, :], boutbc_f[0:1, 512:1024],
                           start=True, stop=True))
        on("tensor", t_bc2)
        p_bc2 = P.v

        def v_bc2(v):
            v.wait_ge(P.sem, p_bc2)
            V.tag(v.tensor_copy(boutbc_f[:, 512:1024], pst0[:, 0:512]))
        on("vector", v_bc2)

    barrier()

    shT_jobs = []
    for (srcname, FF) in (("swg", SH_FF), ("swu", SH_FF)):
        for dc in range(8):
            for fb0 in range(0, FF, 512):
                shT_jobs.append((srcname, dc, fb0, min(512, FF - fb0)))
    for fc in range(SH_FF // 128):
        for db0 in range(0, 1024, 512):
            shT_jobs.append(("swd", fc, db0, 512))
    shT_state = {"done_g": None}

    def emit_sharedT():
        xbase = [X.v]
        gbase = [G.v]

        def x_shT(sy):
            sy.wait_ge(G.sem, mil["swg_bf"])
            seen_u = seen_d = False
            for idx, (kind, a0, b0, wdt) in enumerate(shT_jobs):
                if kind == "swu" and not seen_u:
                    sy.wait_ge(G.sem, mil["swu_bf"])
                    seen_u = True
                if kind == "swd" and not seen_d:
                    sy.wait_ge(G.sem, mil["swd_bf"])
                    seen_d = True
                if idx >= 2:
                    sy.wait_ge(G.sem, gbase[0] + (idx - 1) * 16)
                src_h = {"swg": swg_bf_h, "swu": swu_bf_h, "swd": swd_bf_h}[kind]
                slot = idx % 2
                X.tag(sy.dma_start(stg_b[:, 512 * slot:512 * slot + wdt],
                                   src_h.ap()[b0:b0 + wdt, 128 * a0:128 * a0 + 128],
                                   transpose=True))
        on("sync", x_shT)

        def g_shT(g):
            for idx, (kind, a0, b0, wdt) in enumerate(shT_jobs):
                g.wait_ge(X.sem, xbase[0] + (idx + 1) * 16)
                dst_h = {"swg": swgT_h, "swu": swuT_h, "swd": swdT_h}[kind]
                slot = idx % 2
                G.tag(g.dma_start(dst_h.ap()[128 * a0:128 * a0 + 128, b0:b0 + wdt],
                                  stg_b[:, 512 * slot:512 * slot + wdt]))
        on("gpsimd", g_shT)
        shT_state["done_g"] = G.v

    out_slot_war = [0]

    for seg in range(n_seg):
        with contextlib.ExitStack() as asx:
            asb = lambda name, shape, dt: asx.enter_context(
                nc.sbuf_tensor(f"{name}_s{seg}", shape, dt))
            wqkvT_sb = asb("wqkvT", [128, 24576], BF16)
            woutT_sb = asb("woutT", [128, 8192], BF16)
            xfm_b = asb("xfm", [128, 4096], BF16)
            xtm_f = asb("xtm", [128, 2048], F32)
            qkraw_b = asb("qkraw", [128, 1024], BF16)
            v1_b = asb("v1", [128, 4160], BF16)
            zsq_b = asb("zsq", [128, 1024], BF16)
            invq_b = asb("invq", [16, 512], BF16)
            invk_b = asb("invk", [16, 512], BF16)
            proj_f = asb("proj", [128, 512], F32)
            poly_f = asb("poly", [128, 512], F32)
            prf_f = asb("prf", [128, 512], F32)
            qf_b = asb("qf", [128, 4096], BF16)
            kf_b = asb("kf", [128, 4096], BF16)
            kftm_b = asb("kftm", [128, 4096], BF16)
            asb_b = asb("asbuf", [128, 1280], BF16)
            ctx_f = asb("ctx", [128, 512], F32)
            denr_b = asb("denr", [128, 512], BF16)
            attnh_b = asb("attnh", [64, 512], BF16)
            attn_b = asb("attn", [128, 4096], BF16)
            r_f = asb("r", [128, 1024], F32)
            bna_f = asb("bna", [128, 20], F32)
            rstd_f = asb("rstd", [128, 2], F32)

            PA = asx.enter_context(nc.psum_tensor(f"PA_s{seg}", [128, 1024], F32))
            PB = asx.enter_context(nc.psum_tensor(f"PB_s{seg}", [128, 1024], F32))
            PC = asx.enter_context(nc.psum_tensor(f"PC_s{seg}", [128, 512], F32))
            PD = asx.enter_context(nc.psum_tensor(f"PD_s{seg}", [128, 512], F32))
            PS = asx.enter_context(nc.psum_tensor(f"PS_s{seg}", [128, 512], F32))
            Pbf = asx.enter_context(nc.psum_tensor(f"Pbf_s{seg}", [128, 1024], BF16))

            def x_wT(sy):
                sy.wait_ge(G.sem, mil["wqkv_bf"])
                for dc in range(8):
                    for jb in range(6):
                        X.tag(sy.dma_start(
                            wqkvT_sb[:, 3072 * dc + 512 * jb:3072 * dc + 512 * jb + 512],
                            wqkv_bf_h.ap()[512 * jb:512 * jb + 512, 128 * dc:128 * dc + 128],
                            transpose=True))
                sy.wait_ge(G.sem, mil["wout_bf"])
                for cc in range(8):
                    for ob in range(2):
                        X.tag(sy.dma_start(
                            woutT_sb[:, 1024 * cc + 512 * ob:1024 * cc + 512 * ob + 512],
                            wout_bf_h.ap()[512 * ob:512 * ob + 512, 128 * cc:128 * cc + 128],
                            transpose=True))
            on("sync", x_wT)
            x_wq = X.v - 16 * 16
            x_wo = X.v

            war_A = [0]
            war_ctx = [0]
            war_bden = [0]
            war_mv = [0]
            war_state = [0]
            war_proj = [0]
            war_xtm = [0, 0]

            def emit_chunk(kind, ci):
                is_main = kind == "main"
                xrow = ci * C if is_main else 4096 + ci * C
                xcast_g = mil[("xm", ci)] if is_main else mil[("xp", ci)]

                def x_chunk(sy):
                    sy.wait_ge(G.sem, xcast_g)
                    for dc in range(8):
                        X.tag(sy.dma_start(xfm_b[:, 512 * dc:512 * dc + 512],
                                           xbf_h.ap()[xrow:xrow + C, 128 * dc:128 * dc + 128],
                                           transpose=True))
                on("sync", x_chunk)
                x_xfm = X.v

                evq = [0]
                prev_norm = [0]
                for j in range(NPAIR):
                    def t_qk(t, j=j):
                        t.wait_ge(X.sem, max(x_xfm, x_wq))
                        if evq[0]:
                            t.wait_ge(A.sem, evq[0])
                        for dc in range(8):
                            if is_main:
                                t.matmul(PA[:, 0:512],
                                         wqkvT_sb[:, 3072 * dc + 128 * j:3072 * dc + 128 * (j + 1)],
                                         xfm_b[:, 512 * dc:512 * dc + 512],
                                         start=(dc == 0), stop=(dc == 7))
                            last = t.matmul(PA[:, 512:1024],
                                     wqkvT_sb[:, 3072 * dc + 1024 + 128 * j:3072 * dc + 1024 + 128 * (j + 1)],
                                     xfm_b[:, 512 * dc:512 * dc + 512],
                                     start=(dc == 0), stop=(dc == 7))
                        P.tag(last)
                    on("tensor", t_qk)
                    p_qk = P.v

                    def a_evq(s, j=j):
                        s.wait_ge(P.sem, p_qk)
                        if is_main:
                            s.activation(qkraw_b[:, 0:512], PA[:, 0:512], AF.Identity,
                                         bias=bqkv_f[:, j:j + 1])
                        A.tag(s.activation(qkraw_b[:, 512:1024], PA[:, 512:1024], AF.Identity,
                                           bias=bqkv_f[:, 8 + j:9 + j]))
                    on("scalar", a_evq)
                    evq[0] = A.v

                    def a_sq(s, j=j):
                        if prev_norm[0]:
                            s.wait_ge(P.sem, prev_norm[0])
                        if is_main:
                            s.activation(zsq_b[:, 0:512], qkraw_b[:, 0:512], AF.Square)
                        A.tag(s.activation(zsq_b[:, 512:1024], qkraw_b[:, 512:1024], AF.Square))
                    on("scalar", a_sq)
                    a_sq_c = A.v

                    def v_keep(v, j=j):
                        v.wait_ge(A.sem, a_sq_c)
                        if is_main:
                            v.tensor_copy(qf_b[:, 512 * j:512 * (j + 1)], qkraw_b[:, 0:512])
                        V.tag(v.tensor_copy(kf_b[:, 512 * j:512 * (j + 1)], qkraw_b[:, 512:1024]))
                    on("vector", v_keep)

                    def t_norm(t, j=j):
                        t.wait_ge(A.sem, a_sq_c)
                        if is_main:
                            t.matmul(PD[0:16, 0:512], normE_b[:, 16 * j:16 * (j + 1)],
                                     zsq_b[:, 0:512], start=(j == 0), stop=(j == NPAIR - 1))
                        P.tag(t.matmul(PS[0:16, 0:512], normE_b[:, 16 * j:16 * (j + 1)],
                                 zsq_b[:, 512:1024], start=(j == 0), stop=(j == NPAIR - 1)))
                    on("tensor", t_norm)
                    prev_norm[0] = P.v
                p_norm_done = P.v
                v_keep_done = V.v

                prev_vev = [0]
                for i in range(4):
                    def t_v(t, i=i):
                        if prev_vev[0]:
                            t.wait_ge(V.sem, prev_vev[0])
                        for ob in range(2):
                            for dc in range(8):
                                last = t.matmul(PB[:, 512 * ob:512 * (ob + 1)],
                                         xfm_b[:, 512 * dc + 128 * i:512 * dc + 128 * (i + 1)],
                                         wqkvT_sb[:, 3072 * dc + 2048 + 512 * ob:3072 * dc + 2048 + 512 * (ob + 1)],
                                         start=(dc == 0), stop=(dc == 7))
                        P.tag(last)
                    on("tensor", t_v)
                    p_v = P.v

                    def v_ev(v, i=i):
                        v.wait_ge(P.sem, p_v)
                        v.memset(v1_b[:, 1040 * i:1040 * (i + 1)], 1.0)
                        V.tag(v.tensor_tensor(
                            bass.AP(v1_b, 1040 * i, [[4160, 128], [65, 16], [1, 64]]),
                            bass.AP(PB, 0, [[1024, 128], [64, 16], [1, 64]]),
                            bass.AP(bvbc_f, 0, [[1024, 128], [64, 16], [1, 64]]),
                            AluOpType.add))
                    on("vector", v_ev)
                    prev_vev[0] = V.v
                v_v1 = V.v

                def v_inv(v):
                    v.wait_ge(P.sem, p_norm_done)
                    if is_main:
                        v.tensor_scalar(proj_f[0:16, 0:512], PD[0:16, 0:512], 1e-24, None,
                                        AluOpType.max)
                    V.tag(v.tensor_scalar(poly_f[0:16, 0:512], PS[0:16, 0:512], 1e-24, None,
                                          AluOpType.max))
                on("vector", v_inv)
                v_invp = V.v

                def a_inv(s):
                    s.wait_ge(V.sem, v_invp)
                    if is_main:
                        s.activation(proj_f[0:16, 0:512], proj_f[0:16, 0:512], AF.Ln)
                        s.activation(invq_b[0:16, 0:512], proj_f[0:16, 0:512], AF.Exp, scale=-0.5)
                    s.activation(poly_f[0:16, 0:512], poly_f[0:16, 0:512], AF.Ln)
                    A.tag(s.activation(invk_b[0:16, 0:512], poly_f[0:16, 0:512], AF.Exp, scale=-0.5))
                on("scalar", a_inv)
                a_invc = A.v

                chain_war_v = [0]
                chain_war_a = [0]

                def emit_chain(which, j):
                    src_b = qf_b if which == "q" else kf_b
                    inv_b = invq_b if which == "q" else invk_b

                    def t_c(t, j=j, src_b=src_b, inv_b=inv_b):
                        t.wait_ge(A.sem, a_invc)
                        t.wait_ge(V.sem, v_keep_done)
                        if chain_war_v[0]:
                            t.wait_ge(V.sem, chain_war_v[0])
                        if chain_war_a[0]:
                            t.wait_ge(A.sem, chain_war_a[0])
                        t.matmul(PC[:, 0:512], omgbd_b[:, 128 * j:128 * (j + 1)],
                                 src_b[:, 512 * j:512 * (j + 1)], start=True, stop=True)
                        P.tag(t.matmul(PD[:, 0:512], gbd_b[:, 128 * j:128 * (j + 1)],
                                 inv_b[0:16, 0:512], start=True, stop=True))
                    on("tensor", t_c)
                    p_c = P.v

                    def a_c1(s):
                        s.wait_ge(P.sem, p_c)
                        A.tag(s.activation(proj_f[:, :], PC[:, 0:512], AF.Copy))
                    on("scalar", a_c1)
                    a_c1c = A.v

                    def v_c1(v):
                        v.wait_ge(A.sem, a_c1c)
                        V.tag(v.tensor_tensor(proj_f[:, :], proj_f[:, :], PD[:, 0:512],
                                              AluOpType.mult))
                    on("vector", v_c1)
                    v_c1c = V.v

                    def a_c2(s):
                        s.wait_ge(V.sem, v_c1c)
                        A.tag(s.activation(poly_f[:, :], proj_f[:, :], AF.Square, scale=S4))
                    on("scalar", a_c2)
                    a_c2c = A.v

                    def v_c2(v):
                        v.wait_ge(A.sem, a_c2c)
                        v.tensor_scalar(proj_f[:, :], proj_f[:, :], qsc_f[:, 0:1], qsc_f[:, 1:2],
                                        AluOpType.mult, AluOpType.add)
                        V.tag(v.tensor_scalar(proj_f[:, :], proj_f[:, :], -20.0, 20.0,
                                              AluOpType.max, AluOpType.min))
                    on("vector", v_c2)
                    v_c2c = V.v

                    def a_c3(s):
                        s.wait_ge(V.sem, v_c2c)
                        A.tag(s.activation(prf_f[:, :], proj_f[:, :], AF.Exp,
                                           bias=qsc_f[:, 2:3]))
                    on("scalar", a_c3)
                    chain_war_a[0] = A.v
                    a_c3c = A.v

                    def v_c3(v, j=j, src_b=src_b):
                        v.wait_ge(A.sem, a_c3c)
                        if is_main:
                            V.tag(v.tensor_tensor(src_b[:, 512 * j:512 * (j + 1)], poly_f[:, :],
                                                  prf_f[:, :], AluOpType.mult))
                        else:
                            v.wait_ge(G.sem, mil["pz"])
                            v.tensor_tensor(src_b[:, 512 * j:512 * (j + 1)], poly_f[:, :],
                                            prf_f[:, :], AluOpType.mult)
                            V.tag(v.tensor_scalar(src_b[:, 512 * j:512 * (j + 1)],
                                                  src_b[:, 512 * j:512 * (j + 1)],
                                                  pz_f[:, 0:1], None, AluOpType.mult))
                    on("vector", v_c3)
                    chain_war_v[0] = V.v

                for j in range(NPAIR):
                    if is_main:
                        emit_chain("q", j)
                    emit_chain("k", j)
                v_chains = V.v

                tp_batches = []
                jobs_tp = [(j, i) for j in range(NPAIR) for i in range(4)]
                for b0 in range(0, 32, 8):
                    batch = jobs_tp[b0:b0 + 8]

                    def t_tp(t, batch=batch, b0=b0):
                        t.wait_ge(V.sem, v_chains)
                        if tp_batches:
                            t.wait_ge(V.sem, tp_batches[-1])
                        for s_i, (j, i) in enumerate(batch):
                            last = t.transpose(Pbf[:, 128 * s_i:128 * (s_i + 1)],
                                        kf_b[:, 512 * j + 128 * i:512 * j + 128 * (i + 1)],
                                        ident_b[:, :])
                        P.tag(last)
                    on("tensor", t_tp)
                    p_tp = P.v

                    def v_tp(v, batch=batch):
                        v.wait_ge(P.sem, p_tp)
                        for s_i, (j, i) in enumerate(batch):
                            last = v.tensor_copy(kftm_b[:, 1024 * i + 128 * j:1024 * i + 128 * (j + 1)],
                                          Pbf[:, 128 * s_i:128 * (s_i + 1)])
                        V.tag(last)
                    on("vector", v_tp)
                    tp_batches.append(V.v)
                v_kftm = V.v

                if is_main:
                    def a_reg(j2):
                        return [(PA, 0), (PA, 512), (PB, 0), (PA, 896)][j2]
                    asb_off = [0, 512, 896, 1152]

                    for h in range(NH):
                        j = h // 2
                        par = h % 2
                        bp = 64 * par

                        def t_A(t, j=j, bp=bp):
                            t.wait_ge(V.sem, v_chains)
                            if war_A[0]:
                                t.wait_ge(V.sem, war_A[0])
                            for j2 in range(4):
                                n2 = 512 - 128 * j2
                                pt, c0 = a_reg(j2)
                                last = t.matmul(pt[:, c0:c0 + n2],
                                         kf_b[bp:bp + 64, 512 * j + 128 * j2:512 * j + 128 * (j2 + 1)],
                                         qf_b[bp:bp + 64, 512 * j + 128 * j2:512 * j + 512],
                                         start=True, stop=True)
                            P.tag(last)
                        on("tensor", t_A)
                        p_A = P.v

                        def v_evA(v):
                            v.wait_ge(P.sem, p_A)
                            for j2 in range(4):
                                n2 = 512 - 128 * j2
                                pt, c0 = a_reg(j2)
                                d0 = asb_off[j2]
                                last = v.tensor_tensor(asb_b[:, d0:d0 + 128], pt[:, c0:c0 + 128],
                                                maskT_b[:, :], AluOpType.mult)
                                if n2 > 128:
                                    last = v.tensor_copy(asb_b[:, d0 + 128:d0 + n2],
                                                  pt[:, c0 + 128:c0 + n2])
                            V.tag(last)
                        on("vector", v_evA)
                        war_A[0] = V.v
                        v_evAc = V.v

                        def t_ctx(t, h=h, j=j, par=par):
                            t.wait_ge(V.sem, v_evAc)
                            t.wait_ge(V.sem, v_v1)
                            t.wait_ge(V.sem, war_state[0])
                            if war_ctx[0]:
                                t.wait_ge(V.sem, war_ctx[0])
                            st_b = sa_b if par == 0 else sbp_b
                            t.matmul(PB[0:65, 512:1024], st_b[:, 65 * j:65 * (j + 1)],
                                     qf_b[:, 512 * j:512 * (j + 1)], start=True, stop=False)
                            for j2 in range(4):
                                n2 = 512 - 128 * j2
                                last = t.matmul(PB[0:65, 512 + 128 * j2:1024],
                                         v1_b[:, 1040 * j2 + 65 * h:1040 * j2 + 65 * (h + 1)],
                                         asb_b[:, asb_off[j2]:asb_off[j2] + n2],
                                         start=False, stop=(j2 == 3))
                            P.tag(last)
                        on("tensor", t_ctx)
                        p_ctx = P.v

                        def v_ctx(v):
                            v.wait_ge(P.sem, p_ctx)
                            v.tensor_copy(ctx_f[0:65, :], PB[0:65, 512:1024])
                            v.tensor_scalar(ctx_f[64:65, :], ctx_f[64:65, :], 1e-6, None,
                                            AluOpType.max)
                            with nc.allow_low_precision(reason="bf16 matmul operand"):
                                V.tag(v.reciprocal(denr_b[64:65, :], ctx_f[64:65, :]))
                        on("vector", v_ctx)
                        war_ctx[0] = V.v
                        v_ctxc = V.v

                        def t_bden(t):
                            t.wait_ge(V.sem, v_ctxc)
                            if war_bden[0]:
                                t.wait_ge(V.sem, war_bden[0])
                            P.tag(t.matmul(PD[0:64, 0:512], ones_b[64:65, 0:64],
                                           denr_b[64:65, :], start=True, stop=True))
                        on("tensor", t_bden)
                        p_bden = P.v

                        def v_div(v, j=j, par=par):
                            v.wait_ge(P.sem, p_bden)
                            if par == 0:
                                V.tag(v.tensor_tensor(attn_b[0:64, 512 * j:512 * (j + 1)],
                                                      ctx_f[0:64, :], PD[0:64, 0:512],
                                                      AluOpType.mult))
                            else:
                                V.tag(v.tensor_tensor(attnh_b[0:64, :], ctx_f[0:64, :],
                                                      PD[0:64, 0:512], AluOpType.mult))
                        on("vector", v_div)
                        war_bden[0] = V.v
                        v_divc = V.v

                        if par == 1:
                            def t_mv(t):
                                t.wait_ge(V.sem, v_divc)
                                if war_mv[0]:
                                    t.wait_ge(V.sem, war_mv[0])
                                P.tag(t.matmul(PS[64:128, 0:512], ident_b[0:64, 0:64],
                                               attnh_b[0:64, :], start=True, stop=True,
                                               tile_position=(0, 64)))
                            on("tensor", t_mv)
                            p_mv = P.v

                            def v_mv(v, j=j):
                                v.wait_ge(P.sem, p_mv)
                                V.tag(v.tensor_copy(attn_b[64:128, 512 * j:512 * (j + 1)],
                                                    PS[64:128, 0:512]))
                            on("vector", v_mv)
                            war_mv[0] = V.v

                def t_delta(t):
                    t.wait_ge(V.sem, v_kftm)
                    t.wait_ge(V.sem, v_v1)
                    if is_main and war_A[0]:
                        t.wait_ge(V.sem, war_A[0])
                    if war_state[0]:
                        t.wait_ge(V.sem, war_state[0])
                    for j in range(NPAIR):
                        for par in range(2):
                            h = 2 * j + par
                            for i in range(4):
                                if par == 0:
                                    t.matmul(PA[0:64, 128 * j:128 * j + 65],
                                             kftm_b[:, 1024 * i + 128 * j:1024 * i + 128 * j + 64],
                                             v1_b[:, 1040 * i + 65 * h:1040 * i + 65 * (h + 1)],
                                             start=(i == 0), stop=(i == 3))
                                else:
                                    last = t.matmul(PA[64:128, 128 * j:128 * j + 65],
                                             kftm_b[:, 1024 * i + 128 * j + 64:1024 * i + 128 * (j + 1)],
                                             v1_b[:, 1040 * i + 65 * h:1040 * i + 65 * (h + 1)],
                                             start=(i == 0), stop=(i == 3),
                                             tile_position=(0, 64))
                    P.tag(last)
                on("tensor", t_delta)
                p_delta = P.v

                def v_state(v):
                    v.wait_ge(P.sem, p_delta)
                    v.tensor_tensor(
                        bass.AP(state_f, 0, [[1024, 128], [128, 8], [1, 65]]),
                        bass.AP(state_f, 0, [[1024, 128], [128, 8], [1, 65]]),
                        bass.AP(PA, 0, [[1024, 128], [128, 8], [1, 65]]),
                        AluOpType.add)
                    v.tensor_scalar(
                        bass.AP(sa_b, 0, [[520, 64], [65, 8], [1, 65]]),
                        bass.AP(state_f, 0, [[1024, 64], [128, 8], [1, 65]]),
                        1.0, None, AluOpType.mult)
                    V.tag(v.tensor_scalar(
                        bass.AP(sbp_b, 64 * 520, [[520, 64], [65, 8], [1, 65]]),
                        bass.AP(state_f, 64 * 1024, [[1024, 64], [128, 8], [1, 65]]),
                        1.0, None, AluOpType.mult))
                on("vector", v_state)
                war_state[0] = V.v

                if is_main:
                    v_attn_all = V.v
                    sl0 = 4 * (ci % 2)
                    for i in range(4):
                        def g_xtm(g, i=i):
                            if war_xtm[i % 2]:
                                g.wait_ge(V.sem, war_xtm[i % 2])
                            G.tag(g.dma_start(xtm_f[:, 1024 * (i % 2):1024 * (i % 2) + 1024],
                                              xm_h.ap()[ci * C + 128 * i:ci * C + 128 * (i + 1), :]))
                        on("gpsimd", g_xtm)
                        g_xtm_c = G.v

                        def t_proj(t, i=i):
                            t.wait_ge(V.sem, v_attn_all)
                            t.wait_ge(X.sem, x_wo)
                            if war_proj[0]:
                                t.wait_ge(V.sem, war_proj[0])
                            for ob in range(2):
                                dst = PC if ob == 0 else PS
                                for cc in range(8):
                                    last = t.matmul(dst[:, 0:512],
                                             attn_b[:, 512 * cc + 128 * i:512 * cc + 128 * (i + 1)],
                                             woutT_sb[:, 1024 * cc + 512 * ob:1024 * cc + 512 * (ob + 1)],
                                             start=(cc == 0), stop=(cc == 7))
                            P.tag(last)
                        on("tensor", t_proj)
                        p_proj = P.v

                        def v_ln(v, i=i):
                            v.wait_ge(P.sem, p_proj)
                            v.wait_ge(G.sem, g_xtm_c)
                            v.tensor_tensor(r_f[:, 0:512], PC[:, 0:512], boutbc_f[:, 0:512],
                                            AluOpType.add)
                            v.tensor_tensor(r_f[:, 512:1024], PS[:, 0:512], boutbc_f[:, 512:1024],
                                            AluOpType.add)
                            v.tensor_tensor(r_f[:, :], r_f[:, :],
                                            xtm_f[:, 1024 * (i % 2):1024 * (i % 2) + 1024],
                                            AluOpType.add)
                            v.bn_stats(bna_f[:, 0:6], r_f[:, 0:512])
                            V.tag(v.bn_stats(bna_f[:, 6:12], r_f[:, 512:1024]))
                            v.wait_ge(V.sem, V.v)
                            v.bn_aggr(bna_f[:, 12:14], bna_f[:, 0:6])
                            V.tag(v.bn_aggr(bna_f[:, 14:16], bna_f[:, 6:12]))
                            v.wait_ge(V.sem, V.v)
                            v.tensor_tensor(bna_f[:, 16:17], bna_f[:, 12:13],
                                            bna_f[:, 14:15], AluOpType.add)
                            v.tensor_tensor(bna_f[:, 17:18], bna_f[:, 12:13],
                                            bna_f[:, 14:15], AluOpType.subtract)
                            V.tag(v.tensor_tensor(bna_f[:, 18:19], bna_f[:, 13:14],
                                            bna_f[:, 15:16], AluOpType.add))
                            v.wait_ge(V.sem, V.v)
                            V.tag(v.tensor_tensor(bna_f[:, 19:20], bna_f[:, 17:18],
                                            bna_f[:, 17:18], AluOpType.mult))
                            v.wait_ge(V.sem, V.v)
                            V.tag(v.tensor_scalar(bna_f[:, 19:20], bna_f[:, 19:20],
                                            0.5, None, AluOpType.mult))
                            v.wait_ge(V.sem, V.v)
                            V.tag(v.tensor_tensor(bna_f[:, 18:19], bna_f[:, 18:19],
                                                  bna_f[:, 19:20], AluOpType.add))
                        on("vector", v_ln)
                        v_lnc = V.v
                        war_proj[0] = V.v

                        def a_rstd(s):
                            s.wait_ge(V.sem, v_lnc)
                            s.activation(rstd_f[:, 0:1], bna_f[:, 18:19], AF.Ln,
                                         bias=qsc_f[:, 4:5])
                            A.tag(s.activation(rstd_f[:, 0:1], rstd_f[:, 0:1], AF.Exp,
                                               scale=-0.5, bias=qsc_f[:, 5:6]))
                        on("scalar", a_rstd)
                        a_rstdc = A.v

                        def v_h1(v, i=i, sl0=sl0):
                            v.wait_ge(A.sem, a_rstdc)
                            V.tag(v.tensor_scalar(rstd_f[:, 1:2], bna_f[:, 16:17], rstd_f[:, 0:1],
                                            -0.5, AluOpType.mult, AluOpType.mult))
                            v.wait_ge(V.sem, V.v)
                            V.tag(v.tensor_scalar(
                                h1f[:, 1024 * (sl0 + i):1024 * (sl0 + i + 1)],
                                r_f[:, :], rstd_f[:, 0:1], rstd_f[:, 1:2],
                                AluOpType.mult, AluOpType.add))
                        on("vector", v_h1)
                        war_xtm[i % 2] = V.v

                barrier()

            if seg == 0:
                for pc in range(n_prefix_chunks):
                    emit_chunk("prefix", pc)
                emit_sharedT()
            for k in range(2):
                mc = 2 * seg + k
                if mc < n_main_chunks:
                    emit_chunk("main", mc)

            barrier()

        with contextlib.ExitStack() as fsx:
            fsb = lambda name, shape, dt: fsx.enter_context(
                nc.sbuf_tensor(f"{name}_f{seg}", shape, dt))
            h1b_b = fsb("h1b", [128, 2048], BF16)
            h1fm_b = fsb("h1fm", [128, 8192], BF16)
            acc_f = fsb("acc", [128, 8192], F32)
            wgu_sb = fsb("wgu", [128, 16384], BF16)
            wdn_sb = fsb("wdn", [128, 8192], BF16)
            swgt_sb = fsb("swgt", [128, 2048], BF16)
            swut_sb = fsb("swut", [128, 2048], BF16)
            swdt_sb = fsb("swdt", [128, 2048], BF16)
            silu_f = fsb("silu", [128, 512], F32)
            he_f = fsb("he", [128, 512], F32)
            hew_b = fsb("hew", [128, 1024], BF16)
            shsil_f = fsb("shsil", [128, 512], F32)
            shhe_f = fsb("shhe", [128, 256], F32)
            shhew_b = fsb("shhew", [128, 256], BF16)
            logtm_f = fsb("logtm", [128, 16], F32)
            router_f = fsb("router", [128, 64], F32)
            comb_b = fsb("comb", [128, 16], BF16)
            combfm_b = fsb("combfm", [16, 1024], BF16)
            srow_b = fsb("srow", [1, 1024], BF16)
            bce_b = fsb("bce", [128, 1024], BF16)
            bs_f = fsb("bs", [128, 1024], F32)
            r2_f = fsb("r2", [128, 1024], F32)
            outst_f = fsb("outst", [128, 1024], F32)
            bna2_f = fsb("bna2", [128, 20], F32)
            rstd2_f = fsb("rstd2", [128, 2], F32)

            with nc.psum_tensor(f"PbfT_f{seg}", [128, 1024], BF16) as PbfT, \
                 nc.psum_tensor(f"PFT_f{seg}", [128, 1024], F32) as PFT:
                def v_acc0(v):
                    V.tag(v.memset(acc_f[:, :], 0.0))
                on("vector", v_acc0)

                tpb = []
                h1bc = [0, 0]
                for ts in range(8):
                    hslot = ts % 2

                    def v_h1b(v, ts=ts, hslot=hslot):
                        if len(tpb) >= 2:
                            v.wait_ge(P.sem, tp_pc[ts - 2])
                        V.tag(v.tensor_copy(h1b_b[:, 1024 * hslot:1024 * (hslot + 1)],
                                            h1f[:, 1024 * ts:1024 * (ts + 1)]))
                    on("vector", v_h1b)
                    h1bc[hslot] = V.v
                    if ts == 0:
                        tp_pc = {}

                    def t_tph(t, ts=ts, hslot=hslot):
                        t.wait_ge(V.sem, h1bc[hslot])
                        if tpb:
                            t.wait_ge(V.sem, tpb[-1])
                        for dc in range(8):
                            last = t.transpose(PbfT[:, 128 * dc:128 * (dc + 1)],
                                        h1b_b[:, 1024 * hslot + 128 * dc:1024 * hslot + 128 * (dc + 1)],
                                        ident_b[:, :])
                        P.tag(last)
                    on("tensor", t_tph)
                    tp_pc[ts] = P.v
                    p_tph = P.v

                    def v_tph(v, ts=ts):
                        v.wait_ge(P.sem, p_tph)
                        for dc in range(8):
                            last = v.tensor_copy(h1fm_b[:, 1024 * dc + 128 * ts:1024 * dc + 128 * (ts + 1)],
                                          PbfT[:, 128 * dc:128 * (dc + 1)])
                        V.tag(last)
                    on("vector", v_tph)
                    tpb.append(V.v)
                v_h1fm = V.v

                def t_log(t):
                    t.wait_ge(V.sem, v_h1fm)
                    for half in range(2):
                        for dc in range(8):
                            last = t.matmul(PFT[0:16, 512 * half:512 * (half + 1)],
                                     gwT_b[:, 16 * dc:16 * (dc + 1)],
                                     h1fm_b[:, 1024 * dc + 512 * half:1024 * dc + 512 * (half + 1)],
                                     start=(dc == 0), stop=(dc == 7))
                    P.tag(last)
                on("tensor", t_log)
                p_log = P.v

                def v_log(v):
                    v.wait_ge(P.sem, p_log)
                    V.tag(v.tensor_copy(combfm_b[0:16, 0:1024], PFT[0:16, 0:1024]))
                on("vector", v_log)
                v_logc = V.v

                def t_sgw(t):
                    t.wait_ge(V.sem, v_logc)
                    for half in range(2):
                        for dc in range(8):
                            last = t.matmul(PFT[0:1, 512 * half:512 * (half + 1)],
                                     sgwT_b[:, dc:dc + 1],
                                     h1fm_b[:, 1024 * dc + 512 * half:1024 * dc + 512 * (half + 1)],
                                     start=(dc == 0), stop=(dc == 7))
                    P.tag(last)
                on("tensor", t_sgw)
                p_sgw = P.v

                def a_sgm(s):
                    s.wait_ge(P.sem, p_sgw)
                    A.tag(s.activation(srow_b[0:1, :], PFT[0:1, 0:1024], AF.Sigmoid))
                on("scalar", a_sgm)
                a_sgmc = A.v

                war_rt = [0]
                for ts in range(8):
                    def t_ltp(t, ts=ts):
                        t.wait_ge(V.sem, v_logc)
                        t.wait_ge(A.sem, a_sgmc)
                        if war_rt[0]:
                            t.wait_ge(V.sem, war_rt[0])
                        P.tag(t.transpose(PbfT[:, 128:144],
                                          combfm_b[0:16, 128 * ts:128 * (ts + 1)],
                                          ident_b[0:16, 0:16]))
                    on("tensor", t_ltp)
                    p_ltp = P.v

                    def v_rt(v):
                        v.wait_ge(P.sem, p_ltp)
                        V.tag(v.tensor_copy(logtm_f[:, 0:16], PbfT[:, 128:144]))
                        v.wait_ge(V.sem, V.v)
                        V.tag(v.reduce_max(router_f[:, 0:1], logtm_f[:, 0:16], axis=AX))
                        v.wait_ge(V.sem, V.v)
                        V.tag(v.tensor_scalar(router_f[:, 16:32], logtm_f[:, 0:16], router_f[:, 0:1],
                                        None, AluOpType.is_equal))
                        v.wait_ge(V.sem, V.v)
                        V.tag(v.tensor_scalar(router_f[:, 32:48], router_f[:, 16:32], -1e30, None,
                                        AluOpType.mult))
                        v.wait_ge(V.sem, V.v)
                        V.tag(v.tensor_tensor(router_f[:, 32:48], router_f[:, 32:48], logtm_f[:, 0:16],
                                        AluOpType.add))
                        v.wait_ge(V.sem, V.v)
                        V.tag(v.reduce_max(router_f[:, 1:2], router_f[:, 32:48], axis=AX))
                        v.wait_ge(V.sem, V.v)
                        V.tag(v.tensor_scalar(router_f[:, 48:64], router_f[:, 32:48], router_f[:, 1:2],
                                              None, AluOpType.is_equal))
                        V.tag(v.tensor_scalar(router_f[:, 2:3], router_f[:, 0:1], router_f[:, 1:2],
                                              None, AluOpType.subtract))
                    on("vector", v_rt)
                    v_rtc = V.v

                    def a_rt(s):
                        s.wait_ge(V.sem, v_rtc)
                        A.tag(s.activation(router_f[:, 3:4], router_f[:, 2:3], AF.Sigmoid))
                    on("scalar", a_rt)
                    a_rtc = A.v

                    def v_cb(v):
                        v.wait_ge(A.sem, a_rtc)
                        V.tag(v.tensor_scalar(router_f[:, 4:5], router_f[:, 3:4], -1.0, 1.0,
                                        AluOpType.mult, AluOpType.add))
                        v.wait_ge(V.sem, V.v)
                        V.tag(v.tensor_scalar(router_f[:, 16:32], router_f[:, 16:32], router_f[:, 3:4],
                                        None, AluOpType.mult))
                        V.tag(v.tensor_scalar(router_f[:, 48:64], router_f[:, 48:64], router_f[:, 4:5],
                                        None, AluOpType.mult))
                        v.wait_ge(V.sem, V.v)
                        V.tag(v.tensor_tensor(comb_b[:, 0:16], router_f[:, 16:32],
                                              router_f[:, 48:64], AluOpType.add))
                    on("vector", v_cb)
                    v_cbc = V.v

                    def t_ctp(t, ts=ts):
                        t.wait_ge(V.sem, v_cbc)
                        P.tag(t.transpose(PbfT[0:16, 0:128], comb_b[:, 0:16], ident_b[:, :]))
                    on("tensor", t_ctp)
                    p_ctp = P.v

                    def v_ctp(v, ts=ts):
                        v.wait_ge(P.sem, p_ctp)
                        V.tag(v.tensor_copy(combfm_b[0:16, 128 * ts:128 * (ts + 1)],
                                            PbfT[0:16, 0:128]))
                    on("vector", v_ctp)
                    war_rt[0] = V.v

                prev_bs = [0]
                for wt in range(NWT):
                    def t_bs(t, wt=wt):
                        t.wait_ge(A.sem, a_sgmc)
                        if prev_bs[0]:
                            t.wait_ge(V.sem, prev_bs[0])
                        P.tag(t.matmul(PFT[:, 0:256], ones_b[0:1, 0:128],
                                       srow_b[0:1, 256 * wt:256 * (wt + 1)],
                                       start=True, stop=True))
                    on("tensor", t_bs)
                    p_bs = P.v

                    def v_bsc(v, wt=wt):
                        v.wait_ge(P.sem, p_bs)
                        V.tag(v.tensor_copy(bs_f[:, 256 * wt:256 * (wt + 1)], PFT[:, 0:256]))
                    on("vector", v_bsc)
                    prev_bs[0] = V.v
                v_scopeT = V.v

            with nc.psum_tensor(f"QGU_f{seg}", [128, 2048], F32) as QGU, \
                 nc.psum_tensor(f"QDN_f{seg}", [128, 2048], F32) as QDN:

                gcol = lambda g_: 512 * (g_ % 2) + 256 * (g_ // 2)
                moe_load = {}
                moe_guard = {}
                warQ_a = [0, 0]
                warQ_v = [v_scopeT, v_scopeT]
                war_hew_p = [0, 0, 0, 0]
                war_qdn = [0]
                war_bce = [0]

                def g_moe(g, e):
                    slot = e % 2
                    if (e - 2) in moe_guard:
                        g.wait_ge(P.sem, moe_guard[e - 2])
                    g.wait_ge(G.sem, mil["moe_bf"])
                    G.tag(g.dma_start(
                        bass.AP(wgu_sb, 8192 * slot, [[16384, 128], [1024, 8], [1, 1024]]),
                        bass.AP(wgu_bf_h, e * 1024 * 1024, [[1024, 128], [131072, 8], [1, 1024]])))
                    G.tag(g.dma_start(
                        bass.AP(wdn_sb, 4096 * slot, [[8192, 128], [1024, 4], [1, 1024]]),
                        bass.AP(wdn_bf_h, e * 512 * 1024, [[1024, 128], [131072, 4], [1, 1024]])))
                    moe_load[e] = G.v

                def g_moe01(g):
                    g_moe(g, 0)
                    g_moe(g, 1)
                on("gpsimd", g_moe01)

                for e in range(NE):
                    slot = e % 2

                    for wt in range(NWT):
                        def t_b1(t, e=e, wt=wt):
                            if wt == 0:
                                t.wait_ge(A.sem, warQ_a[0])
                                t.wait_ge(V.sem, warQ_v[0])
                            else:
                                t.wait_ge(V.sem, war_bce[0])
                            P.tag(t.matmul(QGU[:, 0:256], ece_b[:, 128 * e:128 * (e + 1)],
                                           combfm_b[0:16, 256 * wt:256 * (wt + 1)],
                                           start=True, stop=True))
                        on("tensor", t_b1)
                        p_b1 = P.v

                        def v_b1(v, wt=wt):
                            v.wait_ge(P.sem, p_b1)
                            V.tag(v.tensor_copy(bce_b[:, 256 * wt:256 * (wt + 1)], QGU[:, 0:256]))
                        on("vector", v_b1)
                        war_bce[0] = V.v
                    warQ_v[0] = war_bce[0]

                    for wt in range(NWT):
                        tc0 = W * wt
                        for gph in range(4):
                            def t_gu(t, e=e, gph=gph, slot=slot, tc0=tc0):
                                t.wait_ge(G.sem, moe_load[e])
                                if warQ_a[gph % 2]:
                                    t.wait_ge(A.sem, warQ_a[gph % 2])
                                if warQ_v[gph % 2]:
                                    t.wait_ge(V.sem, warQ_v[gph % 2])
                                gc = gcol(gph)
                                for dc in range(8):
                                    t.matmul(QGU[:, gc:gc + 256],
                                             wgu_sb[:, 8192 * slot + 1024 * dc + 128 * gph:
                                                    8192 * slot + 1024 * dc + 128 * (gph + 1)],
                                             h1fm_b[:, 1024 * dc + tc0:1024 * dc + tc0 + 256],
                                             start=(dc == 0), stop=(dc == 7))
                                for dc in range(8):
                                    last = t.matmul(QGU[:, 1024 + gc:1024 + gc + 256],
                                             wgu_sb[:, 8192 * slot + 1024 * dc + 128 * (gph + 4):
                                                    8192 * slot + 1024 * dc + 128 * (gph + 5)],
                                             h1fm_b[:, 1024 * dc + tc0:1024 * dc + tc0 + 256],
                                             start=(dc == 0), stop=(dc == 7))
                                P.tag(last)
                            on("tensor", t_gu)
                            p_gu = P.v

                            def a_sil(s, gph=gph):
                                s.wait_ge(P.sem, p_gu)
                                gc = gcol(gph)
                                A.tag(s.activation(silu_f[:, 256 * (gph % 2):256 * (gph % 2) + 256],
                                                   QGU[:, gc:gc + 256], AF.Silu))
                            on("scalar", a_sil)
                            warQ_a[gph % 2] = A.v
                            a_silc = A.v

                            def v_he(v, gph=gph, wt=wt):
                                v.wait_ge(A.sem, a_silc)
                                if war_hew_p[gph]:
                                    v.wait_ge(P.sem, war_hew_p[gph])
                                gc = gcol(gph)
                                v.tensor_tensor(he_f[:, 256 * (gph % 2):256 * (gph % 2) + 256],
                                                silu_f[:, 256 * (gph % 2):256 * (gph % 2) + 256],
                                                QGU[:, 1024 + gc:1024 + gc + 256],
                                                AluOpType.mult)
                                V.tag(v.tensor_tensor(hew_b[:, 256 * gph:256 * (gph + 1)],
                                                      he_f[:, 256 * (gph % 2):256 * (gph % 2) + 256],
                                                      bce_b[:, 256 * wt:256 * (wt + 1)],
                                                      AluOpType.mult))
                            on("vector", v_he)
                            warQ_v[gph % 2] = V.v
                            hew_done = V.v
                            war_hew_local = hew_done
                            if gph == 0:
                                hewv = [0, 0, 0, 0]
                            hewv[gph] = V.v

                        def t_dn(t, e=e, slot=slot, hewv=hewv):
                            if war_qdn[0]:
                                t.wait_ge(V.sem, war_qdn[0])
                            for fc in range(4):
                                t.wait_ge(V.sem, hewv[fc])
                                for th in range(2):
                                    for dh in range(2):
                                        last = t.matmul(QDN[:, 1024 * th + 512 * dh:1024 * th + 512 * (dh + 1)],
                                                 hew_b[:, 256 * fc + 128 * th:256 * fc + 128 * (th + 1)],
                                                 wdn_sb[:, 4096 * slot + 1024 * fc + 512 * dh:
                                                        4096 * slot + 1024 * fc + 512 * (dh + 1)],
                                                 start=(fc == 0), stop=(fc == 3))
                            P.tag(last)
                        on("tensor", t_dn)
                        p_dnc = P.v
                        for fc in range(4):
                            war_hew_p[fc] = P.v

                        def v_acc(v, wt=wt):
                            v.wait_ge(P.sem, p_dnc)
                            sl = 2 * wt
                            v.tensor_tensor(acc_f[:, 1024 * sl:1024 * (sl + 1)],
                                            acc_f[:, 1024 * sl:1024 * (sl + 1)],
                                            QDN[:, 0:1024], AluOpType.add)
                            V.tag(v.tensor_tensor(acc_f[:, 1024 * (sl + 1):1024 * (sl + 2)],
                                                  acc_f[:, 1024 * (sl + 1):1024 * (sl + 2)],
                                                  QDN[:, 1024:2048], AluOpType.add))
                        on("vector", v_acc)
                        war_qdn[0] = V.v
                    moe_guard[e] = P.v
                    if e + 2 < NE:
                        def g_pre(g, e=e):
                            g_moe(g, e + 2)
                        on("gpsimd", g_pre)

                sh_load = {}
                sh_guard = {}

                def g_sh(g, k, fb, wt):
                    slot = k % 2
                    if (k - 2) in sh_guard:
                        g.wait_ge(P.sem, sh_guard[k - 2])
                    g.wait_ge(G.sem, shT_state["done_g"])
                    G.tag(g.dma_start(
                        bass.AP(swgt_sb, 1024 * slot, [[2048, 128], [128, 8], [1, 128]]),
                        bass.AP(swgT_h, 128 * fb, [[2816, 128], [128 * 2816, 8], [1, 128]])))
                    G.tag(g.dma_start(
                        bass.AP(swut_sb, 1024 * slot, [[2048, 128], [128, 8], [1, 128]]),
                        bass.AP(swuT_h, 128 * fb, [[2816, 128], [128 * 2816, 8], [1, 128]])))
                    G.tag(g.dma_start(
                        bass.AP(swdt_sb, 1024 * slot, [[2048, 128], [1, 1024]]),
                        bass.AP(swdT_h, 128 * fb * 1024, [[1024, 128], [1, 1024]])))
                    sh_load[k] = G.v

                war_sh_p = [0]
                kglob = [0]

                for wt in range(NWT):
                    tc0 = W * wt
                    if wt == 0:
                        def g_sh01(g):
                            g_sh(g, 0, 0, 0)
                            g_sh(g, 1, 1, 0)
                        on("gpsimd", g_sh01)

                    for fb in range(NFB):
                        k = wt * NFB + fb
                        slot = k % 2
                        shc = 512 * (k % 2)

                        def t_shgu(t, k=k, slot=slot, tc0=tc0, shc=shc):
                            t.wait_ge(G.sem, sh_load[k])
                            if warQ_a[k % 2]:
                                t.wait_ge(A.sem, warQ_a[k % 2])
                            if warQ_v[k % 2]:
                                t.wait_ge(V.sem, warQ_v[k % 2])
                            for dc in range(8):
                                t.matmul(QGU[:, shc:shc + 256],
                                         swgt_sb[:, 1024 * slot + 128 * dc:1024 * slot + 128 * (dc + 1)],
                                         h1fm_b[:, 1024 * dc + tc0:1024 * dc + tc0 + 256],
                                         start=(dc == 0), stop=(dc == 7))
                            for dc in range(8):
                                last = t.matmul(QGU[:, 1024 + shc:1024 + shc + 256],
                                         swut_sb[:, 1024 * slot + 128 * dc:1024 * slot + 128 * (dc + 1)],
                                         h1fm_b[:, 1024 * dc + tc0:1024 * dc + tc0 + 256],
                                         start=(dc == 0), stop=(dc == 7))
                            P.tag(last)
                        on("tensor", t_shgu)
                        p_shgu = P.v

                        def a_shs(s, k=k, shc=shc):
                            s.wait_ge(P.sem, p_shgu)
                            A.tag(s.activation(shsil_f[:, 256 * (k % 2):256 * (k % 2) + 256],
                                               QGU[:, shc:shc + 256], AF.Silu))
                        on("scalar", a_shs)
                        warQ_a[k % 2] = A.v
                        a_shsc = A.v

                        def v_shhe(v, k=k, shc=shc, wt=wt):
                            v.wait_ge(A.sem, a_shsc)
                            if war_sh_p[0]:
                                v.wait_ge(P.sem, war_sh_p[0])
                            v.tensor_tensor(shhe_f[:, 0:256],
                                            shsil_f[:, 256 * (k % 2):256 * (k % 2) + 256],
                                            QGU[:, 1024 + shc:1024 + shc + 256],
                                            AluOpType.mult)
                            v.tensor_tensor(shhe_f[:, 0:256], shhe_f[:, 0:256],
                                            bs_f[:, 256 * wt:256 * (wt + 1)],
                                            AluOpType.mult)
                            V.tag(v.tensor_copy(shhew_b[:, 0:256], shhe_f[:, 0:256]))
                        on("vector", v_shhe)
                        warQ_v[k % 2] = V.v
                        v_shhec = V.v

                        def t_shdn(t, fb=fb, slot=slot):
                            t.wait_ge(V.sem, v_shhec)
                            if fb == 0 and war_qdn[0]:
                                t.wait_ge(V.sem, war_qdn[0])
                            for th in range(2):
                                for dh in range(2):
                                    last = t.matmul(QDN[:, 1024 * th + 512 * dh:1024 * th + 512 * (dh + 1)],
                                             shhew_b[:, 128 * th:128 * (th + 1)],
                                             swdt_sb[:, 1024 * slot + 512 * dh:1024 * slot + 512 * (dh + 1)],
                                             start=(fb == 0), stop=(fb == NFB - 1))
                            P.tag(last)
                        on("tensor", t_shdn)
                        sh_guard[k] = P.v
                        war_sh_p[0] = P.v
                        nk = k + 2
                        if nk < NWT * NFB:
                            nwt, nfb = nk // NFB, nk % NFB
                            def g_shp(g, nk=nk, nfb=nfb, nwt=nwt):
                                g_sh(g, nk, nfb, nwt)
                            on("gpsimd", g_shp)
                    p_dn_done = P.v

                    def v_accs(v, wt=wt):
                        v.wait_ge(P.sem, p_dn_done)
                        sl = 2 * wt
                        v.tensor_tensor(acc_f[:, 1024 * sl:1024 * (sl + 1)],
                                        acc_f[:, 1024 * sl:1024 * (sl + 1)],
                                        QDN[:, 0:1024], AluOpType.add)
                        V.tag(v.tensor_tensor(acc_f[:, 1024 * (sl + 1):1024 * (sl + 2)],
                                              acc_f[:, 1024 * (sl + 1):1024 * (sl + 2)],
                                              QDN[:, 1024:2048], AluOpType.add))
                    on("vector", v_accs)
                    war_qdn[0] = V.v

                    for th in range(2):
                        glob_t0 = seg * 1024 + wt * W + 128 * th
                        sl = 2 * wt + th

                        def v_ln2(v, sl=sl):
                            v.tensor_tensor(r2_f[:, :], acc_f[:, 1024 * sl:1024 * (sl + 1)],
                                            h1f[:, 1024 * sl:1024 * (sl + 1)], AluOpType.add)
                            v.bn_stats(bna2_f[:, 0:6], r2_f[:, 0:512])
                            V.tag(v.bn_stats(bna2_f[:, 6:12], r2_f[:, 512:1024]))
                            v.wait_ge(V.sem, V.v)
                            v.bn_aggr(bna2_f[:, 12:14], bna2_f[:, 0:6])
                            V.tag(v.bn_aggr(bna2_f[:, 14:16], bna2_f[:, 6:12]))
                            v.wait_ge(V.sem, V.v)
                            v.tensor_tensor(bna2_f[:, 16:17], bna2_f[:, 12:13],
                                            bna2_f[:, 14:15], AluOpType.add)
                            v.tensor_tensor(bna2_f[:, 17:18], bna2_f[:, 12:13],
                                            bna2_f[:, 14:15], AluOpType.subtract)
                            V.tag(v.tensor_tensor(bna2_f[:, 18:19], bna2_f[:, 13:14],
                                            bna2_f[:, 15:16], AluOpType.add))
                            v.wait_ge(V.sem, V.v)
                            V.tag(v.tensor_tensor(bna2_f[:, 19:20], bna2_f[:, 17:18],
                                            bna2_f[:, 17:18], AluOpType.mult))
                            v.wait_ge(V.sem, V.v)
                            V.tag(v.tensor_scalar(bna2_f[:, 19:20], bna2_f[:, 19:20],
                                            0.5, None, AluOpType.mult))
                            v.wait_ge(V.sem, V.v)
                            V.tag(v.tensor_tensor(bna2_f[:, 18:19], bna2_f[:, 18:19],
                                                  bna2_f[:, 19:20], AluOpType.add))
                        on("vector", v_ln2)
                        v_ln2c = V.v

                        def a_r2(s):
                            s.wait_ge(V.sem, v_ln2c)
                            s.activation(rstd2_f[:, 0:1], bna2_f[:, 18:19], AF.Ln,
                                         bias=qsc_f[:, 4:5])
                            A.tag(s.activation(rstd2_f[:, 0:1], rstd2_f[:, 0:1], AF.Exp,
                                               scale=-0.5, bias=qsc_f[:, 5:6]))
                        on("scalar", a_r2)
                        a_r2c = A.v

                        def v_out(v):
                            v.wait_ge(A.sem, a_r2c)
                            if out_slot_war[0]:
                                v.wait_ge(O.sem, out_slot_war[0])
                            V.tag(v.tensor_scalar(rstd2_f[:, 1:2], bna2_f[:, 16:17], rstd2_f[:, 0:1],
                                            -0.5, AluOpType.mult, AluOpType.mult))
                            v.wait_ge(V.sem, V.v)
                            V.tag(v.tensor_scalar(outst_f[:, 0:1024],
                                                  r2_f[:, :], rstd2_f[:, 0:1], rstd2_f[:, 1:2],
                                                  AluOpType.mult, AluOpType.add))
                        on("vector", v_out)
                        v_outc = V.v

                        def g_out(g, glob_t0=glob_t0):
                            g.wait_ge(V.sem, v_outc)
                            O.tag(g.dma_start(out_h.ap()[glob_t0:glob_t0 + 128, :],
                                              outst_f[:, 0:1024]))
                        on("gpsimd", g_out)
                        out_slot_war[0] = O.v

            barrier()

    def g_fin(g):
        g.wait_ge(O.sem, O.v)
    on("gpsimd", g_fin)

    es.close()
    return nc




def make_consts():
    cident = np.eye(128, dtype=np.float32)
    cmaskT = np.triu(np.ones((128, 128), np.float32))
    cgbd = np.zeros((16, 1024), np.float32)
    for j in range(8):
        for c in range(128):
            cgbd[2 * j + (c // 64), 128 * j + c] = 1.0
    cece = np.zeros((16, 2048), np.float32)
    for e in range(16):
        cece[e, 128 * e:128 * (e + 1)] = 1.0
    cnormE = np.zeros((128, 128), np.float32)
    for j in range(8):
        for p in range(128):
            cnormE[p, 16 * j + 2 * j + (p // 64)] = 1.0
    cpatT = np.zeros((2, 128), np.float32)
    for p in range(128):
        cpatT[(p // 32) % 2, p] = 1.0
    return dict(cident=cident, cmaskT=cmaskT, cgbd=cgbd, cece=cece,
                cnormE=cnormE, cpatT=cpatT)


def make_in_maps(inputs):
    x = np.ascontiguousarray(np.asarray(inputs["L0_attn_x"], dtype=np.float32))
    consts = make_consts()
    shared = dict(
        wqkv=np.ascontiguousarray(inputs["L0_attn_w_qkv"], dtype=np.float32),
        bqkv=np.ascontiguousarray(np.asarray(inputs["L0_attn_b_qkv"], np.float32).reshape(1, 3072)),
        wout=np.ascontiguousarray(inputs["L0_attn_w_out"], dtype=np.float32),
        bout=np.ascontiguousarray(np.asarray(inputs["L0_attn_b_out"], np.float32).reshape(1, 1024)),
        omg=np.ascontiguousarray(inputs["L0_attn_omega"], dtype=np.float32),
        qn=np.ascontiguousarray(np.asarray(inputs["L0_attn_quad_nodes"], np.float32).reshape(2, 1)),
        qw=np.ascontiguousarray(np.asarray(inputs["L0_attn_quad_weights"], np.float32).reshape(2, 1)),
        gw=np.ascontiguousarray(inputs["L0_ffn_gate_w"], dtype=np.float32),
        wgu=np.ascontiguousarray(inputs["L0_ffn_w_gate_up"], dtype=np.float32),
        wdn=np.ascontiguousarray(inputs["L0_ffn_w_down"], dtype=np.float32),
        swg=np.ascontiguousarray(inputs["L0_ffn_shared_w_gate"], dtype=np.float32),
        swu=np.ascontiguousarray(inputs["L0_ffn_shared_w_up"], dtype=np.float32),
        swd=np.ascontiguousarray(inputs["L0_ffn_shared_w_down"], dtype=np.float32),
        sgw=np.ascontiguousarray(inputs["L0_ffn_shared_gate_w"], dtype=np.float32),
        **consts,
    )
    zeros_pref = np.zeros((4096, 1024), np.float32)
    pz1 = np.ones((128, 1), np.float32)
    pz0 = np.zeros((128, 1), np.float32)
    in_maps = []
    for c in range(8):
        b, s = c // 2, c % 2
        xm = np.ascontiguousarray(x[b, s * 4096:(s + 1) * 4096])
        xp = np.ascontiguousarray(x[b, 0:4096]) if s == 1 else zeros_pref
        in_maps.append(dict(xm=xm, xp=xp, pz=(pz1 if s == 1 else pz0), **shared))
    return in_maps


def kernel(**inputs):
    in_maps = make_in_maps(inputs)
    nc = build_nc()
    res = run_bass_kernel_spmd(nc, in_maps, core_ids=list(range(8)))
    out = np.empty((4, 8192, 1024), np.float32)
    for c in range(8):
        b, s = c // 2, c % 2
        out[b, s * 4096:(s + 1) * 4096] = res.results[c]["out"]
    return out


# version 04
# speedup vs baseline: 1.0223x; 1.0223x over previous
import contextlib
import numpy as np
import ml_dtypes
import concourse.bass as bass
import concourse.mybir as mybir
from concourse.bass_utils import run_bass_kernel_spmd
from concourse.alu_op_type import AluOpType

F32 = mybir.dt.float32
BF16 = mybir.dt.bfloat16
AF = mybir.ActivationFunctionType
AX = mybir.AxisListType.X

NH = 16
NPAIR = 8
C = 512
NMC = 8
NPC = 8
W = 256
NWT = 4
NE = 16
SH_FF = 2816
NFB = SH_FF // 128

S4 = float(32.0 ** -0.25)
LN32_HALF = float(0.5 * np.log(32.0))


class Stream:
    def __init__(self, sem, step=1):
        self.sem = sem
        self.v = 0
        self.step = step

    def tag(self, instr, n=None):
        n = self.step if n is None else n
        instr.then_inc(self.sem, n)
        self.v += n
        return self.v


def build_nc(n_prefix_chunks=NPC, n_main_chunks=NMC):
    nc = bass.Bass("TRN2", target_bir_lowering=False)
    n_seg = max(1, (n_main_chunks + 1) // 2)

    def din(name, shape):
        return nc.dram_tensor(name, shape, F32, kind="ExternalInput")

    xm_h = din("xm", [4096, 1024])
    xp_h = din("xp", [4096, 1024])
    wqkv_h = din("wqkv", [3072, 1024])
    bqkv_h = din("bqkv", [1, 3072])
    wout_h = din("wout", [1024, 1024])
    bout_h = din("bout", [1, 1024])
    omg_h = din("omg", [2, 16, 64, 32])
    qn_h = din("qn", [2, 1])
    qw_h = din("qw", [2, 1])
    gw_h = din("gw", [16, 1024])
    wgu_h = din("wgu", [16, 1024, 1024])
    wdn_h = din("wdn", [16, 512, 1024])
    swg_h = din("swg", [2816, 1024])
    swu_h = din("swu", [2816, 1024])
    swd_h = din("swd", [1024, 2816])
    sgw_h = din("sgw", [1, 1024])
    cident_h = din("cident", [128, 128])
    cmaskT_h = din("cmaskT", [128, 128])
    cgbd_h = din("cgbd", [16, 1024])
    cece_h = din("cece", [16, 2048])
    cnormE_h = din("cnormE", [128, 128])
    cpatT_h = din("cpatT", [2, 128])
    pz_h = din("pz", [128, 1])

    out_h = nc.dram_tensor("out", [4096, 1024], F32, kind="ExternalOutput")

    xbf_h = nc.dram_tensor("xbf", [8192, 1024], BF16, kind="Internal")
    wqkv_bf_h = nc.dram_tensor("wqkv_bf", [3072, 1024], BF16, kind="Internal")
    wout_bf_h = nc.dram_tensor("wout_bf", [1024, 1024], BF16, kind="Internal")
    swg_bf_h = nc.dram_tensor("swg_bf", [2816, 1024], BF16, kind="Internal")
    swu_bf_h = nc.dram_tensor("swu_bf", [2816, 1024], BF16, kind="Internal")
    swd_bf_h = nc.dram_tensor("swd_bf", [1024, 2816], BF16, kind="Internal")
    swgT_h = nc.dram_tensor("swgT", [1024, 2816], BF16, kind="Internal")
    swuT_h = nc.dram_tensor("swuT", [1024, 2816], BF16, kind="Internal")
    swdT_h = nc.dram_tensor("swdT", [2816, 1024], BF16, kind="Internal")
    wgu_bf_h = nc.dram_tensor("wgu_bf", [16, 1024, 1024], BF16, kind="Internal")
    wdn_bf_h = nc.dram_tensor("wdn_bf", [16, 512, 1024], BF16, kind="Internal")

    es = contextlib.ExitStack()
    sb = lambda name, shape, dt: es.enter_context(nc.sbuf_tensor(name, shape, dt))

    ident_b = sb("ident_b", [128, 128], BF16)
    maskT_b = sb("maskT_b", [128, 128], BF16)
    ones_b = sb("ones_b", [128, 128], BF16)
    ones_f = sb("ones_f", [128, 128], F32)
    gbd_b = sb("gbd_b", [16, 1024], BF16)
    ece_b = sb("ece_b", [16, 2048], BF16)
    normE_b = sb("normE_b", [128, 128], BF16)
    patT_f = sb("patT_f", [2, 128], F32)
    q2_f = sb("q2_f", [2, 8], F32)
    pz_f = sb("pz_f", [128, 1], F32)
    qsc_f = sb("qsc_f", [128, 8], F32)
    bqkv_f = sb("bqkv_f", [128, 24], F32)
    bvbc_f = sb("bvbc_f", [128, 1024], F32)
    boutbc_f = sb("boutbc_f", [128, 1024], F32)
    omgbd_b = sb("omgbd_b", [128, 1024], BF16)
    gwT_b = sb("gwT_b", [128, 128], BF16)
    sgwT_b = sb("sgwT_b", [128, 8], BF16)
    state_f = sb("state_f", [128, 1024], F32)
    sa_b = sb("sa_b", [128, 520], BF16)
    sbp_b = sb("sbp_b", [128, 520], BF16)
    h1f = sb("h1f", [128, 8192], F32)
    stg_b = sb("stg_b", [128, 1024], BF16)

    G = Stream(es.enter_context(nc.semaphore("semG")), 16)
    X = Stream(es.enter_context(nc.semaphore("semX")), 16)
    P = Stream(es.enter_context(nc.semaphore("semP")), 1)
    V = Stream(es.enter_context(nc.semaphore("semV")), 1)
    A = Stream(es.enter_context(nc.semaphore("semA")), 1)
    O = Stream(es.enter_context(nc.semaphore("semO")), 16)

    block = es.enter_context(nc.Block())

    def on(engname, fn):
        getattr(block, engname)(fn)

    def barrier():
        vals = [(G, G.v), (X, X.v), (P, P.v), (V, V.v), (A, A.v)]
        for eng in ("tensor", "vector", "scalar", "gpsimd", "sync"):
            def _b(e, vals=vals):
                for s, v in vals:
                    if v:
                        e.wait_ge(s.sem, v)
            on(eng, _b)

    def v_pre(v):
        v.memset(ones_b[:, :], 1.0)
        v.memset(ones_f[:, :], 1.0)
        v.memset(omgbd_b[:, :], 0.0)
        v.memset(state_f[:, :], 0.0)
        v.memset(sa_b[:, :], 0.0)
        v.memset(sbp_b[:, :], 0.0)
        v.memset(qsc_f[:, :], 0.0)
        v.memset(qsc_f[:, 3:4], 1e-5)
        v.memset(qsc_f[:, 4:5], 2e-5)
        v.memset(qsc_f[:, 5:6], 0.34657359027997264)
        v.memset(bvbc_f[:, :], 0.0)
        V.tag(v.memset(boutbc_f[:, :], 0.0))
    on("vector", v_pre)
    v_memset = V.v

    mil = {}

    def g_startup(g):
        G.tag(g.dma_start(ident_b[:, :], cident_h.ap()))
        G.tag(g.dma_start(maskT_b[:, :], cmaskT_h.ap()))
        G.tag(g.dma_start(gbd_b[:, :], cgbd_h.ap()))
        G.tag(g.dma_start(ece_b[:, :], cece_h.ap()))
        G.tag(g.dma_start(normE_b[:, :], cnormE_h.ap()))
        G.tag(g.dma_start(patT_f[:, :], cpatT_h.ap()))
        G.tag(g.dma_start(q2_f[:, 0:1], qn_h.ap()))
        G.tag(g.dma_start(q2_f[:, 1:2], qw_h.ap()))
        mil["qnqw"] = G.v
        g.wait_ge(V.sem, v_memset)
        G.tag(g.dma_start(bqkv_f[:, :], bass.AP(bqkv_h, 0, [[1, 128], [128, 24]]),
                          allow_slow_non_contiguous=True))
        G.tag(g.dma_start(bvbc_f[0:1, :], bqkv_h.ap()[0:1, 2048:3072]))
        G.tag(g.dma_start(boutbc_f[0:1, :], bout_h.ap()))
        mil["biasrows"] = G.v
        for j in range(NPAIR):
            for par in range(2):
                for r in range(2):
                    c0 = 128 * j + 64 * par + 32 * r
                    G.tag(g.dma_start(omgbd_b[64 * par:64 * par + 64, c0:c0 + 32],
                                      omg_h.ap()[r, 2 * j + par]))
        for dc in range(8):
            G.tag(g.dma_start(gwT_b[:, 16 * dc:16 * dc + 16],
                              bass.AP(gw_h, 128 * dc, [[1, 128], [1024, 16]]),
                              allow_slow_non_contiguous=True))
        G.tag(g.dma_start(sgwT_b[:, :], bass.AP(sgw_h, 0, [[1, 128], [128, 8]]),
                          allow_slow_non_contiguous=True))
        G.tag(g.dma_start(pz_f[:, :], pz_h.ap()))
        mil["pz"] = G.v
        mil["smalls"] = G.v
        for pc in range(n_prefix_chunks):
            G.tag(g.dma_start(xbf_h.ap()[4096 + pc * C:4096 + (pc + 1) * C, :],
                              xp_h.ap()[pc * C:(pc + 1) * C, :]))
            mil[("xp", pc)] = G.v
        G.tag(g.dma_start(wqkv_bf_h.ap(), wqkv_h.ap()))
        mil["wqkv_bf"] = G.v
        G.tag(g.dma_start(wout_bf_h.ap(), wout_h.ap()))
        mil["wout_bf"] = G.v
        for mc in range(n_main_chunks):
            G.tag(g.dma_start(xbf_h.ap()[mc * C:(mc + 1) * C, :],
                              xm_h.ap()[mc * C:(mc + 1) * C, :]))
            mil[("xm", mc)] = G.v
        G.tag(g.dma_start(swg_bf_h.ap(), swg_h.ap()))
        mil["swg_bf"] = G.v
        G.tag(g.dma_start(swu_bf_h.ap(), swu_h.ap()))
        mil["swu_bf"] = G.v
        G.tag(g.dma_start(swd_bf_h.ap(), swd_h.ap()))
        mil["swd_bf"] = G.v
        for e in range(NE):
            G.tag(g.dma_start(wgu_bf_h.ap()[e], wgu_h.ap()[e]))
            G.tag(g.dma_start(wdn_bf_h.ap()[e], wdn_h.ap()[e]))
        mil["moe_bf"] = G.v
    on("gpsimd", g_startup)

    def v_quad(v):
        v.wait_ge(G.sem, mil["qnqw"])
        v.tensor_scalar(q2_f[:, 2:3], q2_f[:, 0:1], 2.0, 1e-30,
                        AluOpType.mult, AluOpType.max)
        v.tensor_scalar(q2_f[:, 4:5], q2_f[:, 1:2], 1e-30, None, AluOpType.max)
        V.tag(v.tensor_scalar(q2_f[:, 3:4], q2_f[:, 0:1], -1.0, None, AluOpType.mult))
    on("vector", v_quad)
    v_quad_c = V.v

    def a_quad(s):
        s.wait_ge(V.sem, v_quad_c)
        s.activation(q2_f[:, 2:3], q2_f[:, 2:3], AF.Ln)
        s.activation(q2_f[:, 2:3], q2_f[:, 2:3], AF.Exp, scale=0.5)
        A.tag(s.activation(q2_f[:, 4:5], q2_f[:, 4:5], AF.Ln))
    on("scalar", a_quad)
    a_quad_c = A.v

    def v_quad2(v):
        v.wait_ge(A.sem, a_quad_c)
        V.tag(v.tensor_scalar(q2_f[:, 4:5], q2_f[:, 4:5], 0.5, -LN32_HALF,
                              AluOpType.mult, AluOpType.add))
    on("vector", v_quad2)
    v_quad2_c = V.v

    with nc.psum_tensor("pst0", [128, 1024], F32) as pst0, \
         nc.psum_tensor("pst1", [128, 1024], F32) as pst1:
        def t_bc(t):
            t.wait_ge(V.sem, v_quad2_c)
            t.wait_ge(G.sem, mil["biasrows"])
            t.matmul(pst0[:, 0:3], patT_f[:, :], q2_f[:, 2:5], start=True, stop=True)
            t.matmul(pst0[:, 512:1024], ones_f[0:1, :], bvbc_f[0:1, 0:512], start=True, stop=True)
            t.matmul(pst1[:, 0:512], ones_f[0:1, :], bvbc_f[0:1, 512:1024], start=True, stop=True)
            P.tag(t.matmul(pst1[:, 512:1024], ones_f[0:1, :], boutbc_f[0:1, 0:512], start=True, stop=True))
        on("tensor", t_bc)
        p_bc = P.v

        def v_bc(v):
            v.wait_ge(P.sem, p_bc)
            v.tensor_copy(qsc_f[:, 0:3], pst0[:, 0:3])
            v.tensor_copy(bvbc_f[:, 0:512], pst0[:, 512:1024])
            v.tensor_copy(bvbc_f[:, 512:1024], pst1[:, 0:512])
            V.tag(v.tensor_copy(boutbc_f[:, 0:512], pst1[:, 512:1024]))
        on("vector", v_bc)
        v_bc_c = V.v

        def t_bc2(t):
            t.wait_ge(V.sem, v_bc_c)
            P.tag(t.matmul(pst0[:, 0:512], ones_f[0:1, :], boutbc_f[0:1, 512:1024],
                           start=True, stop=True))
        on("tensor", t_bc2)
        p_bc2 = P.v

        def v_bc2(v):
            v.wait_ge(P.sem, p_bc2)
            V.tag(v.tensor_copy(boutbc_f[:, 512:1024], pst0[:, 0:512]))
        on("vector", v_bc2)

    barrier()

    shT_jobs = []
    for (srcname, FF) in (("swg", SH_FF), ("swu", SH_FF)):
        for dc in range(8):
            for fb0 in range(0, FF, 512):
                shT_jobs.append((srcname, dc, fb0, min(512, FF - fb0)))
    for fc in range(SH_FF // 128):
        for db0 in range(0, 1024, 512):
            shT_jobs.append(("swd", fc, db0, 512))
    shT_state = {"done_g": None}

    def emit_sharedT():
        xbase = [X.v]
        gbase = [G.v]

        def x_shT(sy):
            sy.wait_ge(G.sem, mil["swg_bf"])
            seen_u = seen_d = False
            for idx, (kind, a0, b0, wdt) in enumerate(shT_jobs):
                if kind == "swu" and not seen_u:
                    sy.wait_ge(G.sem, mil["swu_bf"])
                    seen_u = True
                if kind == "swd" and not seen_d:
                    sy.wait_ge(G.sem, mil["swd_bf"])
                    seen_d = True
                if idx >= 2:
                    sy.wait_ge(G.sem, gbase[0] + (idx - 1) * 16)
                src_h = {"swg": swg_bf_h, "swu": swu_bf_h, "swd": swd_bf_h}[kind]
                slot = idx % 2
                X.tag(sy.dma_start(stg_b[:, 512 * slot:512 * slot + wdt],
                                   src_h.ap()[b0:b0 + wdt, 128 * a0:128 * a0 + 128],
                                   transpose=True))
        on("sync", x_shT)

        def g_shT(g):
            for idx, (kind, a0, b0, wdt) in enumerate(shT_jobs):
                g.wait_ge(X.sem, xbase[0] + (idx + 1) * 16)
                dst_h = {"swg": swgT_h, "swu": swuT_h, "swd": swdT_h}[kind]
                slot = idx % 2
                G.tag(g.dma_start(dst_h.ap()[128 * a0:128 * a0 + 128, b0:b0 + wdt],
                                  stg_b[:, 512 * slot:512 * slot + wdt]))
        on("gpsimd", g_shT)
        shT_state["done_g"] = G.v

    out_slot_war = [0]

    for seg in range(n_seg):
        with contextlib.ExitStack() as asx:
            asb = lambda name, shape, dt: asx.enter_context(
                nc.sbuf_tensor(f"{name}_s{seg}", shape, dt))
            wqkvT_sb = asb("wqkvT", [128, 24576], BF16)
            woutT_sb = asb("woutT", [128, 8192], BF16)
            xfm_b = asb("xfm", [128, 4096], BF16)
            xtm_f = asb("xtm", [128, 2048], F32)
            qkraw_b = asb("qkraw", [128, 1024], BF16)
            v1_b = asb("v1", [128, 4160], BF16)
            zsq_b = asb("zsq", [128, 1024], BF16)
            invq_b = asb("invq", [16, 512], BF16)
            invk_b = asb("invk", [16, 512], BF16)
            proj_f = asb("proj", [128, 512], F32)
            poly_f = asb("poly", [128, 512], F32)
            prf_f = asb("prf", [128, 512], F32)
            qf_b = asb("qf", [128, 4096], BF16)
            kf_b = asb("kf", [128, 4096], BF16)
            kftm_b = asb("kftm", [128, 4096], BF16)
            asb_b = asb("asbuf", [128, 1280], BF16)
            ctx_f = asb("ctx", [128, 512], F32)
            denr_b = asb("denr", [128, 512], BF16)
            attnh_b = asb("attnh", [64, 512], BF16)
            attn_b = asb("attn", [128, 4096], BF16)
            r_f = asb("r", [128, 1024], F32)
            bna_f = asb("bna", [128, 20], F32)
            rstd_f = asb("rstd", [128, 2], F32)

            PA = asx.enter_context(nc.psum_tensor(f"PA_s{seg}", [128, 1024], F32))
            PB = asx.enter_context(nc.psum_tensor(f"PB_s{seg}", [128, 1024], F32))
            PC = asx.enter_context(nc.psum_tensor(f"PC_s{seg}", [128, 512], F32))
            PD = asx.enter_context(nc.psum_tensor(f"PD_s{seg}", [128, 512], F32))
            PS = asx.enter_context(nc.psum_tensor(f"PS_s{seg}", [128, 512], F32))
            Pbf = asx.enter_context(nc.psum_tensor(f"Pbf_s{seg}", [128, 1024], BF16))

            def x_wT(sy):
                sy.wait_ge(G.sem, mil["wqkv_bf"])
                for dc in range(8):
                    for jb in range(6):
                        X.tag(sy.dma_start(
                            wqkvT_sb[:, 3072 * dc + 512 * jb:3072 * dc + 512 * jb + 512],
                            wqkv_bf_h.ap()[512 * jb:512 * jb + 512, 128 * dc:128 * dc + 128],
                            transpose=True))
                sy.wait_ge(G.sem, mil["wout_bf"])
                for cc in range(8):
                    for ob in range(2):
                        X.tag(sy.dma_start(
                            woutT_sb[:, 1024 * cc + 512 * ob:1024 * cc + 512 * ob + 512],
                            wout_bf_h.ap()[512 * ob:512 * ob + 512, 128 * cc:128 * cc + 128],
                            transpose=True))
            on("sync", x_wT)
            x_wq = X.v - 16 * 16
            x_wo = X.v

            war_A = [0]
            war_ctx = [0]
            war_bden = [0]
            war_mv = [0]
            war_state = [0]
            war_proj = [0]
            war_xtm = [0, 0]

            def emit_chunk(kind, ci):
                is_main = kind == "main"
                xrow = ci * C if is_main else 4096 + ci * C
                xcast_g = mil[("xm", ci)] if is_main else mil[("xp", ci)]

                def x_chunk(sy):
                    sy.wait_ge(G.sem, xcast_g)
                    for dc in range(8):
                        X.tag(sy.dma_start(xfm_b[:, 512 * dc:512 * dc + 512],
                                           xbf_h.ap()[xrow:xrow + C, 128 * dc:128 * dc + 128],
                                           transpose=True))
                on("sync", x_chunk)
                x_xfm = X.v

                evq = [0]
                prev_norm = [0]
                for j in range(NPAIR):
                    def t_qk(t, j=j):
                        t.wait_ge(X.sem, max(x_xfm, x_wq))
                        if evq[0]:
                            t.wait_ge(A.sem, evq[0])
                        for dc in range(8):
                            if is_main:
                                t.matmul(PA[:, 0:512],
                                         wqkvT_sb[:, 3072 * dc + 128 * j:3072 * dc + 128 * (j + 1)],
                                         xfm_b[:, 512 * dc:512 * dc + 512],
                                         start=(dc == 0), stop=(dc == 7))
                            last = t.matmul(PA[:, 512:1024],
                                     wqkvT_sb[:, 3072 * dc + 1024 + 128 * j:3072 * dc + 1024 + 128 * (j + 1)],
                                     xfm_b[:, 512 * dc:512 * dc + 512],
                                     start=(dc == 0), stop=(dc == 7))
                        P.tag(last)
                    on("tensor", t_qk)
                    p_qk = P.v

                    def a_evq(s, j=j):
                        s.wait_ge(P.sem, p_qk)
                        if is_main:
                            s.activation(qkraw_b[:, 0:512], PA[:, 0:512], AF.Identity,
                                         bias=bqkv_f[:, j:j + 1])
                        A.tag(s.activation(qkraw_b[:, 512:1024], PA[:, 512:1024], AF.Identity,
                                           bias=bqkv_f[:, 8 + j:9 + j]))
                    on("scalar", a_evq)
                    evq[0] = A.v

                    def a_sq(s, j=j):
                        if prev_norm[0]:
                            s.wait_ge(P.sem, prev_norm[0])
                        if is_main:
                            s.activation(zsq_b[:, 0:512], qkraw_b[:, 0:512], AF.Square)
                        A.tag(s.activation(zsq_b[:, 512:1024], qkraw_b[:, 512:1024], AF.Square))
                    on("scalar", a_sq)
                    a_sq_c = A.v

                    def v_keep(v, j=j):
                        v.wait_ge(A.sem, a_sq_c)
                        if is_main:
                            v.tensor_copy(qf_b[:, 512 * j:512 * (j + 1)], qkraw_b[:, 0:512])
                        V.tag(v.tensor_copy(kf_b[:, 512 * j:512 * (j + 1)], qkraw_b[:, 512:1024]))
                    on("vector", v_keep)

                    def t_norm(t, j=j):
                        t.wait_ge(A.sem, a_sq_c)
                        if is_main:
                            t.matmul(PD[0:16, 0:512], normE_b[:, 16 * j:16 * (j + 1)],
                                     zsq_b[:, 0:512], start=(j == 0), stop=(j == NPAIR - 1))
                        P.tag(t.matmul(PS[0:16, 0:512], normE_b[:, 16 * j:16 * (j + 1)],
                                 zsq_b[:, 512:1024], start=(j == 0), stop=(j == NPAIR - 1)))
                    on("tensor", t_norm)
                    prev_norm[0] = P.v
                p_norm_done = P.v
                v_keep_done = V.v

                prev_vev = [0]
                for i in range(4):
                    def t_v(t, i=i):
                        if prev_vev[0]:
                            t.wait_ge(V.sem, prev_vev[0])
                        for ob in range(2):
                            for dc in range(8):
                                last = t.matmul(PB[:, 512 * ob:512 * (ob + 1)],
                                         xfm_b[:, 512 * dc + 128 * i:512 * dc + 128 * (i + 1)],
                                         wqkvT_sb[:, 3072 * dc + 2048 + 512 * ob:3072 * dc + 2048 + 512 * (ob + 1)],
                                         start=(dc == 0), stop=(dc == 7))
                        P.tag(last)
                    on("tensor", t_v)
                    p_v = P.v

                    def v_ev(v, i=i):
                        v.wait_ge(P.sem, p_v)
                        v.memset(v1_b[:, 1040 * i:1040 * (i + 1)], 1.0)
                        V.tag(v.tensor_tensor(
                            bass.AP(v1_b, 1040 * i, [[4160, 128], [65, 16], [1, 64]]),
                            bass.AP(PB, 0, [[1024, 128], [64, 16], [1, 64]]),
                            bass.AP(bvbc_f, 0, [[1024, 128], [64, 16], [1, 64]]),
                            AluOpType.add))
                    on("vector", v_ev)
                    prev_vev[0] = V.v
                v_v1 = V.v

                def v_inv(v):
                    v.wait_ge(P.sem, p_norm_done)
                    if is_main:
                        v.tensor_scalar(proj_f[0:16, 0:512], PD[0:16, 0:512], 1e-24, None,
                                        AluOpType.max)
                    V.tag(v.tensor_scalar(poly_f[0:16, 0:512], PS[0:16, 0:512], 1e-24, None,
                                          AluOpType.max))
                on("vector", v_inv)
                v_invp = V.v

                def a_inv(s):
                    s.wait_ge(V.sem, v_invp)
                    if is_main:
                        s.activation(proj_f[0:16, 0:512], proj_f[0:16, 0:512], AF.Ln)
                        s.activation(invq_b[0:16, 0:512], proj_f[0:16, 0:512], AF.Exp, scale=-0.5)
                    s.activation(poly_f[0:16, 0:512], poly_f[0:16, 0:512], AF.Ln)
                    A.tag(s.activation(invk_b[0:16, 0:512], poly_f[0:16, 0:512], AF.Exp, scale=-0.5))
                on("scalar", a_inv)
                a_invc = A.v

                chain_war_v = [0]
                chain_war_a = [0]

                def emit_chain(which, j):
                    src_b = qf_b if which == "q" else kf_b
                    inv_b = invq_b if which == "q" else invk_b

                    def t_c(t, j=j, src_b=src_b, inv_b=inv_b):
                        t.wait_ge(A.sem, a_invc)
                        t.wait_ge(V.sem, v_keep_done)
                        if chain_war_v[0]:
                            t.wait_ge(V.sem, chain_war_v[0])
                        if chain_war_a[0]:
                            t.wait_ge(A.sem, chain_war_a[0])
                        t.matmul(PC[:, 0:512], omgbd_b[:, 128 * j:128 * (j + 1)],
                                 src_b[:, 512 * j:512 * (j + 1)], start=True, stop=True)
                        P.tag(t.matmul(PD[:, 0:512], gbd_b[:, 128 * j:128 * (j + 1)],
                                 inv_b[0:16, 0:512], start=True, stop=True))
                    on("tensor", t_c)
                    p_c = P.v

                    def a_c1(s):
                        s.wait_ge(P.sem, p_c)
                        A.tag(s.activation(proj_f[:, :], PC[:, 0:512], AF.Copy))
                    on("scalar", a_c1)
                    a_c1c = A.v

                    def v_c1(v):
                        v.wait_ge(A.sem, a_c1c)
                        V.tag(v.tensor_tensor(proj_f[:, :], proj_f[:, :], PD[:, 0:512],
                                              AluOpType.mult))
                    on("vector", v_c1)
                    v_c1c = V.v

                    def a_c2(s):
                        s.wait_ge(V.sem, v_c1c)
                        A.tag(s.activation(poly_f[:, :], proj_f[:, :], AF.Square, scale=S4))
                    on("scalar", a_c2)
                    a_c2c = A.v

                    def v_c2(v):
                        v.wait_ge(A.sem, a_c2c)
                        v.tensor_scalar(proj_f[:, :], proj_f[:, :], qsc_f[:, 0:1], qsc_f[:, 1:2],
                                        AluOpType.mult, AluOpType.add)
                        V.tag(v.tensor_scalar(proj_f[:, :], proj_f[:, :], -20.0, 20.0,
                                              AluOpType.max, AluOpType.min))
                    on("vector", v_c2)
                    v_c2c = V.v

                    def a_c3(s):
                        s.wait_ge(V.sem, v_c2c)
                        A.tag(s.activation(prf_f[:, :], proj_f[:, :], AF.Exp,
                                           bias=qsc_f[:, 2:3]))
                    on("scalar", a_c3)
                    chain_war_a[0] = A.v
                    a_c3c = A.v

                    def v_c3(v, j=j, src_b=src_b):
                        v.wait_ge(A.sem, a_c3c)
                        if is_main:
                            V.tag(v.tensor_tensor(src_b[:, 512 * j:512 * (j + 1)], poly_f[:, :],
                                                  prf_f[:, :], AluOpType.mult))
                        else:
                            v.wait_ge(G.sem, mil["pz"])
                            v.tensor_tensor(src_b[:, 512 * j:512 * (j + 1)], poly_f[:, :],
                                            prf_f[:, :], AluOpType.mult)
                            V.tag(v.tensor_scalar(src_b[:, 512 * j:512 * (j + 1)],
                                                  src_b[:, 512 * j:512 * (j + 1)],
                                                  pz_f[:, 0:1], None, AluOpType.mult))
                    on("vector", v_c3)
                    chain_war_v[0] = V.v

                for j in range(NPAIR):
                    if is_main:
                        emit_chain("q", j)
                    emit_chain("k", j)
                v_chains = V.v

                tp_batches = []
                jobs_tp = [(j, i) for j in range(NPAIR) for i in range(4)]
                for b0 in range(0, 32, 8):
                    batch = jobs_tp[b0:b0 + 8]

                    def t_tp(t, batch=batch, b0=b0):
                        t.wait_ge(V.sem, v_chains)
                        if tp_batches:
                            t.wait_ge(V.sem, tp_batches[-1])
                        for s_i, (j, i) in enumerate(batch):
                            last = t.transpose(Pbf[:, 128 * s_i:128 * (s_i + 1)],
                                        kf_b[:, 512 * j + 128 * i:512 * j + 128 * (i + 1)],
                                        ident_b[:, :])
                        P.tag(last)
                    on("tensor", t_tp)
                    p_tp = P.v

                    def v_tp(v, batch=batch):
                        v.wait_ge(P.sem, p_tp)
                        for s_i, (j, i) in enumerate(batch):
                            last = v.tensor_copy(kftm_b[:, 1024 * i + 128 * j:1024 * i + 128 * (j + 1)],
                                          Pbf[:, 128 * s_i:128 * (s_i + 1)])
                        V.tag(last)
                    on("vector", v_tp)
                    tp_batches.append(V.v)
                v_kftm = V.v

                if is_main:
                    def a_reg(j2):
                        return [(PA, 0), (PA, 512), (PB, 0), (PA, 896)][j2]
                    asb_off = [0, 512, 896, 1152]

                    for h in range(NH):
                        j = h // 2
                        par = h % 2
                        bp = 64 * par

                        def t_A(t, j=j, bp=bp):
                            t.wait_ge(V.sem, v_chains)
                            if war_A[0]:
                                t.wait_ge(V.sem, war_A[0])
                            for j2 in range(4):
                                n2 = 512 - 128 * j2
                                pt, c0 = a_reg(j2)
                                last = t.matmul(pt[:, c0:c0 + n2],
                                         kf_b[bp:bp + 64, 512 * j + 128 * j2:512 * j + 128 * (j2 + 1)],
                                         qf_b[bp:bp + 64, 512 * j + 128 * j2:512 * j + 512],
                                         start=True, stop=True)
                            P.tag(last)
                        on("tensor", t_A)
                        p_A = P.v

                        def v_evA(v):
                            v.wait_ge(P.sem, p_A)
                            for j2 in range(4):
                                n2 = 512 - 128 * j2
                                pt, c0 = a_reg(j2)
                                d0 = asb_off[j2]
                                last = v.tensor_tensor(asb_b[:, d0:d0 + 128], pt[:, c0:c0 + 128],
                                                maskT_b[:, :], AluOpType.mult)
                                if n2 > 128:
                                    last = v.tensor_copy(asb_b[:, d0 + 128:d0 + n2],
                                                  pt[:, c0 + 128:c0 + n2])
                            V.tag(last)
                        on("vector", v_evA)
                        war_A[0] = V.v
                        v_evAc = V.v

                        def t_ctx(t, h=h, j=j, par=par):
                            t.wait_ge(V.sem, v_evAc)
                            t.wait_ge(V.sem, v_v1)
                            t.wait_ge(V.sem, war_state[0])
                            if war_ctx[0]:
                                t.wait_ge(V.sem, war_ctx[0])
                            st_b = sa_b if par == 0 else sbp_b
                            t.matmul(PB[0:65, 512:1024], st_b[:, 65 * j:65 * (j + 1)],
                                     qf_b[:, 512 * j:512 * (j + 1)], start=True, stop=False)
                            for j2 in range(4):
                                n2 = 512 - 128 * j2
                                last = t.matmul(PB[0:65, 512 + 128 * j2:1024],
                                         v1_b[:, 1040 * j2 + 65 * h:1040 * j2 + 65 * (h + 1)],
                                         asb_b[:, asb_off[j2]:asb_off[j2] + n2],
                                         start=False, stop=(j2 == 3))
                            P.tag(last)
                        on("tensor", t_ctx)
                        p_ctx = P.v

                        def v_ctx(v):
                            v.wait_ge(P.sem, p_ctx)
                            v.tensor_copy(ctx_f[0:65, :], PB[0:65, 512:1024])
                            v.tensor_scalar(ctx_f[64:65, :], ctx_f[64:65, :], 1e-6, None,
                                            AluOpType.max)
                            with nc.allow_low_precision(reason="bf16 matmul operand"):
                                V.tag(v.reciprocal(denr_b[64:65, :], ctx_f[64:65, :]))
                        on("vector", v_ctx)
                        war_ctx[0] = V.v
                        v_ctxc = V.v

                        def t_bden(t):
                            t.wait_ge(V.sem, v_ctxc)
                            if war_bden[0]:
                                t.wait_ge(V.sem, war_bden[0])
                            P.tag(t.matmul(PD[0:64, 0:512], ones_b[64:65, 0:64],
                                           denr_b[64:65, :], start=True, stop=True))
                        on("tensor", t_bden)
                        p_bden = P.v

                        def v_div(v, j=j, par=par):
                            v.wait_ge(P.sem, p_bden)
                            if par == 0:
                                V.tag(v.tensor_tensor(attn_b[0:64, 512 * j:512 * (j + 1)],
                                                      ctx_f[0:64, :], PD[0:64, 0:512],
                                                      AluOpType.mult))
                            else:
                                V.tag(v.tensor_tensor(attnh_b[0:64, :], ctx_f[0:64, :],
                                                      PD[0:64, 0:512], AluOpType.mult))
                        on("vector", v_div)
                        war_bden[0] = V.v
                        v_divc = V.v

                        if par == 1:
                            def t_mv(t):
                                t.wait_ge(V.sem, v_divc)
                                if war_mv[0]:
                                    t.wait_ge(V.sem, war_mv[0])
                                P.tag(t.matmul(PS[64:128, 0:512], ident_b[0:64, 0:64],
                                               attnh_b[0:64, :], start=True, stop=True,
                                               tile_position=(0, 64)))
                            on("tensor", t_mv)
                            p_mv = P.v

                            def v_mv(v, j=j):
                                v.wait_ge(P.sem, p_mv)
                                V.tag(v.tensor_copy(attn_b[64:128, 512 * j:512 * (j + 1)],
                                                    PS[64:128, 0:512]))
                            on("vector", v_mv)
                            war_mv[0] = V.v

                def t_delta(t):
                    t.wait_ge(V.sem, v_kftm)
                    t.wait_ge(V.sem, v_v1)
                    if is_main and war_A[0]:
                        t.wait_ge(V.sem, war_A[0])
                    if war_state[0]:
                        t.wait_ge(V.sem, war_state[0])
                    for j in range(NPAIR):
                        for par in range(2):
                            h = 2 * j + par
                            for i in range(4):
                                if par == 0:
                                    t.matmul(PA[0:64, 128 * j:128 * j + 65],
                                             kftm_b[:, 1024 * i + 128 * j:1024 * i + 128 * j + 64],
                                             v1_b[:, 1040 * i + 65 * h:1040 * i + 65 * (h + 1)],
                                             start=(i == 0), stop=(i == 3))
                                else:
                                    last = t.matmul(PA[64:128, 128 * j:128 * j + 65],
                                             kftm_b[:, 1024 * i + 128 * j + 64:1024 * i + 128 * (j + 1)],
                                             v1_b[:, 1040 * i + 65 * h:1040 * i + 65 * (h + 1)],
                                             start=(i == 0), stop=(i == 3),
                                             tile_position=(0, 64))
                    P.tag(last)
                on("tensor", t_delta)
                p_delta = P.v

                def v_state(v):
                    v.wait_ge(P.sem, p_delta)
                    v.tensor_tensor(
                        bass.AP(state_f, 0, [[1024, 128], [128, 8], [1, 65]]),
                        bass.AP(state_f, 0, [[1024, 128], [128, 8], [1, 65]]),
                        bass.AP(PA, 0, [[1024, 128], [128, 8], [1, 65]]),
                        AluOpType.add)
                    v.tensor_scalar(
                        bass.AP(sa_b, 0, [[520, 64], [65, 8], [1, 65]]),
                        bass.AP(state_f, 0, [[1024, 64], [128, 8], [1, 65]]),
                        1.0, None, AluOpType.mult)
                    V.tag(v.tensor_scalar(
                        bass.AP(sbp_b, 64 * 520, [[520, 64], [65, 8], [1, 65]]),
                        bass.AP(state_f, 64 * 1024, [[1024, 64], [128, 8], [1, 65]]),
                        1.0, None, AluOpType.mult))
                on("vector", v_state)
                war_state[0] = V.v

                if is_main:
                    v_attn_all = V.v
                    sl0 = 4 * (ci % 2)
                    for i in range(4):
                        def g_xtm(g, i=i):
                            if war_xtm[i % 2]:
                                g.wait_ge(V.sem, war_xtm[i % 2])
                            G.tag(g.dma_start(xtm_f[:, 1024 * (i % 2):1024 * (i % 2) + 1024],
                                              xm_h.ap()[ci * C + 128 * i:ci * C + 128 * (i + 1), :]))
                        on("gpsimd", g_xtm)
                        g_xtm_c = G.v

                        def t_proj(t, i=i):
                            t.wait_ge(V.sem, v_attn_all)
                            t.wait_ge(X.sem, x_wo)
                            if war_proj[0]:
                                t.wait_ge(V.sem, war_proj[0])
                            for ob in range(2):
                                dst = PC if ob == 0 else PS
                                for cc in range(8):
                                    last = t.matmul(dst[:, 0:512],
                                             attn_b[:, 512 * cc + 128 * i:512 * cc + 128 * (i + 1)],
                                             woutT_sb[:, 1024 * cc + 512 * ob:1024 * cc + 512 * (ob + 1)],
                                             start=(cc == 0), stop=(cc == 7))
                            P.tag(last)
                        on("tensor", t_proj)
                        p_proj = P.v

                        def v_ln(v, i=i):
                            v.wait_ge(P.sem, p_proj)
                            v.wait_ge(G.sem, g_xtm_c)
                            v.tensor_tensor(r_f[:, 0:512], PC[:, 0:512], boutbc_f[:, 0:512],
                                            AluOpType.add)
                            v.tensor_tensor(r_f[:, 512:1024], PS[:, 0:512], boutbc_f[:, 512:1024],
                                            AluOpType.add)
                            v.tensor_tensor(r_f[:, :], r_f[:, :],
                                            xtm_f[:, 1024 * (i % 2):1024 * (i % 2) + 1024],
                                            AluOpType.add)
                            v.bn_stats(bna_f[:, 0:6], r_f[:, 0:512])
                            V.tag(v.bn_stats(bna_f[:, 6:12], r_f[:, 512:1024]))
                            v.wait_ge(V.sem, V.v)
                            v.bn_aggr(bna_f[:, 12:14], bna_f[:, 0:6])
                            V.tag(v.bn_aggr(bna_f[:, 14:16], bna_f[:, 6:12]))
                            v.wait_ge(V.sem, V.v)
                            v.tensor_tensor(bna_f[:, 16:17], bna_f[:, 12:13],
                                            bna_f[:, 14:15], AluOpType.add)
                            v.tensor_tensor(bna_f[:, 17:18], bna_f[:, 12:13],
                                            bna_f[:, 14:15], AluOpType.subtract)
                            V.tag(v.tensor_tensor(bna_f[:, 18:19], bna_f[:, 13:14],
                                            bna_f[:, 15:16], AluOpType.add))
                            v.wait_ge(V.sem, V.v)
                            V.tag(v.tensor_tensor(bna_f[:, 19:20], bna_f[:, 17:18],
                                            bna_f[:, 17:18], AluOpType.mult))
                            v.wait_ge(V.sem, V.v)
                            V.tag(v.tensor_scalar(bna_f[:, 19:20], bna_f[:, 19:20],
                                            0.5, None, AluOpType.mult))
                            v.wait_ge(V.sem, V.v)
                            V.tag(v.tensor_tensor(bna_f[:, 18:19], bna_f[:, 18:19],
                                                  bna_f[:, 19:20], AluOpType.add))
                        on("vector", v_ln)
                        v_lnc = V.v
                        war_proj[0] = V.v

                        def a_rstd(s):
                            s.wait_ge(V.sem, v_lnc)
                            s.activation(rstd_f[:, 0:1], bna_f[:, 18:19], AF.Ln,
                                         bias=qsc_f[:, 4:5])
                            A.tag(s.activation(rstd_f[:, 0:1], rstd_f[:, 0:1], AF.Exp,
                                               scale=-0.5, bias=qsc_f[:, 5:6]))
                        on("scalar", a_rstd)
                        a_rstdc = A.v

                        def v_h1(v, i=i, sl0=sl0):
                            v.wait_ge(A.sem, a_rstdc)
                            V.tag(v.tensor_scalar(rstd_f[:, 1:2], bna_f[:, 16:17], rstd_f[:, 0:1],
                                            -0.5, AluOpType.mult, AluOpType.mult))
                            v.wait_ge(V.sem, V.v)
                            V.tag(v.tensor_scalar(
                                h1f[:, 1024 * (sl0 + i):1024 * (sl0 + i + 1)],
                                r_f[:, :], rstd_f[:, 0:1], rstd_f[:, 1:2],
                                AluOpType.mult, AluOpType.add))
                        on("vector", v_h1)
                        war_xtm[i % 2] = V.v

                barrier()

            if seg == 0:
                for pc in range(n_prefix_chunks):
                    emit_chunk("prefix", pc)
                emit_sharedT()
            for k in range(2):
                mc = 2 * seg + k
                if mc < n_main_chunks:
                    emit_chunk("main", mc)

            barrier()

        with contextlib.ExitStack() as fsx:
            fsb = lambda name, shape, dt: fsx.enter_context(
                nc.sbuf_tensor(f"{name}_f{seg}", shape, dt))
            h1b_b = fsb("h1b", [128, 2048], BF16)
            h1fm_b = fsb("h1fm", [128, 8192], BF16)
            acc_f = fsb("acc", [128, 8192], F32)
            wgu_sb = fsb("wgu", [128, 16384], BF16)
            wdn_sb = fsb("wdn", [128, 8192], BF16)
            swgt_sb = fsb("swgt", [128, 2048], BF16)
            swut_sb = fsb("swut", [128, 2048], BF16)
            swdt_sb = fsb("swdt", [128, 2048], BF16)
            silu_f = fsb("silu", [128, 512], F32)
            he_f = fsb("he", [128, 512], F32)
            hew_b = fsb("hew", [128, 1024], BF16)
            shsil_f = fsb("shsil", [128, 512], F32)
            shhe_f = fsb("shhe", [128, 256], F32)
            shhew_b = fsb("shhew", [128, 256], BF16)
            logtm_f = fsb("logtm", [128, 16], F32)
            router_f = fsb("router", [128, 64], F32)
            comb_a = fsb("comba", [128, 128], F32)
            combfm_b = fsb("combfm", [16, 1024], BF16)
            srow_b = fsb("srow", [1, 1024], BF16)
            tmp_f = fsb("tmp", [128, 1024], F32)
            bs_f = fsb("bs", [128, 1024], F32)
            r2_f = fsb("r2", [128, 1024], F32)
            outst_f = fsb("outst", [128, 1024], F32)
            bna2_f = fsb("bna2", [128, 20], F32)
            rstd2_f = fsb("rstd2", [128, 2], F32)

            with nc.psum_tensor(f"PbfT_f{seg}", [128, 1024], BF16) as PbfT, \
                 nc.psum_tensor(f"PFT_f{seg}", [128, 1024], F32) as PFT:
                def v_acc0(v):
                    V.tag(v.memset(acc_f[:, :], 0.0))
                on("vector", v_acc0)

                tpb = []
                h1bc = [0, 0]
                for ts in range(8):
                    hslot = ts % 2

                    def v_h1b(v, ts=ts, hslot=hslot):
                        if len(tpb) >= 2:
                            v.wait_ge(P.sem, tp_pc[ts - 2])
                        V.tag(v.tensor_copy(h1b_b[:, 1024 * hslot:1024 * (hslot + 1)],
                                            h1f[:, 1024 * ts:1024 * (ts + 1)]))
                    on("vector", v_h1b)
                    h1bc[hslot] = V.v
                    if ts == 0:
                        tp_pc = {}

                    def t_tph(t, ts=ts, hslot=hslot):
                        t.wait_ge(V.sem, h1bc[hslot])
                        if tpb:
                            t.wait_ge(V.sem, tpb[-1])
                        for dc in range(8):
                            last = t.transpose(PbfT[:, 128 * dc:128 * (dc + 1)],
                                        h1b_b[:, 1024 * hslot + 128 * dc:1024 * hslot + 128 * (dc + 1)],
                                        ident_b[:, :])
                        P.tag(last)
                    on("tensor", t_tph)
                    tp_pc[ts] = P.v
                    p_tph = P.v

                    def v_tph(v, ts=ts):
                        v.wait_ge(P.sem, p_tph)
                        for dc in range(8):
                            last = v.tensor_copy(h1fm_b[:, 1024 * dc + 128 * ts:1024 * dc + 128 * (ts + 1)],
                                          PbfT[:, 128 * dc:128 * (dc + 1)])
                        V.tag(last)
                    on("vector", v_tph)
                    tpb.append(V.v)
                v_h1fm = V.v

                def t_log(t):
                    t.wait_ge(V.sem, v_h1fm)
                    for half in range(2):
                        for dc in range(8):
                            last = t.matmul(PFT[0:16, 512 * half:512 * (half + 1)],
                                     gwT_b[:, 16 * dc:16 * (dc + 1)],
                                     h1fm_b[:, 1024 * dc + 512 * half:1024 * dc + 512 * (half + 1)],
                                     start=(dc == 0), stop=(dc == 7))
                    P.tag(last)
                on("tensor", t_log)
                p_log = P.v

                def v_log(v):
                    v.wait_ge(P.sem, p_log)
                    V.tag(v.tensor_copy(combfm_b[0:16, 0:1024], PFT[0:16, 0:1024]))
                on("vector", v_log)
                v_logc = V.v

                def t_sgw(t):
                    t.wait_ge(V.sem, v_logc)
                    for half in range(2):
                        for dc in range(8):
                            last = t.matmul(PFT[0:1, 512 * half:512 * (half + 1)],
                                     sgwT_b[:, dc:dc + 1],
                                     h1fm_b[:, 1024 * dc + 512 * half:1024 * dc + 512 * (half + 1)],
                                     start=(dc == 0), stop=(dc == 7))
                    P.tag(last)
                on("tensor", t_sgw)
                p_sgw = P.v

                def a_sgm(s):
                    s.wait_ge(P.sem, p_sgw)
                    A.tag(s.activation(srow_b[0:1, :], PFT[0:1, 0:1024], AF.Sigmoid))
                on("scalar", a_sgm)
                a_sgmc = A.v

                war_rt = [0]
                for ts in range(8):
                    def t_ltp(t, ts=ts):
                        t.wait_ge(V.sem, v_logc)
                        t.wait_ge(A.sem, a_sgmc)
                        if war_rt[0]:
                            t.wait_ge(V.sem, war_rt[0])
                        P.tag(t.transpose(PbfT[:, 128:144],
                                          combfm_b[0:16, 128 * ts:128 * (ts + 1)],
                                          ident_b[0:16, 0:16]))
                    on("tensor", t_ltp)
                    p_ltp = P.v

                    def v_rt(v):
                        v.wait_ge(P.sem, p_ltp)
                        V.tag(v.tensor_copy(logtm_f[:, 0:16], PbfT[:, 128:144]))
                        v.wait_ge(V.sem, V.v)
                        V.tag(v.reduce_max(router_f[:, 0:1], logtm_f[:, 0:16], axis=AX))
                        v.wait_ge(V.sem, V.v)
                        V.tag(v.tensor_scalar(router_f[:, 16:32], logtm_f[:, 0:16], router_f[:, 0:1],
                                        None, AluOpType.is_equal))
                        v.wait_ge(V.sem, V.v)
                        V.tag(v.tensor_scalar(router_f[:, 32:48], router_f[:, 16:32], -1e30, None,
                                        AluOpType.mult))
                        v.wait_ge(V.sem, V.v)
                        V.tag(v.tensor_tensor(router_f[:, 32:48], router_f[:, 32:48], logtm_f[:, 0:16],
                                        AluOpType.add))
                        v.wait_ge(V.sem, V.v)
                        V.tag(v.reduce_max(router_f[:, 1:2], router_f[:, 32:48], axis=AX))
                        v.wait_ge(V.sem, V.v)
                        V.tag(v.tensor_scalar(router_f[:, 48:64], router_f[:, 32:48], router_f[:, 1:2],
                                              None, AluOpType.is_equal))
                        V.tag(v.tensor_scalar(router_f[:, 2:3], router_f[:, 0:1], router_f[:, 1:2],
                                              None, AluOpType.subtract))
                    on("vector", v_rt)
                    v_rtc = V.v

                    def a_rt(s):
                        s.wait_ge(V.sem, v_rtc)
                        A.tag(s.activation(router_f[:, 3:4], router_f[:, 2:3], AF.Sigmoid))
                    on("scalar", a_rt)
                    a_rtc = A.v

                    def v_cb(v):
                        v.wait_ge(A.sem, a_rtc)
                        V.tag(v.tensor_scalar(router_f[:, 4:5], router_f[:, 3:4], -1.0, 1.0,
                                        AluOpType.mult, AluOpType.add))
                        v.wait_ge(V.sem, V.v)
                        V.tag(v.tensor_scalar(router_f[:, 16:32], router_f[:, 16:32], router_f[:, 3:4],
                                        None, AluOpType.mult))
                        V.tag(v.tensor_scalar(router_f[:, 48:64], router_f[:, 48:64], router_f[:, 4:5],
                                        None, AluOpType.mult))
                        v.wait_ge(V.sem, V.v)
                        V.tag(v.tensor_tensor(comb_a[:, 16 * ts:16 * (ts + 1)],
                                              router_f[:, 16:32],
                                              router_f[:, 48:64], AluOpType.add))
                    on("vector", v_cb)
                    war_rt[0] = V.v

                prev_bs = [0]
                for wt in range(NWT):
                    def t_bs(t, wt=wt):
                        t.wait_ge(A.sem, a_sgmc)
                        if prev_bs[0]:
                            t.wait_ge(V.sem, prev_bs[0])
                        P.tag(t.matmul(PFT[:, 0:256], ones_b[0:1, 0:128],
                                       srow_b[0:1, 256 * wt:256 * (wt + 1)],
                                       start=True, stop=True))
                    on("tensor", t_bs)
                    p_bs = P.v

                    def v_bsc(v, wt=wt):
                        v.wait_ge(P.sem, p_bs)
                        V.tag(v.tensor_copy(bs_f[:, 256 * wt:256 * (wt + 1)], PFT[:, 0:256]))
                    on("vector", v_bsc)
                    prev_bs[0] = V.v
                v_scopeT = V.v

            with nc.psum_tensor(f"QGU_f{seg}", [128, 2048], F32) as QGU, \
                 nc.psum_tensor(f"QDN_f{seg}", [128, 2048], F32) as QDN:

                gcol = lambda g_: 512 * (g_ % 2) + 256 * (g_ // 2)
                moe_load = {}
                moe_guard = {}
                warQ_a = [0, 0]
                warQ_v = [v_scopeT, v_scopeT]
                war_hew_p = [0, 0, 0, 0]
                war_qdn = [0]

                def g_moe(g, e):
                    slot = e % 2
                    if (e - 2) in moe_guard:
                        g.wait_ge(P.sem, moe_guard[e - 2])
                    g.wait_ge(G.sem, mil["moe_bf"])
                    G.tag(g.dma_start(
                        bass.AP(wgu_sb, 8192 * slot, [[16384, 128], [1024, 8], [1, 1024]]),
                        bass.AP(wgu_bf_h, e * 1024 * 1024, [[1024, 128], [131072, 8], [1, 1024]])))
                    G.tag(g.dma_start(
                        bass.AP(wdn_sb, 4096 * slot, [[8192, 128], [1024, 4], [1, 1024]]),
                        bass.AP(wdn_bf_h, e * 512 * 1024, [[1024, 128], [131072, 4], [1, 1024]])))
                    moe_load[e] = G.v

                def g_moe01(g):
                    g_moe(g, 0)
                    g_moe(g, 1)
                on("gpsimd", g_moe01)

                for e in range(NE):
                    slot = e % 2

                    for wt in range(NWT):
                        tc0 = W * wt
                        for gph in range(4):
                            def t_gu(t, e=e, gph=gph, slot=slot, tc0=tc0):
                                t.wait_ge(G.sem, moe_load[e])
                                if warQ_a[gph % 2]:
                                    t.wait_ge(A.sem, warQ_a[gph % 2])
                                if warQ_v[gph % 2]:
                                    t.wait_ge(V.sem, warQ_v[gph % 2])
                                gc = gcol(gph)
                                for dc in range(8):
                                    t.matmul(QGU[:, gc:gc + 256],
                                             wgu_sb[:, 8192 * slot + 1024 * dc + 128 * gph:
                                                    8192 * slot + 1024 * dc + 128 * (gph + 1)],
                                             h1fm_b[:, 1024 * dc + tc0:1024 * dc + tc0 + 256],
                                             start=(dc == 0), stop=(dc == 7))
                                for dc in range(8):
                                    last = t.matmul(QGU[:, 1024 + gc:1024 + gc + 256],
                                             wgu_sb[:, 8192 * slot + 1024 * dc + 128 * (gph + 4):
                                                    8192 * slot + 1024 * dc + 128 * (gph + 5)],
                                             h1fm_b[:, 1024 * dc + tc0:1024 * dc + tc0 + 256],
                                             start=(dc == 0), stop=(dc == 7))
                                P.tag(last)
                            on("tensor", t_gu)
                            p_gu = P.v

                            def a_sil(s, gph=gph):
                                s.wait_ge(P.sem, p_gu)
                                gc = gcol(gph)
                                A.tag(s.activation(silu_f[:, 256 * (gph % 2):256 * (gph % 2) + 256],
                                                   QGU[:, gc:gc + 256], AF.Silu))
                            on("scalar", a_sil)
                            warQ_a[gph % 2] = A.v
                            a_silc = A.v

                            def v_he(v, gph=gph, wt=wt):
                                v.wait_ge(A.sem, a_silc)
                                if war_hew_p[gph]:
                                    v.wait_ge(P.sem, war_hew_p[gph])
                                gc = gcol(gph)
                                V.tag(v.tensor_tensor(hew_b[:, 256 * gph:256 * (gph + 1)],
                                                      silu_f[:, 256 * (gph % 2):256 * (gph % 2) + 256],
                                                      QGU[:, 1024 + gc:1024 + gc + 256],
                                                      AluOpType.mult))
                            on("vector", v_he)
                            warQ_v[gph % 2] = V.v
                            hew_done = V.v
                            war_hew_local = hew_done
                            if gph == 0:
                                hewv = [0, 0, 0, 0]
                            hewv[gph] = V.v

                        def t_dn(t, e=e, slot=slot, hewv=hewv):
                            if war_qdn[0]:
                                t.wait_ge(V.sem, war_qdn[0])
                            for fc in range(4):
                                t.wait_ge(V.sem, hewv[fc])
                                for th in range(2):
                                    for dh in range(2):
                                        last = t.matmul(QDN[:, 1024 * th + 512 * dh:1024 * th + 512 * (dh + 1)],
                                                 hew_b[:, 256 * fc + 128 * th:256 * fc + 128 * (th + 1)],
                                                 wdn_sb[:, 4096 * slot + 1024 * fc + 512 * dh:
                                                        4096 * slot + 1024 * fc + 512 * (dh + 1)],
                                                 start=(fc == 0), stop=(fc == 3))
                            P.tag(last)
                        on("tensor", t_dn)
                        p_dnc = P.v
                        for fc in range(4):
                            war_hew_p[fc] = P.v

                        def v_acc(v, wt=wt, e=e):
                            v.wait_ge(P.sem, p_dnc)
                            for th in range(2):
                                sl = 2 * wt + th
                                v.tensor_scalar(tmp_f[:, 0:1024],
                                                QDN[:, 1024 * th:1024 * (th + 1)],
                                                comb_a[:, 16 * sl + e:16 * sl + e + 1],
                                                None, AluOpType.mult)
                                V.tag(v.tensor_tensor(acc_f[:, 1024 * sl:1024 * (sl + 1)],
                                                      acc_f[:, 1024 * sl:1024 * (sl + 1)],
                                                      tmp_f[:, 0:1024], AluOpType.add))
                        on("vector", v_acc)
                        war_qdn[0] = V.v
                    moe_guard[e] = P.v
                    if e + 2 < NE:
                        def g_pre(g, e=e):
                            g_moe(g, e + 2)
                        on("gpsimd", g_pre)

                sh_load = {}
                sh_guard = {}

                def g_sh(g, k, fb, wt):
                    slot = k % 2
                    if (k - 2) in sh_guard:
                        g.wait_ge(P.sem, sh_guard[k - 2])
                    g.wait_ge(G.sem, shT_state["done_g"])
                    G.tag(g.dma_start(
                        bass.AP(swgt_sb, 1024 * slot, [[2048, 128], [128, 8], [1, 128]]),
                        bass.AP(swgT_h, 128 * fb, [[2816, 128], [128 * 2816, 8], [1, 128]])))
                    G.tag(g.dma_start(
                        bass.AP(swut_sb, 1024 * slot, [[2048, 128], [128, 8], [1, 128]]),
                        bass.AP(swuT_h, 128 * fb, [[2816, 128], [128 * 2816, 8], [1, 128]])))
                    G.tag(g.dma_start(
                        bass.AP(swdt_sb, 1024 * slot, [[2048, 128], [1, 1024]]),
                        bass.AP(swdT_h, 128 * fb * 1024, [[1024, 128], [1, 1024]])))
                    sh_load[k] = G.v

                war_sh_p = [0]
                kglob = [0]

                for wt in range(NWT):
                    tc0 = W * wt
                    if wt == 0:
                        def g_sh01(g):
                            g_sh(g, 0, 0, 0)
                            g_sh(g, 1, 1, 0)
                        on("gpsimd", g_sh01)

                    for fb in range(NFB):
                        k = wt * NFB + fb
                        slot = k % 2
                        shc = 512 * (k % 2)

                        def t_shgu(t, k=k, slot=slot, tc0=tc0, shc=shc):
                            t.wait_ge(G.sem, sh_load[k])
                            if warQ_a[k % 2]:
                                t.wait_ge(A.sem, warQ_a[k % 2])
                            if warQ_v[k % 2]:
                                t.wait_ge(V.sem, warQ_v[k % 2])
                            for dc in range(8):
                                t.matmul(QGU[:, shc:shc + 256],
                                         swgt_sb[:, 1024 * slot + 128 * dc:1024 * slot + 128 * (dc + 1)],
                                         h1fm_b[:, 1024 * dc + tc0:1024 * dc + tc0 + 256],
                                         start=(dc == 0), stop=(dc == 7))
                            for dc in range(8):
                                last = t.matmul(QGU[:, 1024 + shc:1024 + shc + 256],
                                         swut_sb[:, 1024 * slot + 128 * dc:1024 * slot + 128 * (dc + 1)],
                                         h1fm_b[:, 1024 * dc + tc0:1024 * dc + tc0 + 256],
                                         start=(dc == 0), stop=(dc == 7))
                            P.tag(last)
                        on("tensor", t_shgu)
                        p_shgu = P.v

                        def a_shs(s, k=k, shc=shc):
                            s.wait_ge(P.sem, p_shgu)
                            A.tag(s.activation(shsil_f[:, 256 * (k % 2):256 * (k % 2) + 256],
                                               QGU[:, shc:shc + 256], AF.Silu))
                        on("scalar", a_shs)
                        warQ_a[k % 2] = A.v
                        a_shsc = A.v

                        def v_shhe(v, k=k, shc=shc, wt=wt):
                            v.wait_ge(A.sem, a_shsc)
                            if war_sh_p[0]:
                                v.wait_ge(P.sem, war_sh_p[0])
                            v.tensor_tensor(shhe_f[:, 0:256],
                                            shsil_f[:, 256 * (k % 2):256 * (k % 2) + 256],
                                            QGU[:, 1024 + shc:1024 + shc + 256],
                                            AluOpType.mult)
                            v.tensor_tensor(shhe_f[:, 0:256], shhe_f[:, 0:256],
                                            bs_f[:, 256 * wt:256 * (wt + 1)],
                                            AluOpType.mult)
                            V.tag(v.tensor_copy(shhew_b[:, 0:256], shhe_f[:, 0:256]))
                        on("vector", v_shhe)
                        warQ_v[k % 2] = V.v
                        v_shhec = V.v

                        def t_shdn(t, fb=fb, slot=slot):
                            t.wait_ge(V.sem, v_shhec)
                            if fb == 0 and war_qdn[0]:
                                t.wait_ge(V.sem, war_qdn[0])
                            for th in range(2):
                                for dh in range(2):
                                    last = t.matmul(QDN[:, 1024 * th + 512 * dh:1024 * th + 512 * (dh + 1)],
                                             shhew_b[:, 128 * th:128 * (th + 1)],
                                             swdt_sb[:, 1024 * slot + 512 * dh:1024 * slot + 512 * (dh + 1)],
                                             start=(fb == 0), stop=(fb == NFB - 1))
                            P.tag(last)
                        on("tensor", t_shdn)
                        sh_guard[k] = P.v
                        war_sh_p[0] = P.v
                        nk = k + 2
                        if nk < NWT * NFB:
                            nwt, nfb = nk // NFB, nk % NFB
                            def g_shp(g, nk=nk, nfb=nfb, nwt=nwt):
                                g_sh(g, nk, nfb, nwt)
                            on("gpsimd", g_shp)
                    p_dn_done = P.v

                    def v_accs(v, wt=wt):
                        v.wait_ge(P.sem, p_dn_done)
                        sl = 2 * wt
                        v.tensor_tensor(acc_f[:, 1024 * sl:1024 * (sl + 1)],
                                        acc_f[:, 1024 * sl:1024 * (sl + 1)],
                                        QDN[:, 0:1024], AluOpType.add)
                        V.tag(v.tensor_tensor(acc_f[:, 1024 * (sl + 1):1024 * (sl + 2)],
                                              acc_f[:, 1024 * (sl + 1):1024 * (sl + 2)],
                                              QDN[:, 1024:2048], AluOpType.add))
                    on("vector", v_accs)
                    war_qdn[0] = V.v

                    for th in range(2):
                        glob_t0 = seg * 1024 + wt * W + 128 * th
                        sl = 2 * wt + th

                        def v_ln2(v, sl=sl):
                            v.tensor_tensor(r2_f[:, :], acc_f[:, 1024 * sl:1024 * (sl + 1)],
                                            h1f[:, 1024 * sl:1024 * (sl + 1)], AluOpType.add)
                            v.bn_stats(bna2_f[:, 0:6], r2_f[:, 0:512])
                            V.tag(v.bn_stats(bna2_f[:, 6:12], r2_f[:, 512:1024]))
                            v.wait_ge(V.sem, V.v)
                            v.bn_aggr(bna2_f[:, 12:14], bna2_f[:, 0:6])
                            V.tag(v.bn_aggr(bna2_f[:, 14:16], bna2_f[:, 6:12]))
                            v.wait_ge(V.sem, V.v)
                            v.tensor_tensor(bna2_f[:, 16:17], bna2_f[:, 12:13],
                                            bna2_f[:, 14:15], AluOpType.add)
                            v.tensor_tensor(bna2_f[:, 17:18], bna2_f[:, 12:13],
                                            bna2_f[:, 14:15], AluOpType.subtract)
                            V.tag(v.tensor_tensor(bna2_f[:, 18:19], bna2_f[:, 13:14],
                                            bna2_f[:, 15:16], AluOpType.add))
                            v.wait_ge(V.sem, V.v)
                            V.tag(v.tensor_tensor(bna2_f[:, 19:20], bna2_f[:, 17:18],
                                            bna2_f[:, 17:18], AluOpType.mult))
                            v.wait_ge(V.sem, V.v)
                            V.tag(v.tensor_scalar(bna2_f[:, 19:20], bna2_f[:, 19:20],
                                            0.5, None, AluOpType.mult))
                            v.wait_ge(V.sem, V.v)
                            V.tag(v.tensor_tensor(bna2_f[:, 18:19], bna2_f[:, 18:19],
                                                  bna2_f[:, 19:20], AluOpType.add))
                        on("vector", v_ln2)
                        v_ln2c = V.v

                        def a_r2(s):
                            s.wait_ge(V.sem, v_ln2c)
                            s.activation(rstd2_f[:, 0:1], bna2_f[:, 18:19], AF.Ln,
                                         bias=qsc_f[:, 4:5])
                            A.tag(s.activation(rstd2_f[:, 0:1], rstd2_f[:, 0:1], AF.Exp,
                                               scale=-0.5, bias=qsc_f[:, 5:6]))
                        on("scalar", a_r2)
                        a_r2c = A.v

                        def v_out(v):
                            v.wait_ge(A.sem, a_r2c)
                            if out_slot_war[0]:
                                v.wait_ge(O.sem, out_slot_war[0])
                            V.tag(v.tensor_scalar(rstd2_f[:, 1:2], bna2_f[:, 16:17], rstd2_f[:, 0:1],
                                            -0.5, AluOpType.mult, AluOpType.mult))
                            v.wait_ge(V.sem, V.v)
                            V.tag(v.tensor_scalar(outst_f[:, 0:1024],
                                                  r2_f[:, :], rstd2_f[:, 0:1], rstd2_f[:, 1:2],
                                                  AluOpType.mult, AluOpType.add))
                        on("vector", v_out)
                        v_outc = V.v

                        def g_out(g, glob_t0=glob_t0):
                            g.wait_ge(V.sem, v_outc)
                            O.tag(g.dma_start(out_h.ap()[glob_t0:glob_t0 + 128, :],
                                              outst_f[:, 0:1024]))
                        on("gpsimd", g_out)
                        out_slot_war[0] = O.v

            barrier()

    def g_fin(g):
        g.wait_ge(O.sem, O.v)
    on("gpsimd", g_fin)

    es.close()
    return nc




def make_consts():
    cident = np.eye(128, dtype=np.float32)
    cmaskT = np.triu(np.ones((128, 128), np.float32))
    cgbd = np.zeros((16, 1024), np.float32)
    for j in range(8):
        for c in range(128):
            cgbd[2 * j + (c // 64), 128 * j + c] = 1.0
    cece = np.zeros((16, 2048), np.float32)
    for e in range(16):
        cece[e, 128 * e:128 * (e + 1)] = 1.0
    cnormE = np.zeros((128, 128), np.float32)
    for j in range(8):
        for p in range(128):
            cnormE[p, 16 * j + 2 * j + (p // 64)] = 1.0
    cpatT = np.zeros((2, 128), np.float32)
    for p in range(128):
        cpatT[(p // 32) % 2, p] = 1.0
    return dict(cident=cident, cmaskT=cmaskT, cgbd=cgbd, cece=cece,
                cnormE=cnormE, cpatT=cpatT)


def make_in_maps(inputs):
    x = np.ascontiguousarray(np.asarray(inputs["L0_attn_x"], dtype=np.float32))
    consts = make_consts()
    shared = dict(
        wqkv=np.ascontiguousarray(inputs["L0_attn_w_qkv"], dtype=np.float32),
        bqkv=np.ascontiguousarray(np.asarray(inputs["L0_attn_b_qkv"], np.float32).reshape(1, 3072)),
        wout=np.ascontiguousarray(inputs["L0_attn_w_out"], dtype=np.float32),
        bout=np.ascontiguousarray(np.asarray(inputs["L0_attn_b_out"], np.float32).reshape(1, 1024)),
        omg=np.ascontiguousarray(inputs["L0_attn_omega"], dtype=np.float32),
        qn=np.ascontiguousarray(np.asarray(inputs["L0_attn_quad_nodes"], np.float32).reshape(2, 1)),
        qw=np.ascontiguousarray(np.asarray(inputs["L0_attn_quad_weights"], np.float32).reshape(2, 1)),
        gw=np.ascontiguousarray(inputs["L0_ffn_gate_w"], dtype=np.float32),
        wgu=np.ascontiguousarray(inputs["L0_ffn_w_gate_up"], dtype=np.float32),
        wdn=np.ascontiguousarray(inputs["L0_ffn_w_down"], dtype=np.float32),
        swg=np.ascontiguousarray(inputs["L0_ffn_shared_w_gate"], dtype=np.float32),
        swu=np.ascontiguousarray(inputs["L0_ffn_shared_w_up"], dtype=np.float32),
        swd=np.ascontiguousarray(inputs["L0_ffn_shared_w_down"], dtype=np.float32),
        sgw=np.ascontiguousarray(inputs["L0_ffn_shared_gate_w"], dtype=np.float32),
        **consts,
    )
    zeros_pref = np.zeros((4096, 1024), np.float32)
    pz1 = np.ones((128, 1), np.float32)
    pz0 = np.zeros((128, 1), np.float32)
    in_maps = []
    for c in range(8):
        b, s = c // 2, c % 2
        xm = np.ascontiguousarray(x[b, s * 4096:(s + 1) * 4096])
        xp = np.ascontiguousarray(x[b, 0:4096]) if s == 1 else zeros_pref
        in_maps.append(dict(xm=xm, xp=xp, pz=(pz1 if s == 1 else pz0), **shared))
    return in_maps


def kernel(**inputs):
    in_maps = make_in_maps(inputs)
    nc = build_nc()
    res = run_bass_kernel_spmd(nc, in_maps, core_ids=list(range(8)))
    out = np.empty((4, 8192, 1024), np.float32)
    for c in range(8):
        b, s = c // 2, c % 2
        out[b, s * 4096:(s + 1) * 4096] = res.results[c]["out"]
    return out


# version 05
# speedup vs baseline: 1.0809x; 1.0572x over previous
import contextlib
import numpy as np
import ml_dtypes
import concourse.bass as bass
import concourse.mybir as mybir
from concourse.bass_utils import run_bass_kernel_spmd
from concourse.alu_op_type import AluOpType

F32 = mybir.dt.float32
BF16 = mybir.dt.bfloat16
AF = mybir.ActivationFunctionType
AX = mybir.AxisListType.X

NH = 16
NPAIR = 8
C = 512
NMC = 8
NPC = 8
W = 256
NWT = 4
NE = 16
SH_FF = 2816
NFB = SH_FF // 128

S4 = float(32.0 ** -0.25)
LN32_HALF = float(0.5 * np.log(32.0))


class Stream:
    def __init__(self, sem, step=1):
        self.sem = sem
        self.v = 0
        self.step = step

    def tag(self, instr, n=None):
        n = self.step if n is None else n
        instr.then_inc(self.sem, n)
        self.v += n
        return self.v


def build_nc(n_prefix_chunks=NPC, n_main_chunks=NMC):
    nc = bass.Bass("TRN2", target_bir_lowering=False)
    n_seg = max(1, (n_main_chunks + 1) // 2)

    def din(name, shape):
        return nc.dram_tensor(name, shape, F32, kind="ExternalInput")

    xm_h = din("xm", [4096, 1024])
    xp_h = din("xp", [4096, 1024])
    wqkv_h = din("wqkv", [3072, 1024])
    bqkv_h = din("bqkv", [1, 3072])
    wout_h = din("wout", [1024, 1024])
    bout_h = din("bout", [1, 1024])
    omg_h = din("omg", [2, 16, 64, 32])
    qn_h = din("qn", [2, 1])
    qw_h = din("qw", [2, 1])
    gw_h = din("gw", [16, 1024])
    wgu_h = din("wgu", [16, 1024, 1024])
    wdn_h = din("wdn", [16, 512, 1024])
    swg_h = din("swg", [2816, 1024])
    swu_h = din("swu", [2816, 1024])
    swd_h = din("swd", [1024, 2816])
    sgw_h = din("sgw", [1, 1024])
    cident_h = din("cident", [128, 128])
    cmaskT_h = din("cmaskT", [128, 128])
    cgbd_h = din("cgbd", [16, 1024])
    cece_h = din("cece", [16, 2048])
    cnormE_h = din("cnormE", [128, 128])
    cpatT_h = din("cpatT", [2, 128])
    pz_h = din("pz", [128, 1])

    out_h = nc.dram_tensor("out", [4096, 1024], F32, kind="ExternalOutput")

    xbf_h = nc.dram_tensor("xbf", [8192, 1024], BF16, kind="Internal")
    wqkv_bf_h = nc.dram_tensor("wqkv_bf", [3072, 1024], BF16, kind="Internal")
    wout_bf_h = nc.dram_tensor("wout_bf", [1024, 1024], BF16, kind="Internal")
    swg_bf_h = nc.dram_tensor("swg_bf", [2816, 1024], BF16, kind="Internal")
    swu_bf_h = nc.dram_tensor("swu_bf", [2816, 1024], BF16, kind="Internal")
    swd_bf_h = nc.dram_tensor("swd_bf", [1024, 2816], BF16, kind="Internal")
    swgT_h = nc.dram_tensor("swgT", [1024, 2816], BF16, kind="Internal")
    swuT_h = nc.dram_tensor("swuT", [1024, 2816], BF16, kind="Internal")
    swdT_h = nc.dram_tensor("swdT", [2816, 1024], BF16, kind="Internal")
    wgu_bf_h = nc.dram_tensor("wgu_bf", [16, 1024, 1024], BF16, kind="Internal")
    wdn_bf_h = nc.dram_tensor("wdn_bf", [16, 512, 1024], BF16, kind="Internal")

    es = contextlib.ExitStack()
    sb = lambda name, shape, dt: es.enter_context(nc.sbuf_tensor(name, shape, dt))

    ident_b = sb("ident_b", [128, 128], BF16)
    maskT_b = sb("maskT_b", [128, 128], BF16)
    ones_b = sb("ones_b", [128, 128], BF16)
    ones_f = sb("ones_f", [128, 128], F32)
    gbd_b = sb("gbd_b", [16, 1024], BF16)
    ece_b = sb("ece_b", [16, 2048], BF16)
    normE_b = sb("normE_b", [128, 128], BF16)
    patT_f = sb("patT_f", [2, 128], F32)
    q2_f = sb("q2_f", [2, 8], F32)
    pz_f = sb("pz_f", [128, 1], F32)
    qsc_f = sb("qsc_f", [128, 8], F32)
    bqkv_f = sb("bqkv_f", [128, 24], F32)
    bvbc_f = sb("bvbc_f", [128, 1024], F32)
    boutbc_f = sb("boutbc_f", [128, 1024], F32)
    omgbd_b = sb("omgbd_b", [128, 1024], BF16)
    gwT_b = sb("gwT_b", [128, 128], BF16)
    sgwT_b = sb("sgwT_b", [128, 8], BF16)
    state_f = sb("state_f", [128, 1024], F32)
    sa_b = sb("sa_b", [128, 520], BF16)
    sbp_b = sb("sbp_b", [128, 520], BF16)
    h1f = sb("h1f", [128, 8192], F32)
    stg_b = sb("stg_b", [128, 1024], BF16)

    G = Stream(es.enter_context(nc.semaphore("semG")), 16)
    X = Stream(es.enter_context(nc.semaphore("semX")), 16)
    P = Stream(es.enter_context(nc.semaphore("semP")), 1)
    V = Stream(es.enter_context(nc.semaphore("semV")), 1)
    A = Stream(es.enter_context(nc.semaphore("semA")), 1)
    O = Stream(es.enter_context(nc.semaphore("semO")), 16)

    block = es.enter_context(nc.Block())

    def on(engname, fn):
        getattr(block, engname)(fn)

    def barrier():
        vals = [(G, G.v), (X, X.v), (P, P.v), (V, V.v), (A, A.v)]
        for eng in ("tensor", "vector", "scalar", "gpsimd", "sync"):
            def _b(e, vals=vals):
                for s, v in vals:
                    if v:
                        e.wait_ge(s.sem, v)
            on(eng, _b)

    def v_pre(v):
        v.memset(ones_b[:, :], 1.0)
        v.memset(ones_f[:, :], 1.0)
        v.memset(omgbd_b[:, :], 0.0)
        v.memset(state_f[:, :], 0.0)
        v.memset(sa_b[:, :], 0.0)
        v.memset(sbp_b[:, :], 0.0)
        v.memset(qsc_f[:, :], 0.0)
        v.memset(qsc_f[:, 3:4], 1e-5)
        v.memset(qsc_f[:, 4:5], 2e-5)
        v.memset(qsc_f[:, 5:6], 0.34657359027997264)
        v.memset(bvbc_f[:, :], 0.0)
        V.tag(v.memset(boutbc_f[:, :], 0.0))
    on("vector", v_pre)
    v_memset = V.v

    mil = {}

    def g_startup(g):
        G.tag(g.dma_start(ident_b[:, :], cident_h.ap()))
        G.tag(g.dma_start(maskT_b[:, :], cmaskT_h.ap()))
        G.tag(g.dma_start(gbd_b[:, :], cgbd_h.ap()))
        G.tag(g.dma_start(ece_b[:, :], cece_h.ap()))
        G.tag(g.dma_start(normE_b[:, :], cnormE_h.ap()))
        G.tag(g.dma_start(patT_f[:, :], cpatT_h.ap()))
        G.tag(g.dma_start(q2_f[:, 0:1], qn_h.ap()))
        G.tag(g.dma_start(q2_f[:, 1:2], qw_h.ap()))
        mil["qnqw"] = G.v
        g.wait_ge(V.sem, v_memset)
        G.tag(g.dma_start(bqkv_f[:, :], bass.AP(bqkv_h, 0, [[1, 128], [128, 24]]),
                          allow_slow_non_contiguous=True))
        G.tag(g.dma_start(bvbc_f[0:1, :], bqkv_h.ap()[0:1, 2048:3072]))
        G.tag(g.dma_start(boutbc_f[0:1, :], bout_h.ap()))
        mil["biasrows"] = G.v
        for j in range(NPAIR):
            for par in range(2):
                for r in range(2):
                    c0 = 128 * j + 64 * par + 32 * r
                    G.tag(g.dma_start(omgbd_b[64 * par:64 * par + 64, c0:c0 + 32],
                                      omg_h.ap()[r, 2 * j + par]))
        for dc in range(8):
            G.tag(g.dma_start(gwT_b[:, 16 * dc:16 * dc + 16],
                              bass.AP(gw_h, 128 * dc, [[1, 128], [1024, 16]]),
                              allow_slow_non_contiguous=True))
        G.tag(g.dma_start(sgwT_b[:, :], bass.AP(sgw_h, 0, [[1, 128], [128, 8]]),
                          allow_slow_non_contiguous=True))
        G.tag(g.dma_start(pz_f[:, :], pz_h.ap()))
        mil["pz"] = G.v
        mil["smalls"] = G.v
        for pc in range(n_prefix_chunks):
            G.tag(g.dma_start(xbf_h.ap()[4096 + pc * C:4096 + (pc + 1) * C, :],
                              xp_h.ap()[pc * C:(pc + 1) * C, :]))
            mil[("xp", pc)] = G.v
        G.tag(g.dma_start(wqkv_bf_h.ap(), wqkv_h.ap()))
        mil["wqkv_bf"] = G.v
        G.tag(g.dma_start(wout_bf_h.ap(), wout_h.ap()))
        mil["wout_bf"] = G.v
        for mc in range(n_main_chunks):
            G.tag(g.dma_start(xbf_h.ap()[mc * C:(mc + 1) * C, :],
                              xm_h.ap()[mc * C:(mc + 1) * C, :]))
            mil[("xm", mc)] = G.v
        G.tag(g.dma_start(swg_bf_h.ap(), swg_h.ap()))
        mil["swg_bf"] = G.v
        G.tag(g.dma_start(swu_bf_h.ap(), swu_h.ap()))
        mil["swu_bf"] = G.v
        G.tag(g.dma_start(swd_bf_h.ap(), swd_h.ap()))
        mil["swd_bf"] = G.v
        for e in range(NE):
            G.tag(g.dma_start(wgu_bf_h.ap()[e], wgu_h.ap()[e]))
            G.tag(g.dma_start(wdn_bf_h.ap()[e], wdn_h.ap()[e]))
        mil["moe_bf"] = G.v
    on("gpsimd", g_startup)

    def v_quad(v):
        v.wait_ge(G.sem, mil["qnqw"])
        v.tensor_scalar(q2_f[:, 2:3], q2_f[:, 0:1], 2.0, 1e-30,
                        AluOpType.mult, AluOpType.max)
        v.tensor_scalar(q2_f[:, 4:5], q2_f[:, 1:2], 1e-30, None, AluOpType.max)
        V.tag(v.tensor_scalar(q2_f[:, 3:4], q2_f[:, 0:1], -1.0, None, AluOpType.mult))
    on("vector", v_quad)
    v_quad_c = V.v

    def a_quad(s):
        s.wait_ge(V.sem, v_quad_c)
        s.activation(q2_f[:, 2:3], q2_f[:, 2:3], AF.Ln)
        s.activation(q2_f[:, 2:3], q2_f[:, 2:3], AF.Exp, scale=0.5)
        A.tag(s.activation(q2_f[:, 4:5], q2_f[:, 4:5], AF.Ln))
    on("scalar", a_quad)
    a_quad_c = A.v

    def v_quad2(v):
        v.wait_ge(A.sem, a_quad_c)
        V.tag(v.tensor_scalar(q2_f[:, 4:5], q2_f[:, 4:5], 0.5, -LN32_HALF,
                              AluOpType.mult, AluOpType.add))
    on("vector", v_quad2)
    v_quad2_c = V.v

    with nc.psum_tensor("pst0", [128, 1024], F32) as pst0, \
         nc.psum_tensor("pst1", [128, 1024], F32) as pst1:
        def t_bc(t):
            t.wait_ge(V.sem, v_quad2_c)
            t.wait_ge(G.sem, mil["biasrows"])
            t.matmul(pst0[:, 0:3], patT_f[:, :], q2_f[:, 2:5], start=True, stop=True)
            t.matmul(pst0[:, 512:1024], ones_f[0:1, :], bvbc_f[0:1, 0:512], start=True, stop=True)
            t.matmul(pst1[:, 0:512], ones_f[0:1, :], bvbc_f[0:1, 512:1024], start=True, stop=True)
            P.tag(t.matmul(pst1[:, 512:1024], ones_f[0:1, :], boutbc_f[0:1, 0:512], start=True, stop=True))
        on("tensor", t_bc)
        p_bc = P.v

        def v_bc(v):
            v.wait_ge(P.sem, p_bc)
            v.tensor_copy(qsc_f[:, 0:3], pst0[:, 0:3])
            v.tensor_copy(bvbc_f[:, 0:512], pst0[:, 512:1024])
            v.tensor_copy(bvbc_f[:, 512:1024], pst1[:, 0:512])
            V.tag(v.tensor_copy(boutbc_f[:, 0:512], pst1[:, 512:1024]))
        on("vector", v_bc)
        v_bc_c = V.v

        def t_bc2(t):
            t.wait_ge(V.sem, v_bc_c)
            P.tag(t.matmul(pst0[:, 0:512], ones_f[0:1, :], boutbc_f[0:1, 512:1024],
                           start=True, stop=True))
        on("tensor", t_bc2)
        p_bc2 = P.v

        def v_bc2(v):
            v.wait_ge(P.sem, p_bc2)
            V.tag(v.tensor_copy(boutbc_f[:, 512:1024], pst0[:, 0:512]))
        on("vector", v_bc2)

    barrier()

    shT_jobs = []
    for (srcname, FF) in (("swg", SH_FF), ("swu", SH_FF)):
        for dc in range(8):
            for fb0 in range(0, FF, 512):
                shT_jobs.append((srcname, dc, fb0, min(512, FF - fb0)))
    for fc in range(SH_FF // 128):
        for db0 in range(0, 1024, 512):
            shT_jobs.append(("swd", fc, db0, 512))
    shT_state = {"done_g": None}

    def emit_sharedT():
        xbase = [X.v]
        gbase = [G.v]

        def x_shT(sy):
            sy.wait_ge(G.sem, mil["swg_bf"])
            seen_u = seen_d = False
            for idx, (kind, a0, b0, wdt) in enumerate(shT_jobs):
                if kind == "swu" and not seen_u:
                    sy.wait_ge(G.sem, mil["swu_bf"])
                    seen_u = True
                if kind == "swd" and not seen_d:
                    sy.wait_ge(G.sem, mil["swd_bf"])
                    seen_d = True
                if idx >= 2:
                    sy.wait_ge(G.sem, gbase[0] + (idx - 1) * 16)
                src_h = {"swg": swg_bf_h, "swu": swu_bf_h, "swd": swd_bf_h}[kind]
                slot = idx % 2
                X.tag(sy.dma_start(stg_b[:, 512 * slot:512 * slot + wdt],
                                   src_h.ap()[b0:b0 + wdt, 128 * a0:128 * a0 + 128],
                                   transpose=True))
        on("sync", x_shT)

        def g_shT(g):
            for idx, (kind, a0, b0, wdt) in enumerate(shT_jobs):
                g.wait_ge(X.sem, xbase[0] + (idx + 1) * 16)
                dst_h = {"swg": swgT_h, "swu": swuT_h, "swd": swdT_h}[kind]
                slot = idx % 2
                G.tag(g.dma_start(dst_h.ap()[128 * a0:128 * a0 + 128, b0:b0 + wdt],
                                  stg_b[:, 512 * slot:512 * slot + wdt]))
        on("gpsimd", g_shT)
        shT_state["done_g"] = G.v

    out_slot_war = [0]

    for seg in range(n_seg):
        with contextlib.ExitStack() as asx:
            asb = lambda name, shape, dt: asx.enter_context(
                nc.sbuf_tensor(f"{name}_s{seg}", shape, dt))
            wqkvT_sb = asb("wqkvT", [128, 24576], BF16)
            woutT_sb = asb("woutT", [128, 8192], BF16)
            xfm_b = asb("xfm", [128, 4096], BF16)
            xtm_f = asb("xtm", [128, 2048], F32)
            qkraw_b = asb("qkraw", [128, 1024], BF16)
            v1_b = asb("v1", [128, 4160], BF16)
            zsq_b = asb("zsq", [128, 1024], BF16)
            invq_b = asb("invq", [16, 512], BF16)
            invk_b = asb("invk", [16, 512], BF16)
            proj_f = asb("proj", [128, 512], F32)
            poly_f = asb("poly", [128, 512], F32)
            prf_f = asb("prf", [128, 512], F32)
            proj2_f = asb("proj2", [128, 512], F32)
            poly2_f = asb("poly2", [128, 512], F32)
            prf2_f = asb("prf2", [128, 512], F32)
            qf_b = asb("qf", [128, 4096], BF16)
            kf_b = asb("kf", [128, 4096], BF16)
            kftm_b = asb("kftm", [128, 4096], BF16)
            asb_b = asb("asbuf", [128, 1280], BF16)
            ctx_f = asb("ctx", [128, 512], F32)
            denr_b = asb("denr", [128, 512], BF16)
            attnh_b = asb("attnh", [64, 512], BF16)
            attn_b = asb("attn", [128, 4096], BF16)
            r_f = asb("r", [128, 1024], F32)
            bna_f = asb("bna", [128, 20], F32)
            rstd_f = asb("rstd", [128, 2], F32)

            PA = asx.enter_context(nc.psum_tensor(f"PA_s{seg}", [128, 1024], F32))
            PB = asx.enter_context(nc.psum_tensor(f"PB_s{seg}", [128, 1024], F32))
            PC = asx.enter_context(nc.psum_tensor(f"PC_s{seg}", [128, 512], F32))
            PD = asx.enter_context(nc.psum_tensor(f"PD_s{seg}", [128, 512], F32))
            PS = asx.enter_context(nc.psum_tensor(f"PS_s{seg}", [128, 512], F32))
            Pbf = asx.enter_context(nc.psum_tensor(f"Pbf_s{seg}", [128, 1024], BF16))

            def x_wT(sy):
                sy.wait_ge(G.sem, mil["wqkv_bf"])
                for dc in range(8):
                    for jb in range(6):
                        X.tag(sy.dma_start(
                            wqkvT_sb[:, 3072 * dc + 512 * jb:3072 * dc + 512 * jb + 512],
                            wqkv_bf_h.ap()[512 * jb:512 * jb + 512, 128 * dc:128 * dc + 128],
                            transpose=True))
                sy.wait_ge(G.sem, mil["wout_bf"])
                for cc in range(8):
                    for ob in range(2):
                        X.tag(sy.dma_start(
                            woutT_sb[:, 1024 * cc + 512 * ob:1024 * cc + 512 * ob + 512],
                            wout_bf_h.ap()[512 * ob:512 * ob + 512, 128 * cc:128 * cc + 128],
                            transpose=True))
            on("sync", x_wT)
            x_wq = X.v - 16 * 16
            x_wo = X.v

            war_A = [0]
            war_ctx = [0]
            war_bden = [0]
            war_mv = [0]
            war_state = [0]
            war_proj = [0]
            war_xtm = [0, 0]

            def emit_chunk(kind, ci):
                is_main = kind == "main"
                xrow = ci * C if is_main else 4096 + ci * C
                xcast_g = mil[("xm", ci)] if is_main else mil[("xp", ci)]

                def x_chunk(sy):
                    sy.wait_ge(G.sem, xcast_g)
                    for dc in range(8):
                        X.tag(sy.dma_start(xfm_b[:, 512 * dc:512 * dc + 512],
                                           xbf_h.ap()[xrow:xrow + C, 128 * dc:128 * dc + 128],
                                           transpose=True))
                on("sync", x_chunk)
                x_xfm = X.v

                evq = [0]
                prev_norm = [0]
                for j in range(NPAIR):
                    def t_qk(t, j=j):
                        t.wait_ge(X.sem, max(x_xfm, x_wq))
                        if evq[0]:
                            t.wait_ge(A.sem, evq[0])
                        for dc in range(8):
                            if is_main:
                                t.matmul(PA[:, 0:512],
                                         wqkvT_sb[:, 3072 * dc + 128 * j:3072 * dc + 128 * (j + 1)],
                                         xfm_b[:, 512 * dc:512 * dc + 512],
                                         start=(dc == 0), stop=(dc == 7))
                            last = t.matmul(PA[:, 512:1024],
                                     wqkvT_sb[:, 3072 * dc + 1024 + 128 * j:3072 * dc + 1024 + 128 * (j + 1)],
                                     xfm_b[:, 512 * dc:512 * dc + 512],
                                     start=(dc == 0), stop=(dc == 7))
                        P.tag(last)
                    on("tensor", t_qk)
                    p_qk = P.v

                    def a_evq(s, j=j):
                        s.wait_ge(P.sem, p_qk)
                        if is_main:
                            s.activation(qkraw_b[:, 0:512], PA[:, 0:512], AF.Identity,
                                         bias=bqkv_f[:, j:j + 1])
                        A.tag(s.activation(qkraw_b[:, 512:1024], PA[:, 512:1024], AF.Identity,
                                           bias=bqkv_f[:, 8 + j:9 + j]))
                    on("scalar", a_evq)
                    evq[0] = A.v

                    def a_sq(s, j=j):
                        if prev_norm[0]:
                            s.wait_ge(P.sem, prev_norm[0])
                        if is_main:
                            s.activation(zsq_b[:, 0:512], qkraw_b[:, 0:512], AF.Square)
                        A.tag(s.activation(zsq_b[:, 512:1024], qkraw_b[:, 512:1024], AF.Square))
                    on("scalar", a_sq)
                    a_sq_c = A.v

                    def v_keep(v, j=j):
                        v.wait_ge(A.sem, a_sq_c)
                        if is_main:
                            v.tensor_copy(qf_b[:, 512 * j:512 * (j + 1)], qkraw_b[:, 0:512])
                        V.tag(v.tensor_copy(kf_b[:, 512 * j:512 * (j + 1)], qkraw_b[:, 512:1024]))
                    on("vector", v_keep)

                    def t_norm(t, j=j):
                        t.wait_ge(A.sem, a_sq_c)
                        if is_main:
                            t.matmul(PD[0:16, 0:512], normE_b[:, 16 * j:16 * (j + 1)],
                                     zsq_b[:, 0:512], start=(j == 0), stop=(j == NPAIR - 1))
                        P.tag(t.matmul(PS[0:16, 0:512], normE_b[:, 16 * j:16 * (j + 1)],
                                 zsq_b[:, 512:1024], start=(j == 0), stop=(j == NPAIR - 1)))
                    on("tensor", t_norm)
                    prev_norm[0] = P.v
                p_norm_done = P.v
                v_keep_done = V.v

                prev_vev = [0]
                for i in range(4):
                    def t_v(t, i=i):
                        if prev_vev[0]:
                            t.wait_ge(V.sem, prev_vev[0])
                        for ob in range(2):
                            for dc in range(8):
                                last = t.matmul(PB[:, 512 * ob:512 * (ob + 1)],
                                         xfm_b[:, 512 * dc + 128 * i:512 * dc + 128 * (i + 1)],
                                         wqkvT_sb[:, 3072 * dc + 2048 + 512 * ob:3072 * dc + 2048 + 512 * (ob + 1)],
                                         start=(dc == 0), stop=(dc == 7))
                        P.tag(last)
                    on("tensor", t_v)
                    p_v = P.v

                    def v_ev(v, i=i):
                        v.wait_ge(P.sem, p_v)
                        v.memset(v1_b[:, 1040 * i:1040 * (i + 1)], 1.0)
                        V.tag(v.tensor_tensor(
                            bass.AP(v1_b, 1040 * i, [[4160, 128], [65, 16], [1, 64]]),
                            bass.AP(PB, 0, [[1024, 128], [64, 16], [1, 64]]),
                            bass.AP(bvbc_f, 0, [[1024, 128], [64, 16], [1, 64]]),
                            AluOpType.add))
                    on("vector", v_ev)
                    prev_vev[0] = V.v
                v_v1 = V.v

                def v_inv(v):
                    v.wait_ge(P.sem, p_norm_done)
                    if is_main:
                        v.tensor_scalar(proj_f[0:16, 0:512], PD[0:16, 0:512], 1e-24, None,
                                        AluOpType.max)
                    V.tag(v.tensor_scalar(poly_f[0:16, 0:512], PS[0:16, 0:512], 1e-24, None,
                                          AluOpType.max))
                on("vector", v_inv)
                v_invp = V.v

                def a_inv(s):
                    s.wait_ge(V.sem, v_invp)
                    if is_main:
                        s.activation(proj_f[0:16, 0:512], proj_f[0:16, 0:512], AF.Ln)
                        s.activation(invq_b[0:16, 0:512], proj_f[0:16, 0:512], AF.Exp, scale=-0.5)
                    s.activation(poly_f[0:16, 0:512], poly_f[0:16, 0:512], AF.Ln)
                    A.tag(s.activation(invk_b[0:16, 0:512], poly_f[0:16, 0:512], AF.Exp, scale=-0.5))
                on("scalar", a_inv)
                a_invc = A.v

                chain_war_v = {0: 0, 1: 0}
                chain_war_a = {0: 0, 1: 0}
                chain_idx = [0]

                def emit_chain(which, j):
                    src_b = qf_b if which == "q" else kf_b
                    inv_b = invq_b if which == "q" else invk_b
                    cp = chain_idx[0] % 2
                    chain_idx[0] += 1
                    PCc = PC[:, 0:512] if cp == 0 else PA[:, 0:512]
                    PDc = PD[:, 0:512] if cp == 0 else PA[:, 512:1024]
                    prj = proj_f if cp == 0 else proj2_f
                    pol = poly_f if cp == 0 else poly2_f
                    prf = prf_f if cp == 0 else prf2_f

                    def t_c(t, j=j, src_b=src_b, inv_b=inv_b, cp=cp):
                        t.wait_ge(A.sem, a_invc)
                        t.wait_ge(V.sem, v_keep_done)
                        if chain_war_v[cp]:
                            t.wait_ge(V.sem, chain_war_v[cp])
                        if chain_war_a[cp]:
                            t.wait_ge(A.sem, chain_war_a[cp])
                        t.matmul(PCc, omgbd_b[:, 128 * j:128 * (j + 1)],
                                 src_b[:, 512 * j:512 * (j + 1)], start=True, stop=True)
                        P.tag(t.matmul(PDc, gbd_b[:, 128 * j:128 * (j + 1)],
                                 inv_b[0:16, 0:512], start=True, stop=True))
                    on("tensor", t_c)
                    p_c = P.v

                    def a_c1(s):
                        s.wait_ge(P.sem, p_c)
                        A.tag(s.activation(prj[:, :], PCc, AF.Copy))
                    on("scalar", a_c1)
                    a_c1c = A.v

                    def v_c1(v):
                        v.wait_ge(A.sem, a_c1c)
                        V.tag(v.tensor_tensor(prj[:, :], prj[:, :], PDc,
                                              AluOpType.mult))
                    on("vector", v_c1)
                    v_c1c = V.v

                    def a_c2(s):
                        s.wait_ge(V.sem, v_c1c)
                        A.tag(s.activation(pol[:, :], prj[:, :], AF.Square, scale=S4))
                    on("scalar", a_c2)
                    a_c2c = A.v

                    def v_c2(v):
                        v.wait_ge(A.sem, a_c2c)
                        v.tensor_scalar(prj[:, :], prj[:, :], qsc_f[:, 0:1], qsc_f[:, 1:2],
                                        AluOpType.mult, AluOpType.add)
                        V.tag(v.tensor_scalar(prj[:, :], prj[:, :], -20.0, 20.0,
                                              AluOpType.max, AluOpType.min))
                    on("vector", v_c2)
                    v_c2c = V.v

                    def a_c3(s):
                        s.wait_ge(V.sem, v_c2c)
                        A.tag(s.activation(prf[:, :], prj[:, :], AF.Exp,
                                           bias=qsc_f[:, 2:3]))
                    on("scalar", a_c3)
                    chain_war_a[cp] = A.v
                    a_c3c = A.v

                    def v_c3(v, j=j, src_b=src_b):
                        v.wait_ge(A.sem, a_c3c)
                        if is_main:
                            V.tag(v.tensor_tensor(src_b[:, 512 * j:512 * (j + 1)], pol[:, :],
                                                  prf[:, :], AluOpType.mult))
                        else:
                            v.wait_ge(G.sem, mil["pz"])
                            v.tensor_tensor(src_b[:, 512 * j:512 * (j + 1)], pol[:, :],
                                            prf[:, :], AluOpType.mult)
                            V.tag(v.tensor_scalar(src_b[:, 512 * j:512 * (j + 1)],
                                                  src_b[:, 512 * j:512 * (j + 1)],
                                                  pz_f[:, 0:1], None, AluOpType.mult))
                    on("vector", v_c3)
                    chain_war_v[cp] = V.v

                for j in range(NPAIR):
                    if is_main:
                        emit_chain("q", j)
                    emit_chain("k", j)
                v_chains = V.v

                tp_batches = []
                jobs_tp = [(j, i) for j in range(NPAIR) for i in range(4)]
                for b0 in range(0, 32, 8):
                    batch = jobs_tp[b0:b0 + 8]

                    def t_tp(t, batch=batch, b0=b0):
                        t.wait_ge(V.sem, v_chains)
                        if tp_batches:
                            t.wait_ge(V.sem, tp_batches[-1])
                        for s_i, (j, i) in enumerate(batch):
                            last = t.transpose(Pbf[:, 128 * s_i:128 * (s_i + 1)],
                                        kf_b[:, 512 * j + 128 * i:512 * j + 128 * (i + 1)],
                                        ident_b[:, :])
                        P.tag(last)
                    on("tensor", t_tp)
                    p_tp = P.v

                    def v_tp(v, batch=batch):
                        v.wait_ge(P.sem, p_tp)
                        for s_i, (j, i) in enumerate(batch):
                            last = v.tensor_copy(kftm_b[:, 1024 * i + 128 * j:1024 * i + 128 * (j + 1)],
                                          Pbf[:, 128 * s_i:128 * (s_i + 1)])
                        V.tag(last)
                    on("vector", v_tp)
                    tp_batches.append(V.v)
                v_kftm = V.v

                if is_main:
                    def a_reg(j2):
                        return [(PA, 0), (PA, 512), (PB, 0), (PA, 896)][j2]
                    asb_off = [0, 512, 896, 1152]

                    for h in range(NH):
                        j = h // 2
                        par = h % 2
                        bp = 64 * par

                        def t_A(t, j=j, bp=bp):
                            t.wait_ge(V.sem, v_chains)
                            if war_A[0]:
                                t.wait_ge(V.sem, war_A[0])
                            for j2 in range(4):
                                n2 = 512 - 128 * j2
                                pt, c0 = a_reg(j2)
                                last = t.matmul(pt[:, c0:c0 + n2],
                                         kf_b[bp:bp + 64, 512 * j + 128 * j2:512 * j + 128 * (j2 + 1)],
                                         qf_b[bp:bp + 64, 512 * j + 128 * j2:512 * j + 512],
                                         start=True, stop=True)
                            P.tag(last)
                        on("tensor", t_A)
                        p_A = P.v

                        def v_evA(v):
                            v.wait_ge(P.sem, p_A)
                            for j2 in range(4):
                                n2 = 512 - 128 * j2
                                pt, c0 = a_reg(j2)
                                d0 = asb_off[j2]
                                last = v.tensor_tensor(asb_b[:, d0:d0 + 128], pt[:, c0:c0 + 128],
                                                maskT_b[:, :], AluOpType.mult)
                                if n2 > 128:
                                    last = v.tensor_copy(asb_b[:, d0 + 128:d0 + n2],
                                                  pt[:, c0 + 128:c0 + n2])
                            V.tag(last)
                        on("vector", v_evA)
                        war_A[0] = V.v
                        v_evAc = V.v

                        def t_ctx(t, h=h, j=j, par=par):
                            t.wait_ge(V.sem, v_evAc)
                            t.wait_ge(V.sem, v_v1)
                            t.wait_ge(V.sem, war_state[0])
                            if war_ctx[0]:
                                t.wait_ge(V.sem, war_ctx[0])
                            st_b = sa_b if par == 0 else sbp_b
                            t.matmul(PB[0:65, 512:1024], st_b[:, 65 * j:65 * (j + 1)],
                                     qf_b[:, 512 * j:512 * (j + 1)], start=True, stop=False)
                            for j2 in range(4):
                                n2 = 512 - 128 * j2
                                last = t.matmul(PB[0:65, 512 + 128 * j2:1024],
                                         v1_b[:, 1040 * j2 + 65 * h:1040 * j2 + 65 * (h + 1)],
                                         asb_b[:, asb_off[j2]:asb_off[j2] + n2],
                                         start=False, stop=(j2 == 3))
                            P.tag(last)
                        on("tensor", t_ctx)
                        p_ctx = P.v

                        def v_ctx(v):
                            v.wait_ge(P.sem, p_ctx)
                            v.tensor_copy(ctx_f[0:65, :], PB[0:65, 512:1024])
                            v.tensor_scalar(ctx_f[64:65, :], ctx_f[64:65, :], 1e-6, None,
                                            AluOpType.max)
                            with nc.allow_low_precision(reason="bf16 matmul operand"):
                                V.tag(v.reciprocal(denr_b[64:65, :], ctx_f[64:65, :]))
                        on("vector", v_ctx)
                        war_ctx[0] = V.v
                        v_ctxc = V.v

                        def t_bden(t):
                            t.wait_ge(V.sem, v_ctxc)
                            if war_bden[0]:
                                t.wait_ge(V.sem, war_bden[0])
                            P.tag(t.matmul(PD[0:64, 0:512], ones_b[64:65, 0:64],
                                           denr_b[64:65, :], start=True, stop=True))
                        on("tensor", t_bden)
                        p_bden = P.v

                        def v_div(v, j=j, par=par):
                            v.wait_ge(P.sem, p_bden)
                            if par == 0:
                                V.tag(v.tensor_tensor(attn_b[0:64, 512 * j:512 * (j + 1)],
                                                      ctx_f[0:64, :], PD[0:64, 0:512],
                                                      AluOpType.mult))
                            else:
                                V.tag(v.tensor_tensor(attnh_b[0:64, :], ctx_f[0:64, :],
                                                      PD[0:64, 0:512], AluOpType.mult))
                        on("vector", v_div)
                        war_bden[0] = V.v
                        v_divc = V.v

                        if par == 1:
                            def t_mv(t):
                                t.wait_ge(V.sem, v_divc)
                                if war_mv[0]:
                                    t.wait_ge(V.sem, war_mv[0])
                                P.tag(t.matmul(PS[64:128, 0:512], ident_b[0:64, 0:64],
                                               attnh_b[0:64, :], start=True, stop=True,
                                               tile_position=(0, 64)))
                            on("tensor", t_mv)
                            p_mv = P.v

                            def v_mv(v, j=j):
                                v.wait_ge(P.sem, p_mv)
                                V.tag(v.tensor_copy(attn_b[64:128, 512 * j:512 * (j + 1)],
                                                    PS[64:128, 0:512]))
                            on("vector", v_mv)
                            war_mv[0] = V.v

                def t_delta(t):
                    t.wait_ge(V.sem, v_kftm)
                    t.wait_ge(V.sem, v_v1)
                    if is_main and war_A[0]:
                        t.wait_ge(V.sem, war_A[0])
                    if war_state[0]:
                        t.wait_ge(V.sem, war_state[0])
                    for j in range(NPAIR):
                        for par in range(2):
                            h = 2 * j + par
                            for i in range(4):
                                if par == 0:
                                    t.matmul(PA[0:64, 128 * j:128 * j + 65],
                                             kftm_b[:, 1024 * i + 128 * j:1024 * i + 128 * j + 64],
                                             v1_b[:, 1040 * i + 65 * h:1040 * i + 65 * (h + 1)],
                                             start=(i == 0), stop=(i == 3))
                                else:
                                    last = t.matmul(PA[64:128, 128 * j:128 * j + 65],
                                             kftm_b[:, 1024 * i + 128 * j + 64:1024 * i + 128 * (j + 1)],
                                             v1_b[:, 1040 * i + 65 * h:1040 * i + 65 * (h + 1)],
                                             start=(i == 0), stop=(i == 3),
                                             tile_position=(0, 64))
                    P.tag(last)
                on("tensor", t_delta)
                p_delta = P.v

                def v_state(v):
                    v.wait_ge(P.sem, p_delta)
                    v.tensor_tensor(
                        bass.AP(state_f, 0, [[1024, 128], [128, 8], [1, 65]]),
                        bass.AP(state_f, 0, [[1024, 128], [128, 8], [1, 65]]),
                        bass.AP(PA, 0, [[1024, 128], [128, 8], [1, 65]]),
                        AluOpType.add)
                    v.tensor_scalar(
                        bass.AP(sa_b, 0, [[520, 64], [65, 8], [1, 65]]),
                        bass.AP(state_f, 0, [[1024, 64], [128, 8], [1, 65]]),
                        1.0, None, AluOpType.mult)
                    V.tag(v.tensor_scalar(
                        bass.AP(sbp_b, 64 * 520, [[520, 64], [65, 8], [1, 65]]),
                        bass.AP(state_f, 64 * 1024, [[1024, 64], [128, 8], [1, 65]]),
                        1.0, None, AluOpType.mult))
                on("vector", v_state)
                war_state[0] = V.v

                if is_main:
                    v_attn_all = V.v
                    sl0 = 4 * (ci % 2)
                    for i in range(4):
                        def g_xtm(g, i=i):
                            if war_xtm[i % 2]:
                                g.wait_ge(V.sem, war_xtm[i % 2])
                            G.tag(g.dma_start(xtm_f[:, 1024 * (i % 2):1024 * (i % 2) + 1024],
                                              xm_h.ap()[ci * C + 128 * i:ci * C + 128 * (i + 1), :]))
                        on("gpsimd", g_xtm)
                        g_xtm_c = G.v

                        def t_proj(t, i=i):
                            t.wait_ge(V.sem, v_attn_all)
                            t.wait_ge(X.sem, x_wo)
                            if war_proj[0]:
                                t.wait_ge(V.sem, war_proj[0])
                            for ob in range(2):
                                dst = PC if ob == 0 else PS
                                for cc in range(8):
                                    last = t.matmul(dst[:, 0:512],
                                             attn_b[:, 512 * cc + 128 * i:512 * cc + 128 * (i + 1)],
                                             woutT_sb[:, 1024 * cc + 512 * ob:1024 * cc + 512 * (ob + 1)],
                                             start=(cc == 0), stop=(cc == 7))
                            P.tag(last)
                        on("tensor", t_proj)
                        p_proj = P.v

                        def v_ln(v, i=i):
                            v.wait_ge(P.sem, p_proj)
                            v.wait_ge(G.sem, g_xtm_c)
                            v.tensor_tensor(r_f[:, 0:512], PC[:, 0:512], boutbc_f[:, 0:512],
                                            AluOpType.add)
                            v.tensor_tensor(r_f[:, 512:1024], PS[:, 0:512], boutbc_f[:, 512:1024],
                                            AluOpType.add)
                            v.tensor_tensor(r_f[:, :], r_f[:, :],
                                            xtm_f[:, 1024 * (i % 2):1024 * (i % 2) + 1024],
                                            AluOpType.add)
                            v.bn_stats(bna_f[:, 0:6], r_f[:, 0:512])
                            V.tag(v.bn_stats(bna_f[:, 6:12], r_f[:, 512:1024]))
                            v.wait_ge(V.sem, V.v)
                            v.bn_aggr(bna_f[:, 12:14], bna_f[:, 0:6])
                            V.tag(v.bn_aggr(bna_f[:, 14:16], bna_f[:, 6:12]))
                            v.wait_ge(V.sem, V.v)
                            v.tensor_tensor(bna_f[:, 16:17], bna_f[:, 12:13],
                                            bna_f[:, 14:15], AluOpType.add)
                            v.tensor_tensor(bna_f[:, 17:18], bna_f[:, 12:13],
                                            bna_f[:, 14:15], AluOpType.subtract)
                            V.tag(v.tensor_tensor(bna_f[:, 18:19], bna_f[:, 13:14],
                                            bna_f[:, 15:16], AluOpType.add))
                            v.wait_ge(V.sem, V.v)
                            V.tag(v.tensor_tensor(bna_f[:, 19:20], bna_f[:, 17:18],
                                            bna_f[:, 17:18], AluOpType.mult))
                            v.wait_ge(V.sem, V.v)
                            V.tag(v.tensor_scalar(bna_f[:, 19:20], bna_f[:, 19:20],
                                            0.5, None, AluOpType.mult))
                            v.wait_ge(V.sem, V.v)
                            V.tag(v.tensor_tensor(bna_f[:, 18:19], bna_f[:, 18:19],
                                                  bna_f[:, 19:20], AluOpType.add))
                        on("vector", v_ln)
                        v_lnc = V.v
                        war_proj[0] = V.v

                        def a_rstd(s):
                            s.wait_ge(V.sem, v_lnc)
                            s.activation(rstd_f[:, 0:1], bna_f[:, 18:19], AF.Ln,
                                         bias=qsc_f[:, 4:5])
                            A.tag(s.activation(rstd_f[:, 0:1], rstd_f[:, 0:1], AF.Exp,
                                               scale=-0.5, bias=qsc_f[:, 5:6]))
                        on("scalar", a_rstd)
                        a_rstdc = A.v

                        def v_h1(v, i=i, sl0=sl0):
                            v.wait_ge(A.sem, a_rstdc)
                            V.tag(v.tensor_scalar(rstd_f[:, 1:2], bna_f[:, 16:17], rstd_f[:, 0:1],
                                            -0.5, AluOpType.mult, AluOpType.mult))
                            v.wait_ge(V.sem, V.v)
                            V.tag(v.tensor_scalar(
                                h1f[:, 1024 * (sl0 + i):1024 * (sl0 + i + 1)],
                                r_f[:, :], rstd_f[:, 0:1], rstd_f[:, 1:2],
                                AluOpType.mult, AluOpType.add))
                        on("vector", v_h1)
                        war_xtm[i % 2] = V.v

                barrier()

            if seg == 0:
                for pc in range(n_prefix_chunks):
                    emit_chunk("prefix", pc)
                emit_sharedT()
            for k in range(2):
                mc = 2 * seg + k
                if mc < n_main_chunks:
                    emit_chunk("main", mc)

            barrier()

        with contextlib.ExitStack() as fsx:
            fsb = lambda name, shape, dt: fsx.enter_context(
                nc.sbuf_tensor(f"{name}_f{seg}", shape, dt))
            h1b_b = fsb("h1b", [128, 2048], BF16)
            h1fm_b = fsb("h1fm", [128, 8192], BF16)
            acc_f = fsb("acc", [128, 8192], F32)
            wgu_sb = fsb("wgu", [128, 16384], BF16)
            wdn_sb = fsb("wdn", [128, 8192], BF16)
            swgt_sb = fsb("swgt", [128, 2048], BF16)
            swut_sb = fsb("swut", [128, 2048], BF16)
            swdt_sb = fsb("swdt", [128, 2048], BF16)
            silu_f = fsb("silu", [128, 512], F32)
            he_f = fsb("he", [128, 512], F32)
            hew_b = fsb("hew", [128, 1024], BF16)
            shsil_f = fsb("shsil", [128, 512], F32)
            shhe_f = fsb("shhe", [128, 256], F32)
            shhew_b = fsb("shhew", [128, 256], BF16)
            logtm_f = fsb("logtm", [128, 16], F32)
            router_f = fsb("router", [128, 64], F32)
            comb_a = fsb("comba", [128, 128], F32)
            combfm_b = fsb("combfm", [16, 1024], BF16)
            srow_b = fsb("srow", [1, 1024], BF16)
            tmp_f = fsb("tmp", [128, 1024], F32)
            bs_f = fsb("bs", [128, 1024], F32)
            r2_f = fsb("r2", [128, 1024], F32)
            outst_f = fsb("outst", [128, 1024], F32)
            bna2_f = fsb("bna2", [128, 20], F32)
            rstd2_f = fsb("rstd2", [128, 2], F32)

            with nc.psum_tensor(f"PbfT_f{seg}", [128, 1024], BF16) as PbfT, \
                 nc.psum_tensor(f"PFT_f{seg}", [128, 1024], F32) as PFT:
                def v_acc0(v):
                    V.tag(v.memset(acc_f[:, :], 0.0))
                on("vector", v_acc0)

                tpb = []
                h1bc = [0, 0]
                for ts in range(8):
                    hslot = ts % 2

                    def v_h1b(v, ts=ts, hslot=hslot):
                        if len(tpb) >= 2:
                            v.wait_ge(P.sem, tp_pc[ts - 2])
                        V.tag(v.tensor_copy(h1b_b[:, 1024 * hslot:1024 * (hslot + 1)],
                                            h1f[:, 1024 * ts:1024 * (ts + 1)]))
                    on("vector", v_h1b)
                    h1bc[hslot] = V.v
                    if ts == 0:
                        tp_pc = {}

                    def t_tph(t, ts=ts, hslot=hslot):
                        t.wait_ge(V.sem, h1bc[hslot])
                        if tpb:
                            t.wait_ge(V.sem, tpb[-1])
                        for dc in range(8):
                            last = t.transpose(PbfT[:, 128 * dc:128 * (dc + 1)],
                                        h1b_b[:, 1024 * hslot + 128 * dc:1024 * hslot + 128 * (dc + 1)],
                                        ident_b[:, :])
                        P.tag(last)
                    on("tensor", t_tph)
                    tp_pc[ts] = P.v
                    p_tph = P.v

                    def v_tph(v, ts=ts):
                        v.wait_ge(P.sem, p_tph)
                        for dc in range(8):
                            last = v.tensor_copy(h1fm_b[:, 1024 * dc + 128 * ts:1024 * dc + 128 * (ts + 1)],
                                          PbfT[:, 128 * dc:128 * (dc + 1)])
                        V.tag(last)
                    on("vector", v_tph)
                    tpb.append(V.v)
                v_h1fm = V.v

                def t_log(t):
                    t.wait_ge(V.sem, v_h1fm)
                    for half in range(2):
                        for dc in range(8):
                            last = t.matmul(PFT[0:16, 512 * half:512 * (half + 1)],
                                     gwT_b[:, 16 * dc:16 * (dc + 1)],
                                     h1fm_b[:, 1024 * dc + 512 * half:1024 * dc + 512 * (half + 1)],
                                     start=(dc == 0), stop=(dc == 7))
                    P.tag(last)
                on("tensor", t_log)
                p_log = P.v

                def v_log(v):
                    v.wait_ge(P.sem, p_log)
                    V.tag(v.tensor_copy(combfm_b[0:16, 0:1024], PFT[0:16, 0:1024]))
                on("vector", v_log)
                v_logc = V.v

                def t_sgw(t):
                    t.wait_ge(V.sem, v_logc)
                    for half in range(2):
                        for dc in range(8):
                            last = t.matmul(PFT[0:1, 512 * half:512 * (half + 1)],
                                     sgwT_b[:, dc:dc + 1],
                                     h1fm_b[:, 1024 * dc + 512 * half:1024 * dc + 512 * (half + 1)],
                                     start=(dc == 0), stop=(dc == 7))
                    P.tag(last)
                on("tensor", t_sgw)
                p_sgw = P.v

                def a_sgm(s):
                    s.wait_ge(P.sem, p_sgw)
                    A.tag(s.activation(srow_b[0:1, :], PFT[0:1, 0:1024], AF.Sigmoid))
                on("scalar", a_sgm)
                a_sgmc = A.v

                war_rt = [0]
                for ts in range(8):
                    def t_ltp(t, ts=ts):
                        t.wait_ge(V.sem, v_logc)
                        t.wait_ge(A.sem, a_sgmc)
                        if war_rt[0]:
                            t.wait_ge(V.sem, war_rt[0])
                        P.tag(t.transpose(PbfT[:, 128:144],
                                          combfm_b[0:16, 128 * ts:128 * (ts + 1)],
                                          ident_b[0:16, 0:16]))
                    on("tensor", t_ltp)
                    p_ltp = P.v

                    def v_rt(v):
                        v.wait_ge(P.sem, p_ltp)
                        V.tag(v.tensor_copy(logtm_f[:, 0:16], PbfT[:, 128:144]))
                        v.wait_ge(V.sem, V.v)
                        V.tag(v.reduce_max(router_f[:, 0:1], logtm_f[:, 0:16], axis=AX))
                        v.wait_ge(V.sem, V.v)
                        V.tag(v.tensor_scalar(router_f[:, 16:32], logtm_f[:, 0:16], router_f[:, 0:1],
                                        None, AluOpType.is_equal))
                        v.wait_ge(V.sem, V.v)
                        V.tag(v.tensor_scalar(router_f[:, 32:48], router_f[:, 16:32], -1e30, None,
                                        AluOpType.mult))
                        v.wait_ge(V.sem, V.v)
                        V.tag(v.tensor_tensor(router_f[:, 32:48], router_f[:, 32:48], logtm_f[:, 0:16],
                                        AluOpType.add))
                        v.wait_ge(V.sem, V.v)
                        V.tag(v.reduce_max(router_f[:, 1:2], router_f[:, 32:48], axis=AX))
                        v.wait_ge(V.sem, V.v)
                        V.tag(v.tensor_scalar(router_f[:, 48:64], router_f[:, 32:48], router_f[:, 1:2],
                                              None, AluOpType.is_equal))
                        V.tag(v.tensor_scalar(router_f[:, 2:3], router_f[:, 0:1], router_f[:, 1:2],
                                              None, AluOpType.subtract))
                    on("vector", v_rt)
                    v_rtc = V.v

                    def a_rt(s):
                        s.wait_ge(V.sem, v_rtc)
                        A.tag(s.activation(router_f[:, 3:4], router_f[:, 2:3], AF.Sigmoid))
                    on("scalar", a_rt)
                    a_rtc = A.v

                    def v_cb(v):
                        v.wait_ge(A.sem, a_rtc)
                        V.tag(v.tensor_scalar(router_f[:, 4:5], router_f[:, 3:4], -1.0, 1.0,
                                        AluOpType.mult, AluOpType.add))
                        v.wait_ge(V.sem, V.v)
                        V.tag(v.tensor_scalar(router_f[:, 16:32], router_f[:, 16:32], router_f[:, 3:4],
                                        None, AluOpType.mult))
                        V.tag(v.tensor_scalar(router_f[:, 48:64], router_f[:, 48:64], router_f[:, 4:5],
                                        None, AluOpType.mult))
                        v.wait_ge(V.sem, V.v)
                        V.tag(v.tensor_tensor(comb_a[:, 16 * ts:16 * (ts + 1)],
                                              router_f[:, 16:32],
                                              router_f[:, 48:64], AluOpType.add))
                    on("vector", v_cb)
                    war_rt[0] = V.v

                prev_bs = [0]
                for wt in range(NWT):
                    def t_bs(t, wt=wt):
                        t.wait_ge(A.sem, a_sgmc)
                        if prev_bs[0]:
                            t.wait_ge(V.sem, prev_bs[0])
                        P.tag(t.matmul(PFT[:, 0:256], ones_b[0:1, 0:128],
                                       srow_b[0:1, 256 * wt:256 * (wt + 1)],
                                       start=True, stop=True))
                    on("tensor", t_bs)
                    p_bs = P.v

                    def v_bsc(v, wt=wt):
                        v.wait_ge(P.sem, p_bs)
                        V.tag(v.tensor_copy(bs_f[:, 256 * wt:256 * (wt + 1)], PFT[:, 0:256]))
                    on("vector", v_bsc)
                    prev_bs[0] = V.v
                v_scopeT = V.v

            with nc.psum_tensor(f"QGU_f{seg}", [128, 2048], F32) as QGU, \
                 nc.psum_tensor(f"QDN_f{seg}", [128, 2048], F32) as QDN:

                gcol = lambda g_: 512 * (g_ % 2) + 256 * (g_ // 2)
                moe_load = {}
                moe_guard = {}
                warQ_a = [0, 0]
                warQ_v = [v_scopeT, v_scopeT]
                war_hew_p = [0, 0, 0, 0]
                war_qdn = [0]

                def g_moe(g, e):
                    slot = e % 2
                    if (e - 2) in moe_guard:
                        g.wait_ge(P.sem, moe_guard[e - 2])
                    g.wait_ge(G.sem, mil["moe_bf"])
                    G.tag(g.dma_start(
                        bass.AP(wgu_sb, 8192 * slot, [[16384, 128], [1024, 8], [1, 1024]]),
                        bass.AP(wgu_bf_h, e * 1024 * 1024, [[1024, 128], [131072, 8], [1, 1024]])))
                    G.tag(g.dma_start(
                        bass.AP(wdn_sb, 4096 * slot, [[8192, 128], [1024, 4], [1, 1024]]),
                        bass.AP(wdn_bf_h, e * 512 * 1024, [[1024, 128], [131072, 4], [1, 1024]])))
                    moe_load[e] = G.v

                def g_moe01(g):
                    g_moe(g, 0)
                    g_moe(g, 1)
                on("gpsimd", g_moe01)

                for e in range(NE):
                    slot = e % 2

                    for wt in range(NWT):
                        tc0 = W * wt
                        for gph in range(4):
                            def t_gu(t, e=e, gph=gph, slot=slot, tc0=tc0):
                                t.wait_ge(G.sem, moe_load[e])
                                if warQ_a[gph % 2]:
                                    t.wait_ge(A.sem, warQ_a[gph % 2])
                                if warQ_v[gph % 2]:
                                    t.wait_ge(V.sem, warQ_v[gph % 2])
                                gc = gcol(gph)
                                for dc in range(8):
                                    t.matmul(QGU[:, gc:gc + 256],
                                             wgu_sb[:, 8192 * slot + 1024 * dc + 128 * gph:
                                                    8192 * slot + 1024 * dc + 128 * (gph + 1)],
                                             h1fm_b[:, 1024 * dc + tc0:1024 * dc + tc0 + 256],
                                             start=(dc == 0), stop=(dc == 7))
                                for dc in range(8):
                                    last = t.matmul(QGU[:, 1024 + gc:1024 + gc + 256],
                                             wgu_sb[:, 8192 * slot + 1024 * dc + 128 * (gph + 4):
                                                    8192 * slot + 1024 * dc + 128 * (gph + 5)],
                                             h1fm_b[:, 1024 * dc + tc0:1024 * dc + tc0 + 256],
                                             start=(dc == 0), stop=(dc == 7))
                                P.tag(last)
                            on("tensor", t_gu)
                            p_gu = P.v

                            def a_sil(s, gph=gph):
                                s.wait_ge(P.sem, p_gu)
                                gc = gcol(gph)
                                A.tag(s.activation(silu_f[:, 256 * (gph % 2):256 * (gph % 2) + 256],
                                                   QGU[:, gc:gc + 256], AF.Silu))
                            on("scalar", a_sil)
                            warQ_a[gph % 2] = A.v
                            a_silc = A.v

                            def v_he(v, gph=gph, wt=wt):
                                v.wait_ge(A.sem, a_silc)
                                if war_hew_p[gph]:
                                    v.wait_ge(P.sem, war_hew_p[gph])
                                gc = gcol(gph)
                                V.tag(v.tensor_tensor(hew_b[:, 256 * gph:256 * (gph + 1)],
                                                      silu_f[:, 256 * (gph % 2):256 * (gph % 2) + 256],
                                                      QGU[:, 1024 + gc:1024 + gc + 256],
                                                      AluOpType.mult))
                            on("vector", v_he)
                            warQ_v[gph % 2] = V.v
                            hew_done = V.v
                            war_hew_local = hew_done
                            if gph == 0:
                                hewv = [0, 0, 0, 0]
                            hewv[gph] = V.v

                        def t_dn(t, e=e, slot=slot, hewv=hewv):
                            if war_qdn[0]:
                                t.wait_ge(V.sem, war_qdn[0])
                            for fc in range(4):
                                t.wait_ge(V.sem, hewv[fc])
                                for th in range(2):
                                    for dh in range(2):
                                        last = t.matmul(QDN[:, 1024 * th + 512 * dh:1024 * th + 512 * (dh + 1)],
                                                 hew_b[:, 256 * fc + 128 * th:256 * fc + 128 * (th + 1)],
                                                 wdn_sb[:, 4096 * slot + 1024 * fc + 512 * dh:
                                                        4096 * slot + 1024 * fc + 512 * (dh + 1)],
                                                 start=(fc == 0), stop=(fc == 3))
                            P.tag(last)
                        on("tensor", t_dn)
                        p_dnc = P.v
                        for fc in range(4):
                            war_hew_p[fc] = P.v

                        def v_acc(v, wt=wt, e=e):
                            v.wait_ge(P.sem, p_dnc)
                            for th in range(2):
                                sl = 2 * wt + th
                                v.tensor_scalar(tmp_f[:, 0:1024],
                                                QDN[:, 1024 * th:1024 * (th + 1)],
                                                comb_a[:, 16 * sl + e:16 * sl + e + 1],
                                                None, AluOpType.mult)
                                V.tag(v.tensor_tensor(acc_f[:, 1024 * sl:1024 * (sl + 1)],
                                                      acc_f[:, 1024 * sl:1024 * (sl + 1)],
                                                      tmp_f[:, 0:1024], AluOpType.add))
                        on("vector", v_acc)
                        war_qdn[0] = V.v
                    moe_guard[e] = P.v
                    if e + 2 < NE:
                        def g_pre(g, e=e):
                            g_moe(g, e + 2)
                        on("gpsimd", g_pre)

                sh_load = {}
                sh_guard = {}

                def g_sh(g, k, fb, wt):
                    slot = k % 2
                    if (k - 2) in sh_guard:
                        g.wait_ge(P.sem, sh_guard[k - 2])
                    g.wait_ge(G.sem, shT_state["done_g"])
                    G.tag(g.dma_start(
                        bass.AP(swgt_sb, 1024 * slot, [[2048, 128], [128, 8], [1, 128]]),
                        bass.AP(swgT_h, 128 * fb, [[2816, 128], [128 * 2816, 8], [1, 128]])))
                    G.tag(g.dma_start(
                        bass.AP(swut_sb, 1024 * slot, [[2048, 128], [128, 8], [1, 128]]),
                        bass.AP(swuT_h, 128 * fb, [[2816, 128], [128 * 2816, 8], [1, 128]])))
                    G.tag(g.dma_start(
                        bass.AP(swdt_sb, 1024 * slot, [[2048, 128], [1, 1024]]),
                        bass.AP(swdT_h, 128 * fb * 1024, [[1024, 128], [1, 1024]])))
                    sh_load[k] = G.v

                war_sh_p = [0]
                kglob = [0]

                for wt in range(NWT):
                    tc0 = W * wt
                    if wt == 0:
                        def g_sh01(g):
                            g_sh(g, 0, 0, 0)
                            g_sh(g, 1, 1, 0)
                        on("gpsimd", g_sh01)

                    for fb in range(NFB):
                        k = wt * NFB + fb
                        slot = k % 2
                        shc = 512 * (k % 2)

                        def t_shgu(t, k=k, slot=slot, tc0=tc0, shc=shc):
                            t.wait_ge(G.sem, sh_load[k])
                            if warQ_a[k % 2]:
                                t.wait_ge(A.sem, warQ_a[k % 2])
                            if warQ_v[k % 2]:
                                t.wait_ge(V.sem, warQ_v[k % 2])
                            for dc in range(8):
                                t.matmul(QGU[:, shc:shc + 256],
                                         swgt_sb[:, 1024 * slot + 128 * dc:1024 * slot + 128 * (dc + 1)],
                                         h1fm_b[:, 1024 * dc + tc0:1024 * dc + tc0 + 256],
                                         start=(dc == 0), stop=(dc == 7))
                            for dc in range(8):
                                last = t.matmul(QGU[:, 1024 + shc:1024 + shc + 256],
                                         swut_sb[:, 1024 * slot + 128 * dc:1024 * slot + 128 * (dc + 1)],
                                         h1fm_b[:, 1024 * dc + tc0:1024 * dc + tc0 + 256],
                                         start=(dc == 0), stop=(dc == 7))
                            P.tag(last)
                        on("tensor", t_shgu)
                        p_shgu = P.v

                        def a_shs(s, k=k, shc=shc):
                            s.wait_ge(P.sem, p_shgu)
                            A.tag(s.activation(shsil_f[:, 256 * (k % 2):256 * (k % 2) + 256],
                                               QGU[:, shc:shc + 256], AF.Silu))
                        on("scalar", a_shs)
                        warQ_a[k % 2] = A.v
                        a_shsc = A.v

                        def v_shhe(v, k=k, shc=shc, wt=wt):
                            v.wait_ge(A.sem, a_shsc)
                            if war_sh_p[0]:
                                v.wait_ge(P.sem, war_sh_p[0])
                            v.tensor_tensor(shhe_f[:, 0:256],
                                            shsil_f[:, 256 * (k % 2):256 * (k % 2) + 256],
                                            QGU[:, 1024 + shc:1024 + shc + 256],
                                            AluOpType.mult)
                            v.tensor_tensor(shhe_f[:, 0:256], shhe_f[:, 0:256],
                                            bs_f[:, 256 * wt:256 * (wt + 1)],
                                            AluOpType.mult)
                            V.tag(v.tensor_copy(shhew_b[:, 0:256], shhe_f[:, 0:256]))
                        on("vector", v_shhe)
                        warQ_v[k % 2] = V.v
                        v_shhec = V.v

                        def t_shdn(t, fb=fb, slot=slot):
                            t.wait_ge(V.sem, v_shhec)
                            if fb == 0 and war_qdn[0]:
                                t.wait_ge(V.sem, war_qdn[0])
                            for th in range(2):
                                for dh in range(2):
                                    last = t.matmul(QDN[:, 1024 * th + 512 * dh:1024 * th + 512 * (dh + 1)],
                                             shhew_b[:, 128 * th:128 * (th + 1)],
                                             swdt_sb[:, 1024 * slot + 512 * dh:1024 * slot + 512 * (dh + 1)],
                                             start=(fb == 0), stop=(fb == NFB - 1))
                            P.tag(last)
                        on("tensor", t_shdn)
                        sh_guard[k] = P.v
                        war_sh_p[0] = P.v
                        nk = k + 2
                        if nk < NWT * NFB:
                            nwt, nfb = nk // NFB, nk % NFB
                            def g_shp(g, nk=nk, nfb=nfb, nwt=nwt):
                                g_sh(g, nk, nfb, nwt)
                            on("gpsimd", g_shp)
                    p_dn_done = P.v

                    def v_accs(v, wt=wt):
                        v.wait_ge(P.sem, p_dn_done)
                        sl = 2 * wt
                        v.tensor_tensor(acc_f[:, 1024 * sl:1024 * (sl + 1)],
                                        acc_f[:, 1024 * sl:1024 * (sl + 1)],
                                        QDN[:, 0:1024], AluOpType.add)
                        V.tag(v.tensor_tensor(acc_f[:, 1024 * (sl + 1):1024 * (sl + 2)],
                                              acc_f[:, 1024 * (sl + 1):1024 * (sl + 2)],
                                              QDN[:, 1024:2048], AluOpType.add))
                    on("vector", v_accs)
                    war_qdn[0] = V.v

                    for th in range(2):
                        glob_t0 = seg * 1024 + wt * W + 128 * th
                        sl = 2 * wt + th

                        def v_ln2(v, sl=sl):
                            v.tensor_tensor(r2_f[:, :], acc_f[:, 1024 * sl:1024 * (sl + 1)],
                                            h1f[:, 1024 * sl:1024 * (sl + 1)], AluOpType.add)
                            v.bn_stats(bna2_f[:, 0:6], r2_f[:, 0:512])
                            V.tag(v.bn_stats(bna2_f[:, 6:12], r2_f[:, 512:1024]))
                            v.wait_ge(V.sem, V.v)
                            v.bn_aggr(bna2_f[:, 12:14], bna2_f[:, 0:6])
                            V.tag(v.bn_aggr(bna2_f[:, 14:16], bna2_f[:, 6:12]))
                            v.wait_ge(V.sem, V.v)
                            v.tensor_tensor(bna2_f[:, 16:17], bna2_f[:, 12:13],
                                            bna2_f[:, 14:15], AluOpType.add)
                            v.tensor_tensor(bna2_f[:, 17:18], bna2_f[:, 12:13],
                                            bna2_f[:, 14:15], AluOpType.subtract)
                            V.tag(v.tensor_tensor(bna2_f[:, 18:19], bna2_f[:, 13:14],
                                            bna2_f[:, 15:16], AluOpType.add))
                            v.wait_ge(V.sem, V.v)
                            V.tag(v.tensor_tensor(bna2_f[:, 19:20], bna2_f[:, 17:18],
                                            bna2_f[:, 17:18], AluOpType.mult))
                            v.wait_ge(V.sem, V.v)
                            V.tag(v.tensor_scalar(bna2_f[:, 19:20], bna2_f[:, 19:20],
                                            0.5, None, AluOpType.mult))
                            v.wait_ge(V.sem, V.v)
                            V.tag(v.tensor_tensor(bna2_f[:, 18:19], bna2_f[:, 18:19],
                                                  bna2_f[:, 19:20], AluOpType.add))
                        on("vector", v_ln2)
                        v_ln2c = V.v

                        def a_r2(s):
                            s.wait_ge(V.sem, v_ln2c)
                            s.activation(rstd2_f[:, 0:1], bna2_f[:, 18:19], AF.Ln,
                                         bias=qsc_f[:, 4:5])
                            A.tag(s.activation(rstd2_f[:, 0:1], rstd2_f[:, 0:1], AF.Exp,
                                               scale=-0.5, bias=qsc_f[:, 5:6]))
                        on("scalar", a_r2)
                        a_r2c = A.v

                        def v_out(v):
                            v.wait_ge(A.sem, a_r2c)
                            if out_slot_war[0]:
                                v.wait_ge(O.sem, out_slot_war[0])
                            V.tag(v.tensor_scalar(rstd2_f[:, 1:2], bna2_f[:, 16:17], rstd2_f[:, 0:1],
                                            -0.5, AluOpType.mult, AluOpType.mult))
                            v.wait_ge(V.sem, V.v)
                            V.tag(v.tensor_scalar(outst_f[:, 0:1024],
                                                  r2_f[:, :], rstd2_f[:, 0:1], rstd2_f[:, 1:2],
                                                  AluOpType.mult, AluOpType.add))
                        on("vector", v_out)
                        v_outc = V.v

                        def g_out(g, glob_t0=glob_t0):
                            g.wait_ge(V.sem, v_outc)
                            O.tag(g.dma_start(out_h.ap()[glob_t0:glob_t0 + 128, :],
                                              outst_f[:, 0:1024]))
                        on("gpsimd", g_out)
                        out_slot_war[0] = O.v

            barrier()

    def g_fin(g):
        g.wait_ge(O.sem, O.v)
    on("gpsimd", g_fin)

    es.close()
    return nc




def make_consts():
    cident = np.eye(128, dtype=np.float32)
    cmaskT = np.triu(np.ones((128, 128), np.float32))
    cgbd = np.zeros((16, 1024), np.float32)
    for j in range(8):
        for c in range(128):
            cgbd[2 * j + (c // 64), 128 * j + c] = 1.0
    cece = np.zeros((16, 2048), np.float32)
    for e in range(16):
        cece[e, 128 * e:128 * (e + 1)] = 1.0
    cnormE = np.zeros((128, 128), np.float32)
    for j in range(8):
        for p in range(128):
            cnormE[p, 16 * j + 2 * j + (p // 64)] = 1.0
    cpatT = np.zeros((2, 128), np.float32)
    for p in range(128):
        cpatT[(p // 32) % 2, p] = 1.0
    return dict(cident=cident, cmaskT=cmaskT, cgbd=cgbd, cece=cece,
                cnormE=cnormE, cpatT=cpatT)


def make_in_maps(inputs):
    x = np.ascontiguousarray(np.asarray(inputs["L0_attn_x"], dtype=np.float32))
    consts = make_consts()
    shared = dict(
        wqkv=np.ascontiguousarray(inputs["L0_attn_w_qkv"], dtype=np.float32),
        bqkv=np.ascontiguousarray(np.asarray(inputs["L0_attn_b_qkv"], np.float32).reshape(1, 3072)),
        wout=np.ascontiguousarray(inputs["L0_attn_w_out"], dtype=np.float32),
        bout=np.ascontiguousarray(np.asarray(inputs["L0_attn_b_out"], np.float32).reshape(1, 1024)),
        omg=np.ascontiguousarray(inputs["L0_attn_omega"], dtype=np.float32),
        qn=np.ascontiguousarray(np.asarray(inputs["L0_attn_quad_nodes"], np.float32).reshape(2, 1)),
        qw=np.ascontiguousarray(np.asarray(inputs["L0_attn_quad_weights"], np.float32).reshape(2, 1)),
        gw=np.ascontiguousarray(inputs["L0_ffn_gate_w"], dtype=np.float32),
        wgu=np.ascontiguousarray(inputs["L0_ffn_w_gate_up"], dtype=np.float32),
        wdn=np.ascontiguousarray(inputs["L0_ffn_w_down"], dtype=np.float32),
        swg=np.ascontiguousarray(inputs["L0_ffn_shared_w_gate"], dtype=np.float32),
        swu=np.ascontiguousarray(inputs["L0_ffn_shared_w_up"], dtype=np.float32),
        swd=np.ascontiguousarray(inputs["L0_ffn_shared_w_down"], dtype=np.float32),
        sgw=np.ascontiguousarray(inputs["L0_ffn_shared_gate_w"], dtype=np.float32),
        **consts,
    )
    zeros_pref = np.zeros((4096, 1024), np.float32)
    pz1 = np.ones((128, 1), np.float32)
    pz0 = np.zeros((128, 1), np.float32)
    in_maps = []
    for c in range(8):
        b, s = c // 2, c % 2
        xm = np.ascontiguousarray(x[b, s * 4096:(s + 1) * 4096])
        xp = np.ascontiguousarray(x[b, 0:4096]) if s == 1 else zeros_pref
        in_maps.append(dict(xm=xm, xp=xp, pz=(pz1 if s == 1 else pz0), **shared))
    return in_maps


def kernel(**inputs):
    in_maps = make_in_maps(inputs)
    nc = build_nc()
    res = run_bass_kernel_spmd(nc, in_maps, core_ids=list(range(8)))
    out = np.empty((4, 8192, 1024), np.float32)
    for c in range(8):
        b, s = c // 2, c % 2
        out[b, s * 4096:(s + 1) * 4096] = res.results[c]["out"]
    return out
